# version 10
# baseline (speedup 1.0000x reference)
import numpy as np
import ml_dtypes
import concourse.bass as bass
import concourse.mybir as mybir

F32 = mybir.dt.float32
BF16 = mybir.dt.bfloat16
AF = mybir.ActivationFunctionType
ALU = mybir.AluOpType

T = 4096
D = 1024
CH = 8
NT = 8
QT = 512
NJ = 32
NPAIR = 8
DH = 64
EXP_SHIFT = -12.0

ENGINES = ("tensor", "scalar", "vector", "gpsimd", "sync")


class MK:

    def __init__(self, nc):
        self.nc = nc
        self.ops = {e: [] for e in ENGINES}
        self.sems = {}
        self.cnt = {}
        self.last_wait = {e: {} for e in ENGINES}

    def sem(self, name):
        self.cnt[name] = 0
        return name

    def emit(self, eng, fn, waits=(), inc=None, n=1):
        ws = []
        for wv in waits:
            if wv is None:
                continue
            s, v = wv
            lw = self.last_wait[eng].get(s, -1)
            if v > lw:
                ws.append((s, v))
                self.last_wait[eng][s] = v
        ev = None
        if inc is not None:
            self.cnt[inc] += n
            ev = (inc, self.cnt[inc])
        self.ops[eng].append((ws, fn, (inc, n) if inc else None))
        return ev

    def dma(self, eng, out, in_, waits=(), inc="ld"):
        return self.emit(eng, lambda e: e.dma_start(out=out, in_=in_),
                         waits=waits, inc=inc, n=16)

    def now(self, sem):
        return (sem, self.cnt[sem])

    def replay(self):
        nc = self.nc
        from contextlib import ExitStack
        with ExitStack() as st:
            for name in self.cnt:
                self.sems[name] = st.enter_context(nc.semaphore(name))
            with nc.Block() as block:
                for eng_name in ENGINES:
                    deco = getattr(block, eng_name)

                    def body(eng, _ops=self.ops[eng_name]):
                        for ws, fn, inc in _ops:
                            for s, v in ws:
                                eng.wait_ge(self.sems[s], v)
                            instr = fn(eng)
                            if inc is not None:
                                instr.then_inc(self.sems[inc[0]], inc[1])
                    deco(body)


def host_inputs(inputs, core):
    x = np.asarray(inputs["L0_attn_x"])
    Wqkv = np.asarray(inputs["L0_attn_W_qkv"])
    Wo = np.asarray(inputs["L0_attn_W_o"])
    Wr = np.asarray(inputs["L0_ffn_W_r"])
    Wk = np.asarray(inputs["L0_ffn_W_k"])
    Wv = np.asarray(inputs["L0_ffn_W_v"])
    g = np.asarray(inputs["L0_attn_ln_gamma"])
    b = np.asarray(inputs["L0_attn_ln_beta"])
    mur = np.asarray(inputs["L0_ffn_mu_r"])
    muk = np.asarray(inputs["L0_ffn_mu_k"])

    d = {}
    xT = np.ascontiguousarray(x[core].T)
    d["xdev"] = np.ascontiguousarray(xT.reshape(CH, 128, T).transpose(1, 0, 2))
    rows = []
    for pr in range(NPAIR):
        h0, h1 = 2 * pr, 2 * pr + 1
        for sec in range(3):
            rows.append(Wqkv[192 * h0 + 64 * sec: 192 * h0 + 64 * sec + 64])
            rows.append(Wqkv[192 * h1 + 64 * sec: 192 * h1 + 64 * sec + 64])
    Wq_re = np.concatenate(rows, axis=0)
    WqT = Wq_re.T
    wq = WqT.reshape(CH, 128, NPAIR, 384).transpose(2, 1, 0, 3)
    d["wqkvdev"] = np.ascontiguousarray(wq)
    for name, W in (("wodev", Wo), ("wrdev", Wr), ("wkdev", Wk), ("wvdev", Wv)):
        WT = W.T
        d[name] = np.ascontiguousarray(WT.reshape(CH, 128, D).transpose(1, 0, 2))
    d["gbdev"] = np.ascontiguousarray(
        np.concatenate([g.reshape(CH, 128).T, b.reshape(CH, 128).T], axis=1))
    d["mudev"] = np.ascontiguousarray(
        np.concatenate([mur.reshape(CH, 128).T, muk.reshape(CH, 128).T], axis=1))
    d["identdev"] = np.eye(128, dtype=np.float32)
    d["eye16dev"] = np.eye(16, dtype=np.float32).reshape(1, 256)
    sel16 = np.zeros((16, 8 * 128), dtype=np.float32)
    for t in range(8):
        sel16[2 * t, 128 * t:128 * t + 64] = 1.0
        sel16[2 * t + 1, 128 * t + 64:128 * t + 128] = 1.0
    d["sel16dev"] = sel16
    masks = np.zeros((128, 4 * QT), dtype=np.float32)
    qq = np.arange(QT)
    kk = np.arange(128)
    for k in range(4):
        masks[:, k * QT:(k + 1) * QT] = (qq[None, :] - 128 * k >= kk[:, None])
    d["maskdev"] = masks
    return d


def host_output(r):
    o = r["outdev"]
    return np.ascontiguousarray(o.transpose(1, 0, 2).reshape(D, T).T)


def build(dbg=0, phases=4):
    nc = bass.Bass("TRN2", target_bir_lowering=False)
    P = MK(nc)
    for s in ("ld", "ldg", "st", "pe", "act", "dve", "gps", "cst", "wq",
              "wld", "wrd", "xbf0", "xbf1", "cx0", "cx1", "x2b0", "x2b1",
              "xf0", "xf1", "xf2", "xf3", "osm0", "osm1", "osm2",
              "stx0", "stx1", "stx2", "stx3", "stc0", "stc1"):
        P.sem(s)

    xdev = nc.dram_tensor("xdev", [128, CH, T], F32, kind="ExternalInput").ap()
    wqkvdev = nc.dram_tensor("wqkvdev", [NPAIR, 128, CH, 384], F32, kind="ExternalInput").ap()
    wodev = nc.dram_tensor("wodev", [128, CH, D], F32, kind="ExternalInput").ap()
    wrdev = nc.dram_tensor("wrdev", [128, CH, D], F32, kind="ExternalInput").ap()
    wkdev = nc.dram_tensor("wkdev", [128, CH, D], F32, kind="ExternalInput").ap()
    wvdev = nc.dram_tensor("wvdev", [128, CH, D], F32, kind="ExternalInput").ap()
    gbdev = nc.dram_tensor("gbdev", [128, 16], F32, kind="ExternalInput").ap()
    mudev = nc.dram_tensor("mudev", [128, 16], F32, kind="ExternalInput").ap()
    identdev = nc.dram_tensor("identdev", [128, 128], F32, kind="ExternalInput").ap()
    eye16dev = nc.dram_tensor("eye16dev", [1, 256], F32, kind="ExternalInput").ap()
    sel16dev = nc.dram_tensor("sel16dev", [16, 1024], F32, kind="ExternalInput").ap()
    maskdev = nc.dram_tensor("maskdev", [128, 4 * QT], F32, kind="ExternalInput").ap()
    outdev = nc.dram_tensor("outdev", [128, CH, T], F32, kind="ExternalOutput").ap()
    ctxspill = nc.dram_tensor("ctxspill", [NPAIR, 128, T], BF16, kind="Internal").ap()
    x2spill = nc.dram_tensor("x2spill", [128, CH, T], F32, kind="Internal").ap()
    dbg_out = {}
    if dbg >= 1 and phases == 1:
        dbg_out["xn2"] = nc.dram_tensor("dbg_xn2", [128, CH, T], F32, kind="ExternalOutput").ap()
    if dbg >= 2:
        dbg_out["ctx"] = nc.dram_tensor("dbg_ctx", [NPAIR, 128, T], F32, kind="ExternalOutput").ap()
        dbg_out["sums"] = nc.dram_tensor("dbg_sums", [NPAIR, 2 * T], F32, kind="ExternalOutput").ap()
    if dbg >= 3 and phases == 3:
        dbg_out["x2"] = nc.dram_tensor("dbg_x2", [128, CH, T], F32, kind="ExternalOutput").ap()
    if dbg >= 5 and phases == 3:
        dbg_out["au"] = nc.dram_tensor("dbg_au", [NT, 128, CH, QT], F32, kind="ExternalOutput").ap()

    from contextlib import ExitStack
    with ExitStack() as stack:
        sb = lambda name, shape, dt: stack.enter_context(nc.sbuf_tensor(name, shape, dt))
        psal = lambda name, shape, dt: stack.enter_context(nc.psum_tensor(name, shape, dt))

        gb_sb = sb("gb_sb", [128, 16], F32)
        mu_sb = sb("mu_sb", [128, 32], F32)
        ident_bf = sb("ident_bf", [128, 128], BF16)
        masks_bf = sb("masks_bf", [128, 4 * QT], BF16)
        ones_col = sb("ones_col", [128, 1], BF16)
        onesf_row = sb("onesf_row", [1, 128], F32)
        expbias = sb("expbias", [128, 1], F32)
        neghalf = sb("neghalf", [128, 1], F32)
        epscol = sb("epscol", [128, 1], F32)

        xn2 = sb("xn2", [128, CH * T], BF16)
        r_a = sb("r_a", [1, QT], F32)
        r_b = sb("r_b", [1, QT], F32)
        r_c = sb("r_c", [1, QT], F32)
        recip16 = sb("recip16", [16, QT], BF16)
        sel16_sb = sb("sel16_sb", [16, 1024], BF16)
        eye16_sb = sb("eye16_sb", [1, 256], BF16)

        xbf_t = [sb(f"xbf{i}", [128, CH, QT], BF16) for i in range(2)]
        srowA = bass.AP(xbf_t[0], 0, [[CH * QT, 1], [1, CH * QT]])
        srowB = bass.AP(xbf_t[1], 0, [[CH * QT, 1], [1, CH * QT]])
        xsq_t = sb("xsq", [128, 2, QT], BF16)
        xf_t = [sb(f"xf{i}", [128, QT], F32) for i in range(4)]
        bc_t = [sb(f"bc{i}", [128, 2, QT], F32) for i in range(2)]

        qp_sb = [sb(f"qp{i}", [128, T], BF16) for i in range(2)]
        kp_sb = [sb(f"kp{i}", [128, T], BF16) for i in range(2)]
        vtm_sb = [sb(f"vtm{i}", [128, NJ * 130], BF16) for i in range(2)]
        vfm_st = sb("vfm_st", [128, QT], BF16)
        wqkv_sb = sb("wqkv_sb", [128, 3072], BF16)
        exps_sb = [sb(f"exps{i}", [128, 1024], BF16) for i in range(4)]
        ctxu_sb = [sb(f"ctxu{i}", [128, T], BF16) for i in range(2)]
        x2sm = [sb(f"x2sm{i}", [128, QT], F32) for i in range(4)]
        osm = [sb(f"osm{i}", [128, QT], F32) for i in range(3)]

        wo_v = lambda c, m: xn2[:, 1024 * c + 128 * m: 1024 * c + 128 * (m + 1)]

        PS_S = [psal("ps_s0", [128, 1024], F32), psal("ps_s1", [128, 1024], F32)]
        PS_C = psal("ps_c", [128, 1024], F32)
        PS_D = psal("ps_d", [128, 512], F32)
        PS_TR = psal("ps_tr", [128, 1024], BF16)

        P.dma("sync", gb_sb[:], gbdev, inc="cst")
        P.dma("sync", mu_sb[:, 0:16], mudev, inc="cst")
        P.dma("gpsimd", ident_bf[:], identdev, inc="cst")
        P.dma("gpsimd", masks_bf[:], maskdev, inc="cst")
        P.dma("gpsimd", eye16_sb[:], eye16dev, inc="cst")
        ev_cst = P.dma("gpsimd", sel16_sb[:], sel16dev, inc="cst")
        ld0 = [ev_cst]
        for f in (lambda e: e.memset(ones_col[:], 1.0),
                  lambda e: e.memset(onesf_row[:], 1.0),
                  lambda e: e.memset(expbias[:], EXP_SHIFT),
                  lambda e: e.memset(neghalf[:], -0.5),
                  lambda e: e.memset(epscol[:], 1e-5),
                  lambda e: e.memset(vtm_sb[0][:], 1.0),
                  lambda e: e.memset(vtm_sb[1][:], 1.0)):
            P.emit("vector", f, inc="dve")
        P.emit("vector", lambda e: e.tensor_scalar(mu_sb[:, 16:32], mu_sb[:, 0:16],
                                                   -1.0, 1.0, ALU.mult, ALU.add),
               waits=ld0, inc="dve")
        consts_ev = P.now("dve")

        def xn2v(c, t0, n):
            return xn2[:, T * c + t0: T * c + t0 + n]

        xbf_free = [None, None]
        xf_free = [None] * 4
        sxx_mm_ev = {}
        stats_bank_free = None
        bcast_bank_free = None
        rows_free = None
        xn2_evs = {}

        for t in range(NT):
            sl = t % 2
            ev_x = P.dma("gpsimd", xbf_t[sl][:], xdev[:, :, QT * t:QT * (t + 1)],
                         waits=[xbf_free[sl]], inc=f"xbf{sl}")
            for c in range(CH):
                evq = P.emit("vector",
                             lambda e, c=c, sl=sl: e.tensor_mul(
                                 xsq_t[:, c % 2, :], xbf_t[sl][:, c, :], xbf_t[sl][:, c, :]),
                             waits=[ev_x, sxx_mm_ev.get(c - 2)], inc="dve")
                P.emit("tensor",
                       lambda e, c=c, sl=sl: e.matmul(
                           PS_S[0][0:1, 0:512], ones_col[:], xbf_t[sl][:, c, :],
                           start=(c == 0), stop=(c == CH - 1)),
                       waits=[ev_x, consts_ev] + ([stats_bank_free] if c == 0 else []))
                sxx_mm_ev[c] = P.emit("tensor",
                                      lambda e, c=c: e.matmul(
                                          PS_S[0][0:1, 512:1024], ones_col[:], xsq_t[:, c % 2, :],
                                          start=(c == 0), stop=(c == CH - 1)),
                                      waits=[evq], inc="pe")
            stats_pe = P.now("pe")
            xbf_free[sl] = stats_pe
            ev_sx = P.emit("vector",
                           lambda e: e.tensor_copy(r_a[0:1, :], PS_S[0][0:1, 0:512]),
                           waits=[stats_pe, rows_free], inc="dve")
            ev_sxx = P.emit("vector",
                            lambda e: e.tensor_copy(r_b[0:1, :], PS_S[0][0:1, 512:1024]),
                            waits=[stats_pe], inc="dve")
            stats_bank_free = ev_sxx
            e1 = P.emit("vector", lambda e: e.tensor_scalar_mul(r_a[0:1, :], r_a[0:1, :], 1.0 / D),
                        waits=[ev_sx], inc="dve")
            e2 = P.emit("vector", lambda e: e.tensor_scalar_mul(r_b[0:1, :], r_b[0:1, :], 1.0 / D),
                        waits=[ev_sxx], inc="dve")
            e3 = P.emit("vector", lambda e: e.tensor_mul(r_c[0:1, :], r_a[0:1, :], r_a[0:1, :]),
                        waits=[e1], inc="dve")
            e4 = P.emit("vector", lambda e: e.tensor_sub(r_b[0:1, :], r_b[0:1, :], r_c[0:1, :]),
                        waits=[e2, e3], inc="dve")
            e5 = P.emit("vector", lambda e: e.tensor_scalar(r_b[0:1, :], r_b[0:1, :],
                                                            1.0 + 1e-5, 1e-10, ALU.mult, ALU.add),
                        waits=[e4], inc="dve")
            e6 = P.emit("scalar", lambda e: e.activation(r_b[0:1, :], r_b[0:1, :], AF.Ln),
                        waits=[e5], inc="act")
            e7 = P.emit("scalar", lambda e: e.activation(r_c[0:1, :], r_b[0:1, :], AF.Exp,
                                                         scale=neghalf[0:1, 0:1]),
                        waits=[e6], inc="act")
            e8 = P.emit("vector", lambda e: e.tensor_mul(r_a[0:1, :], r_a[0:1, :], r_c[0:1, :]),
                        waits=[e7], inc="dve")
            evb1 = P.emit("tensor",
                          lambda e: e.matmul(PS_S[1][:, 0:512], onesf_row[0:1, :],
                                             r_c[0:1, :], start=True, stop=True),
                          waits=[e7, e8, bcast_bank_free], inc="pe")
            evb2 = P.emit("tensor",
                          lambda e: e.matmul(PS_S[1][:, 512:1024], onesf_row[0:1, :],
                                             r_a[0:1, :], start=True, stop=True),
                          waits=[e8], inc="pe")
            rows_free = evb2
            evc1 = P.emit("vector",
                          lambda e, sl=sl: e.tensor_copy(bc_t[sl][:, 0, :], PS_S[1][:, 0:512]),
                          waits=[evb1, P.now("gps")], inc="dve")
            evc2 = P.emit("vector",
                          lambda e, sl=sl: e.tensor_copy(bc_t[sl][:, 1, :], PS_S[1][:, 512:1024]),
                          waits=[evb2], inc="dve")
            bcast_bank_free = evc2
            for c in range(CH):
                fsl = (t * CH + c) % 4
                eng, esem = ("vector", "dve")
                ev_xf = P.dma("sync", xf_t[fsl][:], xdev[:, c, QT * t:QT * (t + 1)],
                              waits=[xf_free[fsl]], inc=f"xf{fsl}")
                P.emit(eng,
                       lambda e, fsl=fsl, sl=sl: e.tensor_mul(xf_t[fsl][:], xf_t[fsl][:], bc_t[sl][:, 0, :]),
                       waits=[ev_xf, evc1], inc=esem)
                esub = P.emit(eng,
                              lambda e, fsl=fsl, sl=sl: e.tensor_sub(xf_t[fsl][:], xf_t[fsl][:], bc_t[sl][:, 1, :]),
                              waits=[evc2], inc=esem)
                e_ap = P.emit("vector",
                              lambda e, c=c, t=t, fsl=fsl: e.tensor_scalar(
                                  xn2v(c, QT * t, QT), xf_t[fsl][:],
                                  gb_sb[:, c:c + 1], gb_sb[:, 8 + c:9 + c], ALU.mult, ALU.add),
                              waits=[esub], inc="dve")
                xf_free[fsl] = e_ap
                xn2_evs[(t, c)] = e_ap

        if dbg >= 1 and phases == 1:
            dump_free = {}
            for t in range(NT):
                for c in range(CH):
                    fsl = (t * CH + c) % 2
                    ev = P.emit("vector",
                                lambda e, c=c, t=t, fsl=fsl: e.tensor_copy(
                                    x2sm[fsl][:], xn2v(c, QT * t, QT)),
                                waits=[xn2_evs[(t, c)], dump_free.get(fsl)], inc="dve")
                    dump_free[fsl] = P.dma("sync", dbg_out["xn2"][:, c, QT * t:QT * (t + 1)],
                                           x2sm[fsl][:], waits=[ev], inc="st")

        xn2_ready = P.now("dve")

        if phases >= 2:
            s_free = [None, None]
            eb_free = [None] * 4
            wq_free = None
            proj_done = [None] * NPAIR
            qk_last = [None] * NPAIR
            av_last = [None] * NPAIR
            ctx_bank_free = None
            psd_free = None
            pstr_free = None
            ctxu_free = [None, None]
            spill_evs = {}
            norm_done = [None] * NPAIR
            jj = 0

            def emit_proj_group(p, gi):
                nonlocal psd_free, pstr_free
                sl = p % 2
                mt = gi % 3
                tt = gi // 3
                evs = []
                for c in range(CH):
                    w = [wq_ev[p], xn2_ready]
                    if c == 0:
                        w.append(psd_free)
                    ev = P.emit("tensor",
                                lambda e, c=c, mt=mt, tt=tt: e.matmul(
                                    PS_D[:, 0:512],
                                    wqkv_sb[:, 384 * c + 128 * mt: 384 * c + 128 * (mt + 1)],
                                    xn2v(c, QT * tt, QT),
                                    start=(c == 0), stop=(c == CH - 1)),
                                waits=w, inc="pe" if c == CH - 1 else None)
                    if c == CH - 1:
                        evs.append(ev)
                mm_ev = evs[-1]
                ev_guard = qk_last[p - 2] if p >= 2 and mt < 2 else None
                ev_guard_av = av_last[p - 2] if p >= 2 and mt == 2 else None
                if mt == 0:
                    evac = P.emit("vector",
                                  lambda e, sl=sl, tt=tt: e.tensor_scalar_mul(
                                      qp_sb[sl][:, QT * tt:QT * (tt + 1)], PS_D[:, 0:512], 0.125),
                                  waits=[mm_ev, ev_guard], inc="dve")
                    psd_free = evac
                elif mt == 1:
                    evac = P.emit("vector",
                                  lambda e, sl=sl, tt=tt: e.tensor_copy(
                                      kp_sb[sl][:, QT * tt:QT * (tt + 1)], PS_D[:, 0:512]),
                                  waits=[mm_ev, ev_guard], inc="dve")
                    psd_free = evac
                else:
                    evac0 = P.emit("vector",
                                   lambda e: e.tensor_copy(vfm_st[:, 0:512], PS_D[:, 0:512]),
                                   waits=[mm_ev, vfm_free[0]], inc="dve")
                    psd_free = evac0
                    trev = None
                    for b in range(4):
                        trev = P.emit("tensor",
                                      lambda e, b=b: e.transpose(
                                          PS_TR[:, 128 * b:128 * (b + 1)],
                                          vfm_st[:, 128 * b:128 * (b + 1)], ident_bf[:]),
                                      waits=[evac0] + ([pstr_free] if b == 0 else []),
                                      inc="pe" if b == 3 else None)
                    dst = bass.AP(vtm_sb[sl], 130 * (4 * tt),
                                  [[130 * NJ, 128], [130, 4], [65, 2], [1, 64]])
                    evac = P.emit("vector",
                                  lambda e, dst=dst: e.tensor_copy(dst, PS_TR[:, 0:512]),
                                  waits=[trev, ev_guard_av], inc="dve")
                    pstr_free = evac
                    vfm_free[0] = evac
                return evac

            vfm_free = [None]
            wq_ev = {}

            def emit_pair_load(p):
                w = [wq_free] if wq_free is not None else []
                wq_ev[p] = P.dma("gpsimd", wqkv_sb[:], wqkvdev[p], waits=w, inc="wq")

            emit_pair_load(0)
            proj_evs = [emit_proj_group(0, gi) for gi in range(24)]
            proj_done[0] = proj_evs[-1]
            wq_free = P.now("pe")
            srow_free = None

            for p in range(NPAIR):
                sl = p % 2
                proj_q = list(range(24)) if p + 1 < NPAIR else []
                pace_ctr = [0]
                if p + 1 < NPAIR:
                    emit_pair_load(p + 1)
                last_av_ev = None
                last_qk_ev = None
                pend_av = None
                for t in range(NT):
                    nj = 4 * t + 4
                    for j in range(nj):
                        par = jj % 2
                        eb = jj % 4
                        w = [proj_done[p], s_free[par]]
                        P.emit("tensor",
                               lambda e, sl=sl, t=t, j=j, par=par: e.matmul(
                                   PS_S[par][:, 0:512],
                                   kp_sb[sl][0:64, 128 * j:128 * (j + 1)],
                                   qp_sb[sl][0:64, QT * t:QT * (t + 1)],
                                   start=True, stop=True, tile_position=(0, 0)),
                               waits=w)
                        qk_ev = P.emit("tensor",
                                       lambda e, sl=sl, t=t, j=j, par=par: e.matmul(
                                           PS_S[par][:, 512:1024],
                                           kp_sb[sl][64:128, 128 * j:128 * (j + 1)],
                                           qp_sb[sl][64:128, QT * t:QT * (t + 1)],
                                           start=True, stop=True, tile_position=(64, 0)),
                                       inc="pe")
                        last_qk_ev = qk_ev
                        if pend_av is not None:
                            last_av_ev = pend_av()
                            pend_av = None
                        exp_ev = P.emit("scalar",
                                        lambda e, eb=eb, par=par: e.activation(
                                            exps_sb[eb][:, :], PS_S[par][:, :], AF.Exp,
                                            bias=expbias[:, 0:1]),
                                        waits=[qk_ev, eb_free[eb]], inc="act")
                        s_free[par] = exp_ev
                        ready_a = exp_ev
                        ready_b = exp_ev
                        if j >= 4 * t:
                            o = j - 4 * t
                            ready_a = P.emit("vector",
                                             lambda e, eb=eb, o=o: e.tensor_mul(
                                                 exps_sb[eb][:, 0:512], exps_sb[eb][:, 0:512],
                                                 masks_bf[:, QT * o:QT * (o + 1)]),
                                             waits=[exp_ev], inc="dve")
                            ready_b = P.emit("vector",
                                             lambda e, eb=eb, o=o: e.tensor_mul(
                                                 exps_sb[eb][:, 512:1024], exps_sb[eb][:, 512:1024],
                                                 masks_bf[:, QT * o:QT * (o + 1)]),
                                             inc="dve")

                        def mk_av(j=j, t=t, sl=sl, eb=eb, nj=nj, ready_a=ready_a, ready_b=ready_b):
                            w = [ready_a]
                            if j == 0:
                                w.append(ctx_bank_free)
                            P.emit("tensor",
                                   lambda e: e.matmul(
                                       PS_C[0:65, 0:512],
                                       bass.AP(vtm_sb[sl], 130 * j, [[130 * NJ, 128], [1, 65]]),
                                       exps_sb[eb][:, 0:512],
                                       start=(j == 0), stop=(j == nj - 1)),
                                   waits=w)
                            ev = P.emit("tensor",
                                        lambda e: e.matmul(
                                            PS_C[0:65, 512:1024],
                                            bass.AP(vtm_sb[sl], 130 * j + 65, [[130 * NJ, 128], [1, 65]]),
                                            exps_sb[eb][:, 512:1024],
                                            start=(j == 0), stop=(j == nj - 1)),
                                        waits=[ready_b], inc="pe")
                            eb_free[eb] = ev
                            return ev
                        pend_av = mk_av
                        jj += 1
                        if p + 1 < NPAIR and proj_q:
                            pace_ctr[0] += 1
                            if pace_ctr[0] >= 6:
                                pace_ctr[0] = 0
                                gi = proj_q.pop(0)
                                ev = emit_proj_group(p + 1, gi)
                                if gi == 23:
                                    proj_done[p + 1] = ev
                    last_av_ev = pend_av()
                    pend_av = None
                    ca = P.emit("vector",
                                lambda e, sl=sl, t=t: e.tensor_copy(
                                    ctxu_sb[sl][0:64, QT * t:QT * (t + 1)], PS_C[0:64, 0:512]),
                                waits=[last_av_ev, ctxu_free[sl]], inc="dve")
                    cb = P.emit("vector",
                                lambda e, sl=sl, t=t: e.tensor_copy(
                                    ctxu_sb[sl][64:128, QT * t:QT * (t + 1)], PS_C[0:64, 512:1024]),
                                inc="dve")

                    sa = P.emit("vector",
                                lambda e, t=t: e.tensor_copy(
                                    srowA[0:1, QT * t:QT * (t + 1)], PS_C[64:65, 0:512]),
                                waits=[srow_free], inc="dve")
                    sbv = P.emit("vector",
                                 lambda e, t=t: e.tensor_copy(
                                     srowB[0:1, QT * t:QT * (t + 1)], PS_C[64:65, 512:1024]),
                                 inc="dve")
                    ctx_bank_free = sbv
                while proj_q:
                    gi = proj_q.pop(0)
                    ev = emit_proj_group(p + 1, gi)
                    if gi == 23:
                        proj_done[p + 1] = ev
                qk_last[p] = last_qk_ev
                av_last[p] = last_av_ev
                wq_free = P.now("pe")
                r2 = P.now("dve")
                gv = None
                for rr in range(16):
                    t2, h = rr // 2, rr % 2
                    srw = srowA if h == 0 else srowB
                    gv = P.emit("tensor",
                                lambda e, rr=rr, t2=t2, srw=srw: e.matmul(
                                    PS_D[0:16, 0:512], eye16_sb[0:1, 16 * rr:16 * (rr + 1)],
                                    srw[0:1, QT * t2:QT * (t2 + 1)],
                                    start=(rr == 0), stop=(rr == 15)),
                                waits=[r2, consts_ev, psd_free] if rr == 0 else [],
                                inc="pe" if rr == 15 else None)
                srow_free = gv

                def _recip16(e):
                    with nc.allow_low_precision(reason="bf16 softmax denominators"):
                        return e.reciprocal(recip16[:], PS_D[0:16, 0:512])
                r16 = P.emit("vector", _recip16, waits=[gv], inc="dve")
                psd_free = r16
                nb_ev = None
                psc_norm_free = None
                for t2 in range(NT):
                    if t2 % 2 == 0:
                        dst_ps = PS_D[:, 0:512]
                        wbc = [r16, psd_free]
                    else:
                        dst_ps = PS_C[:, 0:512]
                        wbc = [r16, psc_norm_free]
                    bm = P.emit("tensor",
                                lambda e, t2=t2, dst_ps=dst_ps: e.matmul(
                                    dst_ps, sel16_sb[:, 128 * t2:128 * (t2 + 1)],
                                    recip16[:], start=True, stop=True),
                                waits=wbc, inc="pe")
                    nb_ev = P.emit("vector",
                                   lambda e, sl=sl, t2=t2, dst_ps=dst_ps: e.tensor_mul(
                                       ctxu_sb[sl][:, QT * t2:QT * (t2 + 1)],
                                       ctxu_sb[sl][:, QT * t2:QT * (t2 + 1)], dst_ps),
                                   waits=[bm], inc="dve")
                    if t2 % 2 == 0:
                        psd_free = nb_ev
                    else:
                        psc_norm_free = nb_ev
                ctx_bank_free = psc_norm_free
                norm_done[p] = nb_ev
                ctxu_free[sl] = P.dma("sync", ctxspill[p], ctxu_sb[sl][:],
                                      waits=[nb_ev], inc=f"stc{sl}")
                spill_evs[p] = ctxu_free[sl]
                if dbg >= 2:
                    P.dma("gpsimd", dbg_out["ctx"][p], ctxspill[p],
                          waits=[ctxu_free[sl]], inc="ldg")
                    P.dma("gpsimd", dbg_out["sums"][p:p + 1, 0:T], srowA, waits=[r2], inc="ldg")
                    P.dma("gpsimd", dbg_out["sums"][p:p + 1, T:2 * T], srowB, waits=[r2], inc="ldg")

        if phases >= 3:
            bar = [P.now("pe"), P.now("act"), P.now("dve"), P.now("gps"),
                   P.now("st"), P.now("ld"), P.now("ldg")]
            wo_dst = bass.AP(xn2, 0, [[CH * T, 128], [1024, CH], [1, 1024]])
            ev_wo = P.dma("gpsimd", wo_dst, wodev, waits=bar, inc="wld")
            wk_dst = bass.AP(xn2, 8192, [[CH * T, 128], [1024, CH], [1, 1024]])
            wv_dst = bass.AP(xn2, 16384, [[CH * T, 128], [1024, CH], [1, 1024]])
            ev_wk = P.dma("gpsimd", wk_dst, wkdev, waits=bar, inc="wld")
            ev_wv = P.dma("gpsimd", wv_dst, wvdev, waits=bar, inc="wld")
            ev_wo = ev_wk = ev_wv
            wo_v = lambda c, m: xn2[:, 1024 * c + 128 * m: 1024 * c + 128 * (m + 1)]

            site_ps = [PS_C, PS_S[1]]
            site_free = [bar[2], bar[1]]
            site_idx = [0]
            rows_free = [bar[0]]

            def acquire_site():
                par = site_idx[0] % 2
                site_idx[0] += 1
                return site_ps[par], par

            def rows_and_bcast(ps, par, sl, stats_pe):
                a = P.emit("vector", lambda e, ps=ps: e.tensor_scalar_mul(
                               r_a[0:1, :], ps[0:1, 0:512], 1.0 / D),
                           waits=[stats_pe, rows_free[0]], inc="dve")
                m2 = P.emit("vector", lambda e: e.tensor_mul(r_c[0:1, :], r_a[0:1, :], r_a[0:1, :]),
                            waits=[a], inc="dve")
                v = P.emit("vector", lambda e, ps=ps: e.scalar_tensor_tensor(
                               r_b[0:1, :], ps[0:1, 512:1024], 1.0 / D, r_c[0:1, :],
                               ALU.mult, ALU.subtract),
                           waits=[m2], inc="dve")
                ln = P.emit("scalar", lambda e: e.activation(
                                r_b[0:1, :], r_b[0:1, :], AF.Ln,
                                bias=epscol[0:1, 0:1]),
                            waits=[v], inc="act")
                ex = P.emit("scalar", lambda e: e.activation(r_c[0:1, :], r_b[0:1, :], AF.Exp,
                                                             scale=neghalf[0:1, 0:1]),
                            waits=[ln], inc="act")
                mr = P.emit("vector", lambda e: e.tensor_mul(r_a[0:1, :], r_a[0:1, :], r_c[0:1, :]),
                            waits=[ex], inc="dve")
                evb1 = P.emit("tensor", lambda e, ps=ps: e.matmul(
                                  ps[:, 0:512], onesf_row[0:1, :], r_c[0:1, :],
                                  start=True, stop=True),
                              waits=[ex, mr], inc="pe")
                evb2 = P.emit("tensor", lambda e, ps=ps: e.matmul(
                                  ps[:, 512:1024], onesf_row[0:1, :], r_a[0:1, :],
                                  start=True, stop=True),
                              waits=[mr], inc="pe")
                rows_free[0] = evb2
                evc1 = P.emit("vector", lambda e, sl=sl, ps=ps: e.tensor_copy(
                                  bc_t[sl][:, 0, :], ps[:, 0:512]),
                              waits=[evb1, P.now("gps")], inc="dve")
                evc2 = P.emit("vector", lambda e, sl=sl, ps=ps: e.tensor_copy(
                                  bc_t[sl][:, 1, :], ps[:, 512:1024]),
                              waits=[evb2], inc="dve")
                site_free[par] = evc2
                return evc1, evc2

            mm_reg2 = [(PS_S[0], 0), (PS_S[0], 512)]
            reg_free = [bar[1], bar[1]]
            au_free = [[], []]
            cx_free = [None, None]
            x2_evs = {}
            xf_free2 = list(xf_free)
            x2sm_free = [None] * 4
            wo_mm_last = None

            for t in range(NT):
                sl = t % 2
                cx = ctxu_sb[sl]
                ev_cx = []
                for c in range(CH):
                    ev_cx.append(P.dma("sync", cx[:, QT * c:QT * (c + 1)],
                                       ctxspill[c, :, QT * t:QT * (t + 1)],
                                       waits=[spill_evs[6], spill_evs[7]]
                                       + ([cx_free[sl]] if c == 0 else []),
                                       inc=f"cx{sl}"))
                ev_cx = [ev_cx[-1]] * CH
                au = xbf_t[sl]
                evac_evs = {}
                cs_evs = {}
                ps, par = acquire_site()
                for m in range(CH):
                    reg, off = mm_reg2[m % 2]
                    for c in range(CH):
                        w = [ev_wo, ev_cx[c]]
                        if c == 0:
                            w.append(reg_free[m % 2])
                        ev = P.emit("tensor",
                                    lambda e, c=c, m=m, reg=reg, off=off, cx=cx: e.matmul(
                                        reg[:, off:off + 512], wo_v(c, m),
                                        cx[:, QT * c:QT * (c + 1)],
                                        start=(c == 0), stop=(c == CH - 1)),
                                    waits=w, inc="pe" if c == CH - 1 else None)
                    wo_mm_last = ev
                    ae = P.emit("vector",
                                lambda e, m=m, reg=reg, off=off, au=au: e.tensor_copy(
                                    au[:, m, :], reg[:, off:off + 512]),
                                waits=[ev] + (au_free[sl] if m == 0 else []), inc="dve")
                    reg_free[m % 2] = ae
                    evac_evs[m] = ae
                    sqeng, sqsem = ("vector", "dve")
                    sq = P.emit(sqeng,
                                lambda e, m=m, au=au: e.tensor_mul(
                                    xsq_t[:, m % 2, :], au[:, m, :], au[:, m, :]),
                                waits=[ae, cs_evs.get(m - 2)], inc=sqsem)
                    P.emit("tensor",
                           lambda e, m=m, au=au, ps=ps: e.matmul(
                               ps[0:1, 0:512], ones_col[:], au[:, m, :],
                               start=(m == 0), stop=(m == CH - 1)),
                           waits=[ae] + ([site_free[par]] if m == 0 else []))
                    cs_evs[m] = P.emit("tensor",
                                       lambda e, m=m, ps=ps: e.matmul(
                                           ps[0:1, 512:1024], ones_col[:], xsq_t[:, m % 2, :],
                                           start=(m == 0), stop=(m == CH - 1)),
                                       waits=[sq], inc="pe")
                evc1, evc2 = rows_and_bcast(ps, par, sl, cs_evs[CH - 1])
                if dbg >= 5 and phases == 3:
                    au_dump = P.dma("gpsimd", dbg_out["au"][t], au[:],
                                    waits=[evac_evs[CH - 1]], inc="ldg")
                    au_free[sl] = au_free[sl] + [au_dump]
                for c in range(CH):
                    fsl = (t * CH + c) % 4
                    eng, esem = ("vector", "dve")
                    ev_xf = P.dma("sync", xf_t[fsl][:], xdev[:, c, QT * t:QT * (t + 1)],
                                  waits=[xf_free2[fsl]], inc=f"xf{fsl}")
                    s1 = P.emit(eng,
                                lambda e, c=c, fsl=fsl, sl=sl, au=au: e.tensor_mul(
                                    x2sm[fsl][:], au[:, c, :], bc_t[sl][:, 0, :]),
                                waits=[evc1, evac_evs[c], x2sm_free[fsl]], inc=esem)
                    s2 = P.emit(eng,
                                lambda e, fsl=fsl, sl=sl: e.tensor_sub(
                                    x2sm[fsl][:], x2sm[fsl][:], bc_t[sl][:, 1, :]),
                                waits=[evc2], inc=esem)
                    s3 = P.emit(eng,
                                lambda e, fsl=fsl: e.tensor_add(
                                    x2sm[fsl][:], x2sm[fsl][:], xf_t[fsl][:]),
                                waits=[ev_xf], inc=esem)
                    xf_free2[fsl] = s3
                    ev_st = P.dma("sync", x2spill[:, c, QT * t:QT * (t + 1)], x2sm[fsl][:],
                                  waits=[s3], inc=f"stx{fsl}")
                    x2sm_free[fsl] = ev_st
                    x2_evs[(t, c)] = ev_st
                    if c >= CH - 2:
                        au_free[sl] = (au_free[sl] + [s1])[-2:] if c == CH - 1 else [s1]
                    if dbg >= 3 and phases == 3:
                        P.dma("gpsimd", dbg_out["x2"][:, c, QT * t:QT * (t + 1)],
                              x2spill[:, c, QT * t:QT * (t + 1)], waits=[ev_st], inc="ldg")
                cx_free[sl] = P.now("pe")

        if phases >= 4:
            wr_dst = bass.AP(xn2, 0, [[CH * T, 128], [1024, CH], [1, 1024]])
            ev_wr = P.dma("gpsimd", wr_dst, wrdev, waits=[wo_mm_last], inc="wrd")
            wr_v = lambda c, m: xn2[:, 1024 * c + 128 * m: 1024 * c + 128 * (m + 1)]
            wk_v = lambda c, m: xn2[:, 8192 + 1024 * c + 128 * m: 8192 + 1024 * c + 128 * (m + 1)]
            wv_v = lambda c, m: xn2[:, 16384 + 1024 * c + 128 * m: 16384 + 1024 * c + 128 * (m + 1)]
            Z0 = 24576
            zv = lambda c, i0, n: xn2[:, Z0 + 513 * c + i0: Z0 + 513 * c + i0 + n]
            z_head = bass.AP(xn2, Z0, [[CH * T, 128], [513, CH], [1, 1]])
            z_tail = bass.AP(xn2, Z0 + 512, [[CH * T, 128], [513, CH], [1, 1]])
            xr_v = lambda c: qp_sb[0][:, QT * c:QT * (c + 1)]
            xk_v = lambda c: qp_sb[1][:, QT * c:QT * (c + 1)]
            r_v = lambda m: kp_sb[0][:, QT * m:QT * (m + 1)]
            ka_v = lambda m: kp_sb[1][:, QT * m:QT * (m + 1)]
            g_v = lambda m: vtm_sb[0][:, QT * m:QT * (m + 1)]

            stx_finals = [P.now(f"stx{i}") for i in range(4)]
            mm_reg3 = [(PS_S[0], 0), (PS_S[0], 512), (PS_D, 0)]
            reg_free3 = [P.now("act"), P.now("act"), P.now("dve")]
            x2b_free = list(cx_free)
            xq_free = list(xf_free2)
            xo_free = [None] * 3
            zcopy_prev = None
            xr_done_prev = []
            g_free = []
            prev_cs4 = {}
            rk_reads_z = None

            for t in range(NT):
                sl = t % 2
                x2b = bass.AP(ctxu_sb[sl], 0, [[T, 128], [512, CH], [1, 512]])
                ev_x2b = P.dma("gpsimd", x2b, x2spill[:, :, QT * t:QT * (t + 1)],
                               waits=[x2b_free[sl]] + stx_finals,
                               inc=f"x2b{sl}")
                cs3 = {}
                ps, par = acquire_site()
                for c in range(CH):
                    sqeng, sqsem = ("vector", "dve")
                    w_guard = cs3.get(c - 2) if c >= 2 else prev_cs4.get(CH - 2 + c)
                    sq = P.emit(sqeng,
                                lambda e, c=c, sl=sl: e.tensor_mul(
                                    xsq_t[:, c % 2, :],
                                    bass.AP(ctxu_sb[sl], 512 * c, [[T, 128], [1, 512]]),
                                    bass.AP(ctxu_sb[sl], 512 * c, [[T, 128], [1, 512]])),
                                waits=[ev_x2b, w_guard], inc=sqsem)
                    P.emit("tensor",
                           lambda e, c=c, sl=sl, ps=ps: e.matmul(
                               ps[0:1, 0:512], ones_col[:],
                               bass.AP(ctxu_sb[sl], 512 * c, [[T, 128], [1, 512]]),
                               start=(c == 0), stop=(c == CH - 1)),
                           waits=[ev_x2b] + ([site_free[par]] if c == 0 else []))
                    cs3[c] = P.emit("tensor",
                                    lambda e, c=c, ps=ps: e.matmul(
                                        ps[0:1, 512:1024], ones_col[:], xsq_t[:, c % 2, :],
                                        start=(c == 0), stop=(c == CH - 1)),
                                    waits=[sq], inc="pe")
                evc1, evc2 = rows_and_bcast(ps, par, sl, cs3[CH - 1])
                if t == 0:
                    zb = P.emit("vector", lambda e: e.memset(z_head, 0.0),
                                waits=[ev_wv], inc="dve")
                else:
                    zb = P.emit("vector", lambda e: e.tensor_copy(z_head, z_tail),
                                waits=xr_done_prev, inc="dve")
                xr_evs = {}
                xq_evs = {}
                tcol_free = {}
                for c in range(4):
                    fsl = c % 4
                    xq_evs[c] = P.dma("sync", xf_t[fsl][:], x2spill[:, c, QT * t:QT * (t + 1)],
                                      waits=[xq_free[fsl]] + stx_finals, inc=f"xf{fsl}")
                for c in range(CH):
                    fsl = c % 4
                    eng, esem = ("vector", "dve")
                    ev_xq = xq_evs[c]
                    z1 = P.emit(eng,
                                lambda e, fsl=fsl, sl=sl: e.tensor_mul(
                                    xf_t[fsl][:], xf_t[fsl][:], bc_t[sl][:, 0, :]),
                                waits=[ev_xq, evc1, rk_reads_z], inc=esem)
                    z2 = P.emit(eng,
                                lambda e, c=c, fsl=fsl, sl=sl: e.tensor_sub(
                                    zv(c, 1, QT), xf_t[fsl][:], bc_t[sl][:, 1, :]),
                                waits=[evc2, zb], inc=esem)
                    xq_free[fsl] = z2
                    if c + 4 < CH:
                        xq_evs[c + 4] = P.dma("sync", xf_t[fsl][:],
                                              x2spill[:, c + 4, QT * t:QT * (t + 1)],
                                              waits=[z2], inc=f"xf{fsl}")
                    tcol = vtm_sb[1][:, 0:512] if c % 2 == 0 else vtm_sb[1][:, 512:1024]
                    d1 = P.emit(eng,
                                lambda e, c=c, tcol=tcol: e.tensor_sub(
                                    tcol, zv(c, 1, QT), zv(c, 0, QT)),
                                waits=[z2, zcopy_prev, tcol_free.get(c % 2)], inc=esem)
                    x1 = P.emit("vector",
                                lambda e, c=c, tcol=tcol: e.scalar_tensor_tensor(
                                    xr_v(c), tcol, mu_sb[:, c:c + 1], zv(c, 0, QT),
                                    ALU.mult, ALU.add),
                                waits=[d1], inc="dve")
                    x2e = P.emit("vector",
                                 lambda e, c=c, tcol=tcol: e.scalar_tensor_tensor(
                                     xk_v(c), tcol, mu_sb[:, 8 + c:9 + c], zv(c, 0, QT),
                                     ALU.mult, ALU.add),
                                 waits=[x1], inc="dve")
                    xr_evs[c] = x2e
                    tcol_free[c % 2] = x2e
                xr_done_prev = [P.now("dve"), P.now("gps")]
                sig_evs = {}
                for m in range(CH):
                    reg, off = mm_reg3[m % 3]
                    for c in range(CH):
                        w = [ev_wr, xr_evs[c]]
                        if c == 0:
                            w.append(reg_free3[m % 3])
                        ev = P.emit("tensor",
                                    lambda e, c=c, m=m, reg=reg, off=off: e.matmul(
                                        reg[:, off:off + 512], wr_v(c, m), xr_v(c),
                                        start=(c == 0), stop=(c == CH - 1)),
                                    waits=w, inc="pe" if c == CH - 1 else None)
                    sig_evs[m] = P.emit("scalar",
                                        lambda e, m=m, reg=reg, off=off: e.activation(
                                            r_v(m), reg[:, off:off + 512], AF.Sigmoid),
                                        waits=[ev] + (g_free if m == 0 else []), inc="act")
                    reg_free3[m % 3] = sig_evs[m]
                ka_evs = {}
                for m in range(CH):
                    reg, off = mm_reg3[m % 3]
                    for c in range(CH):
                        w = [ev_wk, xr_evs[c]]
                        if c == 0:
                            w.append(reg_free3[m % 3])
                        ev = P.emit("tensor",
                                    lambda e, c=c, m=m, reg=reg, off=off: e.matmul(
                                        reg[:, off:off + 512], wk_v(c, m), xk_v(c),
                                        start=(c == 0), stop=(c == CH - 1)),
                                    waits=w, inc="pe" if c == CH - 1 else None)
                    rl = P.emit("scalar",
                                lambda e, m=m, reg=reg, off=off: e.activation(
                                    ka_v(m), reg[:, off:off + 512], AF.Relu),
                                waits=[ev], inc="act")
                    reg_free3[m % 3] = rl
                    ka_evs[m] = P.emit("vector",
                                       lambda e, m=m: e.tensor_mul(ka_v(m), ka_v(m), ka_v(m)),
                                       waits=[rl], inc="dve")
                gs_evs = {}
                cs4 = {}
                ps2, par2 = acquire_site()
                gfree = g_free
                for m in range(CH):
                    reg, off = mm_reg3[m % 3]
                    for c in range(CH):
                        w = [ev_wv, ka_evs[c]]
                        if c == 0:
                            w.append(reg_free3[m % 3])
                        ev = P.emit("tensor",
                                    lambda e, c=c, m=m, reg=reg, off=off: e.matmul(
                                        reg[:, off:off + 512], wv_v(c, m), ka_v(c),
                                        start=(c == 0), stop=(c == CH - 1)),
                                    waits=w, inc="pe" if c == CH - 1 else None)
                    ge = P.emit("vector",
                                lambda e, m=m, reg=reg, off=off: e.tensor_mul(
                                    g_v(m), r_v(m), reg[:, off:off + 512]),
                                waits=[ev, sig_evs[m]] + (gfree if m == 0 else []), inc="dve")
                    reg_free3[m % 3] = ge
                    gs_evs[m] = ge
                    sqeng, sqsem = ("vector", "dve")
                    sqg = P.emit(sqeng,
                                 lambda e, m=m: e.tensor_mul(
                                     xsq_t[:, m % 2, :], g_v(m), g_v(m)),
                                 waits=[ge, cs4.get(m - 2)], inc=sqsem)
                    P.emit("tensor",
                           lambda e, m=m, ps2=ps2: e.matmul(
                               ps2[0:1, 0:512], ones_col[:], g_v(m),
                               start=(m == 0), stop=(m == CH - 1)),
                           waits=[ge] + ([site_free[par2]] if m == 0 else []))
                    cs4[m] = P.emit("tensor",
                                    lambda e, m=m, ps2=ps2: e.matmul(
                                        ps2[0:1, 512:1024], ones_col[:], xsq_t[:, m % 2, :],
                                        start=(m == 0), stop=(m == CH - 1)),
                                    waits=[sqg], inc="pe")
                evg1, evg2 = rows_and_bcast(ps2, par2, sl, cs4[CH - 1])
                xo_evs = {}
                for c in range(3):
                    osl = c % 3
                    xo_evs[c] = P.dma("sync", osm[osl][:], x2spill[:, c, QT * t:QT * (t + 1)],
                                      waits=[xo_free[osl]], inc=f"osm{osl}")
                for c in range(CH):
                    fsl = c % 4
                    osl = c % 3
                    eng, esem = ("vector", "dve")
                    o1 = P.emit(eng,
                                lambda e, c=c, fsl=fsl, sl=sl: e.tensor_mul(
                                    x2sm[fsl][:], g_v(c), bc_t[sl][:, 0, :]),
                                waits=[evg1, gs_evs[c], x2sm_free[fsl]], inc=esem)
                    o2 = P.emit(eng,
                                lambda e, fsl=fsl, sl=sl: e.tensor_sub(
                                    x2sm[fsl][:], x2sm[fsl][:], bc_t[sl][:, 1, :]),
                                waits=[evg2], inc=esem)
                    o3 = P.emit(eng,
                                lambda e, fsl=fsl, osl=osl: e.tensor_add(
                                    x2sm[fsl][:], x2sm[fsl][:], osm[osl][:]),
                                waits=[xo_evs[c]], inc=esem)
                    xo_free[osl] = o3
                    if c + 3 < CH:
                        xo_evs[c + 3] = P.dma("sync", osm[osl][:],
                                              x2spill[:, c + 3, QT * t:QT * (t + 1)],
                                              waits=[o3], inc=f"osm{osl}")
                    ev_st = P.dma("sync", outdev[:, c, QT * t:QT * (t + 1)], x2sm[fsl][:],
                                  waits=[o3], inc=f"stx{fsl}")
                    x2sm_free[fsl] = ev_st
                x2b_free[sl] = P.now("pe")
                rk_reads_z = P.now("pe")
                zcopy_prev = zb
                g_free = [P.now("dve"), P.now("gps")]
                prev_cs4 = cs4

        P.emit("sync", lambda e: e.nop(),
               waits=[P.now("st"), P.now("dve"), P.now("act"), P.now("pe"),
                      P.now("gps"), P.now("ld"), P.now("ldg"),
                      P.now("stx0"), P.now("stx1"), P.now("stx2"), P.now("stx3"),
                      P.now("stc0"), P.now("stc1")])
        P.replay()
    return nc, dbg_out


from concourse.bass_utils import run_bass_kernel_spmd

_NC_CACHE = {}


def kernel(**inputs):
    if "nc" not in _NC_CACHE:
        _NC_CACHE["nc"] = build(dbg=0, phases=4)[0]
    nc = _NC_CACHE["nc"]
    in_maps = [host_inputs(inputs, core) for core in range(8)]
    res = run_bass_kernel_spmd(nc, in_maps, core_ids=list(range(8)))
    outs = [host_output(r) for r in res.results]
    return np.stack(outs, axis=0).astype(np.float32)


# version 11
# speedup vs baseline: 1.2267x; 1.2267x over previous
import numpy as np
import ml_dtypes
import concourse.bass as bass
import concourse.mybir as mybir

F32 = mybir.dt.float32
BF16 = mybir.dt.bfloat16
AF = mybir.ActivationFunctionType
ALU = mybir.AluOpType

T = 4096
D = 1024
CH = 8
NT = 8
QT = 512
NJ = 32
NPAIR = 8
DH = 64
EXP_SHIFT = -12.0

ENGINES = ("tensor", "scalar", "vector", "gpsimd", "sync")


class MK:

    def __init__(self, nc):
        self.nc = nc
        self.ops = {e: [] for e in ENGINES}
        self.sems = {}
        self.cnt = {}
        self.last_wait = {e: {} for e in ENGINES}

    def sem(self, name):
        self.cnt[name] = 0
        return name

    def emit(self, eng, fn, waits=(), inc=None, n=1):
        ws = []
        for wv in waits:
            if wv is None:
                continue
            s, v = wv
            lw = self.last_wait[eng].get(s, -1)
            if v > lw:
                ws.append((s, v))
                self.last_wait[eng][s] = v
        ev = None
        if inc is not None:
            self.cnt[inc] += n
            ev = (inc, self.cnt[inc])
        self.ops[eng].append((ws, fn, (inc, n) if inc else None))
        return ev

    def dma(self, eng, out, in_, waits=(), inc="ld"):
        return self.emit(eng, lambda e: e.dma_start(out=out, in_=in_),
                         waits=waits, inc=inc, n=16)

    def now(self, sem):
        return (sem, self.cnt[sem])

    def replay(self):
        nc = self.nc
        from contextlib import ExitStack
        with ExitStack() as st:
            for name in self.cnt:
                self.sems[name] = st.enter_context(nc.semaphore(name))
            with nc.Block() as block:
                for eng_name in ENGINES:
                    deco = getattr(block, eng_name)

                    def body(eng, _ops=self.ops[eng_name]):
                        for ws, fn, inc in _ops:
                            for s, v in ws:
                                eng.wait_ge(self.sems[s], v)
                            instr = fn(eng)
                            if inc is not None:
                                instr.then_inc(self.sems[inc[0]], inc[1])
                    deco(body)


def host_inputs(inputs, core):
    x = np.asarray(inputs["L0_attn_x"])
    Wqkv = np.asarray(inputs["L0_attn_W_qkv"])
    Wo = np.asarray(inputs["L0_attn_W_o"])
    Wr = np.asarray(inputs["L0_ffn_W_r"])
    Wk = np.asarray(inputs["L0_ffn_W_k"])
    Wv = np.asarray(inputs["L0_ffn_W_v"])
    g = np.asarray(inputs["L0_attn_ln_gamma"])
    b = np.asarray(inputs["L0_attn_ln_beta"])
    mur = np.asarray(inputs["L0_ffn_mu_r"])
    muk = np.asarray(inputs["L0_ffn_mu_k"])

    d = {}
    xT = np.ascontiguousarray(x[core].T)
    d["xdev"] = np.ascontiguousarray(xT.reshape(CH, 128, T).transpose(1, 0, 2))
    rows = []
    for pr in range(NPAIR):
        h0, h1 = 2 * pr, 2 * pr + 1
        for sec in range(3):
            rows.append(Wqkv[192 * h0 + 64 * sec: 192 * h0 + 64 * sec + 64])
            rows.append(Wqkv[192 * h1 + 64 * sec: 192 * h1 + 64 * sec + 64])
    Wq_re = np.concatenate(rows, axis=0)
    WqT = Wq_re.T
    wq = WqT.reshape(CH, 128, NPAIR, 384).transpose(2, 1, 0, 3)
    d["wqkvdev"] = np.ascontiguousarray(wq)
    for name, W in (("wodev", Wo), ("wrdev", Wr), ("wkdev", Wk), ("wvdev", Wv)):
        WT = W.T
        d[name] = np.ascontiguousarray(WT.reshape(CH, 128, D).transpose(1, 0, 2))
    d["gbdev"] = np.ascontiguousarray(
        np.concatenate([g.reshape(CH, 128).T, b.reshape(CH, 128).T], axis=1))
    d["mudev"] = np.ascontiguousarray(
        np.concatenate([mur.reshape(CH, 128).T, muk.reshape(CH, 128).T], axis=1))
    d["identdev"] = np.eye(128, dtype=np.float32)
    d["eye16dev"] = np.eye(16, dtype=np.float32).reshape(1, 256)
    sel16 = np.zeros((16, 8 * 128), dtype=np.float32)
    for t in range(8):
        sel16[2 * t, 128 * t:128 * t + 64] = 1.0
        sel16[2 * t + 1, 128 * t + 64:128 * t + 128] = 1.0
    d["sel16dev"] = sel16
    masks = np.zeros((128, 4 * QT), dtype=np.float32)
    qq = np.arange(QT)
    kk = np.arange(128)
    for k in range(4):
        masks[:, k * QT:(k + 1) * QT] = (qq[None, :] - 128 * k >= kk[:, None])
    d["maskdev"] = masks
    return d


def host_output(r):
    o = r["outdev"]
    return np.ascontiguousarray(o.transpose(1, 0, 2).reshape(D, T).T)


def build(dbg=0, phases=4):
    nc = bass.Bass("TRN2", target_bir_lowering=False)
    P = MK(nc)
    for s in ("ld", "ldg", "st", "pe", "act", "dve", "gps", "cst", "wq",
              "wld", "wrd", "xbf0", "xbf1", "cx0", "cx1", "x2b0", "x2b1",
              "xf0", "xf1", "xf2", "xf3", "osm0", "osm1", "osm2",
              "stx0", "stx1", "stx2", "stx3", "stc0", "stc1"):
        P.sem(s)

    xdev = nc.dram_tensor("xdev", [128, CH, T], F32, kind="ExternalInput").ap()
    wqkvdev = nc.dram_tensor("wqkvdev", [NPAIR, 128, CH, 384], F32, kind="ExternalInput").ap()
    wodev = nc.dram_tensor("wodev", [128, CH, D], F32, kind="ExternalInput").ap()
    wrdev = nc.dram_tensor("wrdev", [128, CH, D], F32, kind="ExternalInput").ap()
    wkdev = nc.dram_tensor("wkdev", [128, CH, D], F32, kind="ExternalInput").ap()
    wvdev = nc.dram_tensor("wvdev", [128, CH, D], F32, kind="ExternalInput").ap()
    gbdev = nc.dram_tensor("gbdev", [128, 16], F32, kind="ExternalInput").ap()
    mudev = nc.dram_tensor("mudev", [128, 16], F32, kind="ExternalInput").ap()
    identdev = nc.dram_tensor("identdev", [128, 128], F32, kind="ExternalInput").ap()
    eye16dev = nc.dram_tensor("eye16dev", [1, 256], F32, kind="ExternalInput").ap()
    sel16dev = nc.dram_tensor("sel16dev", [16, 1024], F32, kind="ExternalInput").ap()
    maskdev = nc.dram_tensor("maskdev", [128, 4 * QT], F32, kind="ExternalInput").ap()
    outdev = nc.dram_tensor("outdev", [128, CH, T], F32, kind="ExternalOutput").ap()
    ctxspill = nc.dram_tensor("ctxspill", [NPAIR, 128, T], BF16, kind="Internal").ap()
    x2spill = nc.dram_tensor("x2spill", [128, CH, T], F32, kind="Internal").ap()
    dbg_out = {}
    if dbg >= 1 and phases == 1:
        dbg_out["xn2"] = nc.dram_tensor("dbg_xn2", [128, CH, T], F32, kind="ExternalOutput").ap()
    if dbg >= 2:
        dbg_out["ctx"] = nc.dram_tensor("dbg_ctx", [NPAIR, 128, T], F32, kind="ExternalOutput").ap()
        dbg_out["sums"] = nc.dram_tensor("dbg_sums", [NPAIR, 2 * T], F32, kind="ExternalOutput").ap()
    if dbg >= 3 and phases == 3:
        dbg_out["x2"] = nc.dram_tensor("dbg_x2", [128, CH, T], F32, kind="ExternalOutput").ap()
    if dbg >= 5 and phases == 3:
        dbg_out["au"] = nc.dram_tensor("dbg_au", [NT, 128, CH, QT], F32, kind="ExternalOutput").ap()

    from contextlib import ExitStack
    with ExitStack() as stack:
        sb = lambda name, shape, dt: stack.enter_context(nc.sbuf_tensor(name, shape, dt))
        psal = lambda name, shape, dt: stack.enter_context(nc.psum_tensor(name, shape, dt))

        gb_sb = sb("gb_sb", [128, 16], F32)
        mu_sb = sb("mu_sb", [128, 32], F32)
        ident_bf = sb("ident_bf", [128, 128], BF16)
        masks_bf = sb("masks_bf", [128, 4 * QT], BF16)
        ones_col = sb("ones_col", [128, 1], BF16)
        onesf_row = sb("onesf_row", [1, 128], F32)
        expbias = sb("expbias", [128, 1], F32)
        neghalf = sb("neghalf", [128, 1], F32)
        epscol = sb("epscol", [128, 1], F32)

        xn2 = sb("xn2", [128, CH * T], BF16)
        r_a = sb("r_a", [1, QT], F32)
        r_b = sb("r_b", [1, QT], F32)
        r_c = sb("r_c", [1, QT], F32)
        recip16 = sb("recip16", [16, QT], BF16)
        sel16_sb = sb("sel16_sb", [16, 1024], BF16)
        eye16_sb = sb("eye16_sb", [1, 256], BF16)

        xbf_t = [sb(f"xbf{i}", [128, CH, QT], BF16) for i in range(2)]
        srowA = bass.AP(xbf_t[0], 0, [[CH * QT, 1], [1, CH * QT]])
        srowB = bass.AP(xbf_t[1], 0, [[CH * QT, 1], [1, CH * QT]])
        xsq_t = sb("xsq", [128, 2, QT], BF16)
        xf_t = [sb(f"xf{i}", [128, QT], F32) for i in range(4)]
        bc_t = [sb(f"bc{i}", [128, 2, QT], F32) for i in range(2)]

        qp_sb = [sb(f"qp{i}", [128, T], BF16) for i in range(2)]
        kp_sb = [sb(f"kp{i}", [128, T], BF16) for i in range(2)]
        vtm_sb = [sb(f"vtm{i}", [128, NJ * 130], BF16) for i in range(2)]
        vfm_st = sb("vfm_st", [128, QT], BF16)
        wqkv_sb = sb("wqkv_sb", [128, 3072], BF16)
        exps_sb = [sb(f"exps{i}", [128, 1024], BF16) for i in range(4)]
        ctxu_sb = [sb(f"ctxu{i}", [128, T], BF16) for i in range(2)]
        x2sm = [sb(f"x2sm{i}", [128, QT], F32) for i in range(4)]
        osm = [sb(f"osm{i}", [128, QT], F32) for i in range(3)]

        wo_v = lambda c, m: xn2[:, 1024 * c + 128 * m: 1024 * c + 128 * (m + 1)]

        PS_S = [psal("ps_s0", [128, 1024], F32), psal("ps_s1", [128, 1024], F32)]
        PS_C = psal("ps_c", [128, 1024], F32)
        PS_D = psal("ps_d", [128, 512], F32)
        PS_TR = psal("ps_tr", [128, 1024], BF16)

        P.dma("sync", gb_sb[:], gbdev, inc="cst")
        P.dma("sync", mu_sb[:, 0:16], mudev, inc="cst")
        P.dma("gpsimd", ident_bf[:], identdev, inc="cst")
        P.dma("gpsimd", masks_bf[:], maskdev, inc="cst")
        P.dma("gpsimd", eye16_sb[:], eye16dev, inc="cst")
        ev_cst = P.dma("gpsimd", sel16_sb[:], sel16dev, inc="cst")
        ld0 = [ev_cst]
        for f in (lambda e: e.memset(ones_col[:], 1.0),
                  lambda e: e.memset(onesf_row[:], 1.0),
                  lambda e: e.memset(expbias[:], EXP_SHIFT),
                  lambda e: e.memset(neghalf[:], -0.5),
                  lambda e: e.memset(epscol[:], 1e-5),
                  lambda e: e.memset(vtm_sb[0][:], 1.0),
                  lambda e: e.memset(vtm_sb[1][:], 1.0)):
            P.emit("vector", f, inc="dve")
        P.emit("vector", lambda e: e.tensor_scalar(mu_sb[:, 16:32], mu_sb[:, 0:16],
                                                   -1.0, 1.0, ALU.mult, ALU.add),
               waits=ld0, inc="dve")
        consts_ev = P.now("dve")

        def xn2v(c, t0, n):
            return xn2[:, T * c + t0: T * c + t0 + n]

        xbf_free = [None, None]
        xf_free = [None] * 4
        sxx_mm_ev = {}
        stats_bank_free = None
        bcast_bank_free = None
        rows_free = None
        xn2_evs = {}

        for t in range(NT):
            sl = t % 2
            ev_x = P.dma("gpsimd", xbf_t[sl][:], xdev[:, :, QT * t:QT * (t + 1)],
                         waits=[xbf_free[sl]], inc=f"xbf{sl}")
            for c in range(CH):
                evq = P.emit("vector",
                             lambda e, c=c, sl=sl: e.tensor_mul(
                                 xsq_t[:, c % 2, :], xbf_t[sl][:, c, :], xbf_t[sl][:, c, :]),
                             waits=[ev_x, sxx_mm_ev.get(c - 2)], inc="dve")
                P.emit("tensor",
                       lambda e, c=c, sl=sl: e.matmul(
                           PS_S[0][0:1, 0:512], ones_col[:], xbf_t[sl][:, c, :],
                           start=(c == 0), stop=(c == CH - 1)),
                       waits=[ev_x, consts_ev] + ([stats_bank_free] if c == 0 else []))
                sxx_mm_ev[c] = P.emit("tensor",
                                      lambda e, c=c: e.matmul(
                                          PS_S[0][0:1, 512:1024], ones_col[:], xsq_t[:, c % 2, :],
                                          start=(c == 0), stop=(c == CH - 1)),
                                      waits=[evq], inc="pe")
            stats_pe = P.now("pe")
            xbf_free[sl] = stats_pe
            ev_sx = P.emit("vector",
                           lambda e: e.tensor_copy(r_a[0:1, :], PS_S[0][0:1, 0:512]),
                           waits=[stats_pe, rows_free], inc="dve")
            ev_sxx = P.emit("vector",
                            lambda e: e.tensor_copy(r_b[0:1, :], PS_S[0][0:1, 512:1024]),
                            waits=[stats_pe], inc="dve")
            stats_bank_free = ev_sxx
            e1 = P.emit("vector", lambda e: e.tensor_scalar_mul(r_a[0:1, :], r_a[0:1, :], 1.0 / D),
                        waits=[ev_sx], inc="dve")
            e2 = P.emit("vector", lambda e: e.tensor_scalar_mul(r_b[0:1, :], r_b[0:1, :], 1.0 / D),
                        waits=[ev_sxx], inc="dve")
            e3 = P.emit("vector", lambda e: e.tensor_mul(r_c[0:1, :], r_a[0:1, :], r_a[0:1, :]),
                        waits=[e1], inc="dve")
            e4 = P.emit("vector", lambda e: e.tensor_sub(r_b[0:1, :], r_b[0:1, :], r_c[0:1, :]),
                        waits=[e2, e3], inc="dve")
            e5 = P.emit("vector", lambda e: e.tensor_scalar(r_b[0:1, :], r_b[0:1, :],
                                                            1.0 + 1e-5, 1e-10, ALU.mult, ALU.add),
                        waits=[e4], inc="dve")
            e6 = P.emit("scalar", lambda e: e.activation(r_b[0:1, :], r_b[0:1, :], AF.Ln),
                        waits=[e5], inc="act")
            e7 = P.emit("scalar", lambda e: e.activation(r_c[0:1, :], r_b[0:1, :], AF.Exp,
                                                         scale=neghalf[0:1, 0:1]),
                        waits=[e6], inc="act")
            e8 = P.emit("vector", lambda e: e.tensor_mul(r_a[0:1, :], r_a[0:1, :], r_c[0:1, :]),
                        waits=[e7], inc="dve")
            evb1 = P.emit("tensor",
                          lambda e: e.matmul(PS_S[1][:, 0:512], onesf_row[0:1, :],
                                             r_c[0:1, :], start=True, stop=True),
                          waits=[e7, e8, bcast_bank_free], inc="pe")
            evb2 = P.emit("tensor",
                          lambda e: e.matmul(PS_S[1][:, 512:1024], onesf_row[0:1, :],
                                             r_a[0:1, :], start=True, stop=True),
                          waits=[e8], inc="pe")
            rows_free = evb2
            evc1 = P.emit("vector",
                          lambda e, sl=sl: e.tensor_copy(bc_t[sl][:, 0, :], PS_S[1][:, 0:512]),
                          waits=[evb1, P.now("gps")], inc="dve")
            evc2 = P.emit("vector",
                          lambda e, sl=sl: e.tensor_copy(bc_t[sl][:, 1, :], PS_S[1][:, 512:1024]),
                          waits=[evb2], inc="dve")
            bcast_bank_free = evc2
            for c in range(CH):
                fsl = (t * CH + c) % 4
                eng, esem = ("vector", "dve")
                ev_xf = P.dma("sync", xf_t[fsl][:], xdev[:, c, QT * t:QT * (t + 1)],
                              waits=[xf_free[fsl]], inc=f"xf{fsl}")
                P.emit(eng,
                       lambda e, fsl=fsl, sl=sl: e.tensor_mul(xf_t[fsl][:], xf_t[fsl][:], bc_t[sl][:, 0, :]),
                       waits=[ev_xf, evc1], inc=esem)
                esub = P.emit(eng,
                              lambda e, fsl=fsl, sl=sl: e.tensor_sub(xf_t[fsl][:], xf_t[fsl][:], bc_t[sl][:, 1, :]),
                              waits=[evc2], inc=esem)
                e_ap = P.emit("vector",
                              lambda e, c=c, t=t, fsl=fsl: e.tensor_scalar(
                                  xn2v(c, QT * t, QT), xf_t[fsl][:],
                                  gb_sb[:, c:c + 1], gb_sb[:, 8 + c:9 + c], ALU.mult, ALU.add),
                              waits=[esub], inc="dve")
                xf_free[fsl] = e_ap
                xn2_evs[(t, c)] = e_ap

        if dbg >= 1 and phases == 1:
            dump_free = {}
            for t in range(NT):
                for c in range(CH):
                    fsl = (t * CH + c) % 2
                    ev = P.emit("vector",
                                lambda e, c=c, t=t, fsl=fsl: e.tensor_copy(
                                    x2sm[fsl][:], xn2v(c, QT * t, QT)),
                                waits=[xn2_evs[(t, c)], dump_free.get(fsl)], inc="dve")
                    dump_free[fsl] = P.dma("sync", dbg_out["xn2"][:, c, QT * t:QT * (t + 1)],
                                           x2sm[fsl][:], waits=[ev], inc="st")

        xn2_ready = P.now("dve")

        if phases >= 2:
            s_free = [None, None]
            eb_free = [None] * 4
            wq_free = None
            proj_done = [None] * NPAIR
            qk_last = [None] * NPAIR
            av_last = [None] * NPAIR
            ctx_bank_free = None
            psd_free = None
            pstr_free = None
            ctxu_free = [None, None]
            spill_evs = {}
            norm_done = [None] * NPAIR
            jj = 0

            def emit_proj_group(p, gi):
                nonlocal psd_free, pstr_free
                sl = p % 2
                mt = gi % 3
                tt = gi // 3
                evs = []
                for c in range(CH):
                    w = [wq_ev[p], xn2_ready]
                    if c == 0:
                        w.append(psd_free)
                    ev = P.emit("tensor",
                                lambda e, c=c, mt=mt, tt=tt: e.matmul(
                                    PS_D[:, 0:512],
                                    wqkv_sb[:, 384 * c + 128 * mt: 384 * c + 128 * (mt + 1)],
                                    xn2v(c, QT * tt, QT),
                                    start=(c == 0), stop=(c == CH - 1)),
                                waits=w, inc="pe" if c == CH - 1 else None)
                    if c == CH - 1:
                        evs.append(ev)
                mm_ev = evs[-1]
                ev_guard = qk_last[p - 2] if p >= 2 and mt < 2 else None
                ev_guard_av = av_last[p - 2] if p >= 2 and mt == 2 else None
                if mt == 0:
                    evac = P.emit("vector",
                                  lambda e, sl=sl, tt=tt: e.tensor_scalar_mul(
                                      qp_sb[sl][:, QT * tt:QT * (tt + 1)], PS_D[:, 0:512], 0.125),
                                  waits=[mm_ev, ev_guard], inc="dve")
                    psd_free = evac
                elif mt == 1:
                    evac = P.emit("vector",
                                  lambda e, sl=sl, tt=tt: e.tensor_copy(
                                      kp_sb[sl][:, QT * tt:QT * (tt + 1)], PS_D[:, 0:512]),
                                  waits=[mm_ev, ev_guard], inc="dve")
                    psd_free = evac
                else:
                    evac0 = P.emit("vector",
                                   lambda e: e.tensor_copy(vfm_st[:, 0:512], PS_D[:, 0:512]),
                                   waits=[mm_ev, vfm_free[0]], inc="dve")
                    psd_free = evac0
                    trev = None
                    for b in range(4):
                        trev = P.emit("tensor",
                                      lambda e, b=b: e.transpose(
                                          PS_TR[:, 128 * b:128 * (b + 1)],
                                          vfm_st[:, 128 * b:128 * (b + 1)], ident_bf[:]),
                                      waits=[evac0] + ([pstr_free] if b == 0 else []),
                                      inc="pe" if b == 3 else None)
                    dst = bass.AP(vtm_sb[sl], 130 * (4 * tt),
                                  [[130 * NJ, 128], [130, 4], [65, 2], [1, 64]])
                    evac = P.emit("vector",
                                  lambda e, dst=dst: e.tensor_copy(dst, PS_TR[:, 0:512]),
                                  waits=[trev, ev_guard_av], inc="dve")
                    pstr_free = evac
                    vfm_free[0] = evac
                return evac

            vfm_free = [None]
            wq_ev = {}

            def emit_pair_load(p):
                w = [wq_free] if wq_free is not None else []
                wq_ev[p] = P.dma("gpsimd", wqkv_sb[:], wqkvdev[p], waits=w, inc="wq")

            emit_pair_load(0)
            proj_evs = [emit_proj_group(0, gi) for gi in range(24)]
            proj_done[0] = proj_evs[-1]
            wq_free = P.now("pe")
            srow_free = None

            for p in range(NPAIR):
                sl = p % 2
                proj_q = list(range(24)) if p + 1 < NPAIR else []
                pace_ctr = [0]
                if p + 1 < NPAIR:
                    emit_pair_load(p + 1)
                last_av_ev = None
                last_qk_ev = None
                pend_av = None
                for t in range(NT):
                    nj = 4 * t + 4
                    for j in range(nj):
                        par = jj % 2
                        eb = jj % 4
                        w = [proj_done[p], s_free[par]]
                        P.emit("tensor",
                               lambda e, sl=sl, t=t, j=j, par=par: e.matmul(
                                   PS_S[par][:, 0:512],
                                   kp_sb[sl][0:64, 128 * j:128 * (j + 1)],
                                   qp_sb[sl][0:64, QT * t:QT * (t + 1)],
                                   start=True, stop=True, tile_position=(0, 0)),
                               waits=w)
                        qk_ev = P.emit("tensor",
                                       lambda e, sl=sl, t=t, j=j, par=par: e.matmul(
                                           PS_S[par][:, 512:1024],
                                           kp_sb[sl][64:128, 128 * j:128 * (j + 1)],
                                           qp_sb[sl][64:128, QT * t:QT * (t + 1)],
                                           start=True, stop=True, tile_position=(64, 0)),
                                       inc="pe")
                        last_qk_ev = qk_ev
                        if pend_av is not None:
                            last_av_ev = pend_av()
                            pend_av = None
                        exp_ev = P.emit("scalar",
                                        lambda e, eb=eb, par=par: e.activation(
                                            exps_sb[eb][:, :], PS_S[par][:, :], AF.Exp,
                                            bias=expbias[:, 0:1]),
                                        waits=[qk_ev, eb_free[eb]], inc="act")
                        s_free[par] = exp_ev
                        ready_a = exp_ev
                        ready_b = exp_ev
                        if j >= 4 * t:
                            o = j - 4 * t
                            ready_a = P.emit("vector",
                                             lambda e, eb=eb, o=o: e.tensor_mul(
                                                 exps_sb[eb][:, 0:512], exps_sb[eb][:, 0:512],
                                                 masks_bf[:, QT * o:QT * (o + 1)]),
                                             waits=[exp_ev], inc="dve")
                            ready_b = P.emit("vector",
                                             lambda e, eb=eb, o=o: e.tensor_mul(
                                                 exps_sb[eb][:, 512:1024], exps_sb[eb][:, 512:1024],
                                                 masks_bf[:, QT * o:QT * (o + 1)]),
                                             inc="dve")

                        def mk_av(j=j, t=t, sl=sl, eb=eb, nj=nj, ready_a=ready_a, ready_b=ready_b):
                            w = [ready_a]
                            if j == 0:
                                w.append(ctx_bank_free)
                            P.emit("tensor",
                                   lambda e: e.matmul(
                                       PS_C[0:65, 0:512],
                                       bass.AP(vtm_sb[sl], 130 * j, [[130 * NJ, 128], [1, 65]]),
                                       exps_sb[eb][:, 0:512],
                                       start=(j == 0), stop=(j == nj - 1)),
                                   waits=w)
                            ev = P.emit("tensor",
                                        lambda e: e.matmul(
                                            PS_C[0:65, 512:1024],
                                            bass.AP(vtm_sb[sl], 130 * j + 65, [[130 * NJ, 128], [1, 65]]),
                                            exps_sb[eb][:, 512:1024],
                                            start=(j == 0), stop=(j == nj - 1)),
                                        waits=[ready_b], inc="pe")
                            eb_free[eb] = ev
                            return ev
                        pend_av = mk_av
                        jj += 1
                        if p + 1 < NPAIR and proj_q:
                            pace_ctr[0] += 1
                            if pace_ctr[0] >= 6:
                                pace_ctr[0] = 0
                                gi = proj_q.pop(0)
                                ev = emit_proj_group(p + 1, gi)
                                if gi == 23:
                                    proj_done[p + 1] = ev
                    last_av_ev = pend_av()
                    pend_av = None
                    ca = P.emit("vector",
                                lambda e, sl=sl, t=t: e.tensor_copy(
                                    ctxu_sb[sl][0:64, QT * t:QT * (t + 1)], PS_C[0:64, 0:512]),
                                waits=[last_av_ev, ctxu_free[sl]], inc="dve")
                    cb = P.emit("vector",
                                lambda e, sl=sl, t=t: e.tensor_copy(
                                    ctxu_sb[sl][64:128, QT * t:QT * (t + 1)], PS_C[0:64, 512:1024]),
                                inc="dve")

                    sa = P.emit("vector",
                                lambda e, t=t: e.tensor_copy(
                                    srowA[0:1, QT * t:QT * (t + 1)], PS_C[64:65, 0:512]),
                                waits=[srow_free], inc="dve")
                    sbv = P.emit("vector",
                                 lambda e, t=t: e.tensor_copy(
                                     srowB[0:1, QT * t:QT * (t + 1)], PS_C[64:65, 512:1024]),
                                 inc="dve")
                    ctx_bank_free = sbv
                while proj_q:
                    gi = proj_q.pop(0)
                    ev = emit_proj_group(p + 1, gi)
                    if gi == 23:
                        proj_done[p + 1] = ev
                qk_last[p] = last_qk_ev
                av_last[p] = last_av_ev
                wq_free = P.now("pe")
                r2 = P.now("dve")
                gv = None
                for rr in range(16):
                    t2, h = rr // 2, rr % 2
                    srw = srowA if h == 0 else srowB
                    gv = P.emit("tensor",
                                lambda e, rr=rr, t2=t2, srw=srw: e.matmul(
                                    PS_D[0:16, 0:512], eye16_sb[0:1, 16 * rr:16 * (rr + 1)],
                                    srw[0:1, QT * t2:QT * (t2 + 1)],
                                    start=(rr == 0), stop=(rr == 15)),
                                waits=[r2, consts_ev, psd_free] if rr == 0 else [],
                                inc="pe" if rr == 15 else None)
                srow_free = gv

                def _recip16(e):
                    with nc.allow_low_precision(reason="bf16 softmax denominators"):
                        return e.reciprocal(recip16[:], PS_D[0:16, 0:512])
                r16 = P.emit("vector", _recip16, waits=[gv], inc="dve")
                psd_free = r16
                nb_ev = None
                psc_norm_free = None
                for t2 in range(NT):
                    if t2 % 2 == 0:
                        dst_ps = PS_D[:, 0:512]
                        wbc = [r16, psd_free]
                    else:
                        dst_ps = PS_C[:, 0:512]
                        wbc = [r16, psc_norm_free]
                    bm = P.emit("tensor",
                                lambda e, t2=t2, dst_ps=dst_ps: e.matmul(
                                    dst_ps, sel16_sb[:, 128 * t2:128 * (t2 + 1)],
                                    recip16[:], start=True, stop=True),
                                waits=wbc, inc="pe")
                    nb_ev = P.emit("vector",
                                   lambda e, sl=sl, t2=t2, dst_ps=dst_ps: e.tensor_mul(
                                       ctxu_sb[sl][:, QT * t2:QT * (t2 + 1)],
                                       ctxu_sb[sl][:, QT * t2:QT * (t2 + 1)], dst_ps),
                                   waits=[bm], inc="dve")
                    if t2 % 2 == 0:
                        psd_free = nb_ev
                    else:
                        psc_norm_free = nb_ev
                ctx_bank_free = psc_norm_free
                norm_done[p] = nb_ev
                ctxu_free[sl] = P.dma("sync", ctxspill[p], ctxu_sb[sl][:],
                                      waits=[nb_ev], inc=f"stc{sl}")
                spill_evs[p] = ctxu_free[sl]
                if dbg >= 2:
                    P.dma("gpsimd", dbg_out["ctx"][p], ctxspill[p],
                          waits=[ctxu_free[sl]], inc="ldg")
                    P.dma("gpsimd", dbg_out["sums"][p:p + 1, 0:T], srowA, waits=[r2], inc="ldg")
                    P.dma("gpsimd", dbg_out["sums"][p:p + 1, T:2 * T], srowB, waits=[r2], inc="ldg")

        if phases >= 3:
            bar = [P.now("pe"), P.now("act"), P.now("dve"), P.now("gps"),
                   P.now("st"), P.now("ld"), P.now("ldg")]
            wo_dst = bass.AP(xn2, 0, [[CH * T, 128], [1024, CH], [1, 1024]])
            ev_wo = P.dma("gpsimd", wo_dst, wodev, waits=bar, inc="wld")
            wk_dst = bass.AP(xn2, 8192, [[CH * T, 128], [1024, CH], [1, 1024]])
            wv_dst = bass.AP(xn2, 16384, [[CH * T, 128], [1024, CH], [1, 1024]])
            ev_wk = P.dma("gpsimd", wk_dst, wkdev, waits=bar, inc="wld")
            ev_wv = P.dma("gpsimd", wv_dst, wvdev, waits=bar, inc="wld")
            ev_wo = ev_wk = ev_wv
            wo_v = lambda c, m: xn2[:, 1024 * c + 128 * m: 1024 * c + 128 * (m + 1)]

            site_ps = [PS_C, PS_S[1]]
            site_free = [bar[2], bar[1]]
            site_idx = [0]
            rows_free = [bar[0]]

            def acquire_site():
                par = site_idx[0] % 2
                site_idx[0] += 1
                return site_ps[par], par

            def rows_and_bcast(ps, par, sl, stats_pe):
                a = P.emit("vector", lambda e, ps=ps: e.tensor_scalar_mul(
                               r_a[0:1, :], ps[0:1, 0:512], 1.0 / D),
                           waits=[stats_pe, rows_free[0]], inc="dve")
                m2 = P.emit("vector", lambda e: e.tensor_mul(r_c[0:1, :], r_a[0:1, :], r_a[0:1, :]),
                            waits=[a], inc="dve")
                v = P.emit("vector", lambda e, ps=ps: e.scalar_tensor_tensor(
                               r_b[0:1, :], ps[0:1, 512:1024], 1.0 / D, r_c[0:1, :],
                               ALU.mult, ALU.subtract),
                           waits=[m2], inc="dve")
                ln = P.emit("scalar", lambda e: e.activation(
                                r_b[0:1, :], r_b[0:1, :], AF.Ln,
                                bias=epscol[0:1, 0:1]),
                            waits=[v], inc="act")
                ex = P.emit("scalar", lambda e: e.activation(r_c[0:1, :], r_b[0:1, :], AF.Exp,
                                                             scale=neghalf[0:1, 0:1]),
                            waits=[ln], inc="act")
                mr = P.emit("vector", lambda e: e.tensor_mul(r_a[0:1, :], r_a[0:1, :], r_c[0:1, :]),
                            waits=[ex], inc="dve")
                evb1 = P.emit("tensor", lambda e, ps=ps: e.matmul(
                                  ps[:, 0:512], onesf_row[0:1, :], r_c[0:1, :],
                                  start=True, stop=True),
                              waits=[ex, mr], inc="pe")
                evb2 = P.emit("tensor", lambda e, ps=ps: e.matmul(
                                  ps[:, 512:1024], onesf_row[0:1, :], r_a[0:1, :],
                                  start=True, stop=True),
                              waits=[mr], inc="pe")
                rows_free[0] = evb2
                evc1 = P.emit("vector", lambda e, sl=sl, ps=ps: e.tensor_copy(
                                  bc_t[sl][:, 0, :], ps[:, 0:512]),
                              waits=[evb1, P.now("gps")], inc="dve")
                evc2 = P.emit("vector", lambda e, sl=sl, ps=ps: e.tensor_copy(
                                  bc_t[sl][:, 1, :], ps[:, 512:1024]),
                              waits=[evb2], inc="dve")
                site_free[par] = evc2
                return evc1, evc2

            mm_reg2 = [(PS_S[0], 0), (PS_S[0], 512)]
            reg_free = [bar[1], bar[1]]
            au_free = [[], []]
            cx_free = [None, None]
            x2_evs = {}
            xf_free2 = list(xf_free)
            x2sm_free = [None] * 4
            wo_mm_last = None

            for t in range(NT):
                sl = t % 2
                cx = ctxu_sb[sl]
                ev_cx = []
                for c in range(CH):
                    ev_cx.append(P.dma("sync", cx[:, QT * c:QT * (c + 1)],
                                       ctxspill[c, :, QT * t:QT * (t + 1)],
                                       waits=[spill_evs[6], spill_evs[7]]
                                       + ([cx_free[sl]] if c == 0 else []),
                                       inc=f"cx{sl}"))
                ev_cx = [ev_cx[-1]] * CH
                au = xbf_t[sl]
                evac_evs = {}
                cs_evs = {}
                ps, par = acquire_site()
                for m in range(CH):
                    reg, off = mm_reg2[m % 2]
                    for c in range(CH):
                        w = [ev_wo, ev_cx[c]]
                        if c == 0:
                            w.append(reg_free[m % 2])
                        ev = P.emit("tensor",
                                    lambda e, c=c, m=m, reg=reg, off=off, cx=cx: e.matmul(
                                        reg[:, off:off + 512], wo_v(c, m),
                                        cx[:, QT * c:QT * (c + 1)],
                                        start=(c == 0), stop=(c == CH - 1)),
                                    waits=w, inc="pe" if c == CH - 1 else None)
                    wo_mm_last = ev
                    ae = P.emit("vector",
                                lambda e, m=m, reg=reg, off=off, au=au: e.tensor_copy(
                                    au[:, m, :], reg[:, off:off + 512]),
                                waits=[ev] + (au_free[sl] if m == 0 else []), inc="dve")
                    reg_free[m % 2] = ae
                    evac_evs[m] = ae
                    sqeng, sqsem = ("vector", "dve")
                    sq = P.emit(sqeng,
                                lambda e, m=m, au=au: e.tensor_mul(
                                    xsq_t[:, m % 2, :], au[:, m, :], au[:, m, :]),
                                waits=[ae, cs_evs.get(m - 2)], inc=sqsem)
                    P.emit("tensor",
                           lambda e, m=m, au=au, ps=ps: e.matmul(
                               ps[0:1, 0:512], ones_col[:], au[:, m, :],
                               start=(m == 0), stop=(m == CH - 1)),
                           waits=[ae] + ([site_free[par]] if m == 0 else []))
                    cs_evs[m] = P.emit("tensor",
                                       lambda e, m=m, ps=ps: e.matmul(
                                           ps[0:1, 512:1024], ones_col[:], xsq_t[:, m % 2, :],
                                           start=(m == 0), stop=(m == CH - 1)),
                                       waits=[sq], inc="pe")
                evc1, evc2 = rows_and_bcast(ps, par, sl, cs_evs[CH - 1])
                if dbg >= 5 and phases == 3:
                    au_dump = P.dma("gpsimd", dbg_out["au"][t], au[:],
                                    waits=[evac_evs[CH - 1]], inc="ldg")
                    au_free[sl] = au_free[sl] + [au_dump]
                xf_evs = {}
                for c in range(4):
                    fsl = c % 4
                    xf_evs[c] = P.dma("sync", xf_t[fsl][:], xdev[:, c, QT * t:QT * (t + 1)],
                                      waits=[xf_free2[fsl]], inc=f"xf{fsl}")
                for c in range(CH):
                    fsl = (t * CH + c) % 4
                    eng, esem = ("vector", "dve")
                    ev_xf = xf_evs[c]
                    s1 = P.emit(eng,
                                lambda e, c=c, fsl=fsl, sl=sl, au=au: e.tensor_mul(
                                    x2sm[fsl][:], au[:, c, :], bc_t[sl][:, 0, :]),
                                waits=[evc1, evac_evs[c], x2sm_free[fsl]], inc=esem)
                    s2 = P.emit(eng,
                                lambda e, fsl=fsl, sl=sl: e.tensor_sub(
                                    x2sm[fsl][:], x2sm[fsl][:], bc_t[sl][:, 1, :]),
                                waits=[evc2], inc=esem)
                    s3 = P.emit(eng,
                                lambda e, fsl=fsl: e.tensor_add(
                                    x2sm[fsl][:], x2sm[fsl][:], xf_t[fsl][:]),
                                waits=[ev_xf], inc=esem)
                    xf_free2[fsl] = s3
                    if c + 4 < CH:
                        xf_evs[c + 4] = P.dma("sync", xf_t[fsl][:],
                                              xdev[:, c + 4, QT * t:QT * (t + 1)],
                                              waits=[s3], inc=f"xf{fsl}")
                    ev_st = P.dma("sync", x2spill[:, c, QT * t:QT * (t + 1)], x2sm[fsl][:],
                                  waits=[s3], inc=f"stx{fsl}")
                    x2sm_free[fsl] = ev_st
                    x2_evs[(t, c)] = ev_st
                    if c >= CH - 2:
                        au_free[sl] = (au_free[sl] + [s1])[-2:] if c == CH - 1 else [s1]
                    if dbg >= 3 and phases == 3:
                        P.dma("gpsimd", dbg_out["x2"][:, c, QT * t:QT * (t + 1)],
                              x2spill[:, c, QT * t:QT * (t + 1)], waits=[ev_st], inc="ldg")
                cx_free[sl] = P.now("pe")

        if phases >= 4:
            wr_dst = bass.AP(xn2, 0, [[CH * T, 128], [1024, CH], [1, 1024]])
            ev_wr = P.dma("gpsimd", wr_dst, wrdev, waits=[wo_mm_last], inc="wrd")
            wr_v = lambda c, m: xn2[:, 1024 * c + 128 * m: 1024 * c + 128 * (m + 1)]
            wk_v = lambda c, m: xn2[:, 8192 + 1024 * c + 128 * m: 8192 + 1024 * c + 128 * (m + 1)]
            wv_v = lambda c, m: xn2[:, 16384 + 1024 * c + 128 * m: 16384 + 1024 * c + 128 * (m + 1)]
            Z0 = 24576
            zv = lambda c, i0, n: xn2[:, Z0 + 513 * c + i0: Z0 + 513 * c + i0 + n]
            z_head = bass.AP(xn2, Z0, [[CH * T, 128], [513, CH], [1, 1]])
            z_tail = bass.AP(xn2, Z0 + 512, [[CH * T, 128], [513, CH], [1, 1]])
            xr_v = lambda c: qp_sb[0][:, QT * c:QT * (c + 1)]
            xk_v = lambda c: qp_sb[1][:, QT * c:QT * (c + 1)]
            r_v = lambda m: kp_sb[0][:, QT * m:QT * (m + 1)]
            ka_v = lambda m: kp_sb[1][:, QT * m:QT * (m + 1)]
            g_v = lambda m: vtm_sb[0][:, QT * m:QT * (m + 1)]

            stx_finals = [P.now(f"stx{i}") for i in range(4)]
            mm_reg3 = [(PS_S[0], 0), (PS_S[0], 512), (PS_D, 0)]
            reg_free3 = [P.now("act"), P.now("act"), P.now("dve")]
            x2b_free = list(cx_free)
            xq_free = list(xf_free2)
            xo_free = [None] * 3
            zcopy_prev = None
            xr_done_prev = []
            g_free = []
            prev_cs4 = {}
            rk_reads_z = None

            for t in range(NT):
                sl = t % 2
                x2b = bass.AP(ctxu_sb[sl], 0, [[T, 128], [512, CH], [1, 512]])
                ev_x2b = P.dma("gpsimd", x2b, x2spill[:, :, QT * t:QT * (t + 1)],
                               waits=[x2b_free[sl]] + stx_finals,
                               inc=f"x2b{sl}")
                cs3 = {}
                ps, par = acquire_site()
                for c in range(CH):
                    sqeng, sqsem = ("vector", "dve")
                    w_guard = cs3.get(c - 2) if c >= 2 else prev_cs4.get(CH - 2 + c)
                    sq = P.emit(sqeng,
                                lambda e, c=c, sl=sl: e.tensor_mul(
                                    xsq_t[:, c % 2, :],
                                    bass.AP(ctxu_sb[sl], 512 * c, [[T, 128], [1, 512]]),
                                    bass.AP(ctxu_sb[sl], 512 * c, [[T, 128], [1, 512]])),
                                waits=[ev_x2b, w_guard], inc=sqsem)
                    P.emit("tensor",
                           lambda e, c=c, sl=sl, ps=ps: e.matmul(
                               ps[0:1, 0:512], ones_col[:],
                               bass.AP(ctxu_sb[sl], 512 * c, [[T, 128], [1, 512]]),
                               start=(c == 0), stop=(c == CH - 1)),
                           waits=[ev_x2b] + ([site_free[par]] if c == 0 else []))
                    cs3[c] = P.emit("tensor",
                                    lambda e, c=c, ps=ps: e.matmul(
                                        ps[0:1, 512:1024], ones_col[:], xsq_t[:, c % 2, :],
                                        start=(c == 0), stop=(c == CH - 1)),
                                    waits=[sq], inc="pe")
                evc1, evc2 = rows_and_bcast(ps, par, sl, cs3[CH - 1])
                if t == 0:
                    zb = P.emit("vector", lambda e: e.memset(z_head, 0.0),
                                waits=[ev_wv], inc="dve")
                else:
                    zb = P.emit("vector", lambda e: e.tensor_copy(z_head, z_tail),
                                waits=xr_done_prev, inc="dve")
                xr_evs = {}
                xq_evs = {}
                tcol_free = {}
                for c in range(4):
                    fsl = c % 4
                    xq_evs[c] = P.dma("sync", xf_t[fsl][:], x2spill[:, c, QT * t:QT * (t + 1)],
                                      waits=[xq_free[fsl]] + stx_finals, inc=f"xf{fsl}")
                for c in range(CH):
                    fsl = c % 4
                    eng, esem = ("vector", "dve")
                    ev_xq = xq_evs[c]
                    z1 = P.emit(eng,
                                lambda e, fsl=fsl, sl=sl: e.tensor_mul(
                                    xf_t[fsl][:], xf_t[fsl][:], bc_t[sl][:, 0, :]),
                                waits=[ev_xq, evc1, rk_reads_z], inc=esem)
                    z2 = P.emit(eng,
                                lambda e, c=c, fsl=fsl, sl=sl: e.tensor_sub(
                                    zv(c, 1, QT), xf_t[fsl][:], bc_t[sl][:, 1, :]),
                                waits=[evc2, zb], inc=esem)
                    xq_free[fsl] = z2
                    if c + 4 < CH:
                        xq_evs[c + 4] = P.dma("sync", xf_t[fsl][:],
                                              x2spill[:, c + 4, QT * t:QT * (t + 1)],
                                              waits=[z2], inc=f"xf{fsl}")
                    tcol = vtm_sb[1][:, 0:512] if c % 2 == 0 else vtm_sb[1][:, 512:1024]
                    d1 = P.emit(eng,
                                lambda e, c=c, tcol=tcol: e.tensor_sub(
                                    tcol, zv(c, 1, QT), zv(c, 0, QT)),
                                waits=[z2, zcopy_prev, tcol_free.get(c % 2)], inc=esem)
                    x1 = P.emit("vector",
                                lambda e, c=c, tcol=tcol: e.scalar_tensor_tensor(
                                    xr_v(c), tcol, mu_sb[:, c:c + 1], zv(c, 0, QT),
                                    ALU.mult, ALU.add),
                                waits=[d1], inc="dve")
                    x2e = P.emit("vector",
                                 lambda e, c=c, tcol=tcol: e.scalar_tensor_tensor(
                                     xk_v(c), tcol, mu_sb[:, 8 + c:9 + c], zv(c, 0, QT),
                                     ALU.mult, ALU.add),
                                 waits=[x1], inc="dve")
                    xr_evs[c] = x2e
                    tcol_free[c % 2] = x2e
                xr_done_prev = [P.now("dve"), P.now("gps")]
                sig_evs = {}
                for m in range(CH):
                    reg, off = mm_reg3[m % 3]
                    for c in range(CH):
                        w = [ev_wr, xr_evs[c]]
                        if c == 0:
                            w.append(reg_free3[m % 3])
                        ev = P.emit("tensor",
                                    lambda e, c=c, m=m, reg=reg, off=off: e.matmul(
                                        reg[:, off:off + 512], wr_v(c, m), xr_v(c),
                                        start=(c == 0), stop=(c == CH - 1)),
                                    waits=w, inc="pe" if c == CH - 1 else None)
                    sig_evs[m] = P.emit("scalar",
                                        lambda e, m=m, reg=reg, off=off: e.activation(
                                            r_v(m), reg[:, off:off + 512], AF.Sigmoid),
                                        waits=[ev] + (g_free if m == 0 else []), inc="act")
                    reg_free3[m % 3] = sig_evs[m]
                ka_evs = {}
                for m in range(CH):
                    reg, off = mm_reg3[m % 3]
                    for c in range(CH):
                        w = [ev_wk, xr_evs[c]]
                        if c == 0:
                            w.append(reg_free3[m % 3])
                        ev = P.emit("tensor",
                                    lambda e, c=c, m=m, reg=reg, off=off: e.matmul(
                                        reg[:, off:off + 512], wk_v(c, m), xk_v(c),
                                        start=(c == 0), stop=(c == CH - 1)),
                                    waits=w, inc="pe" if c == CH - 1 else None)
                    rl = P.emit("scalar",
                                lambda e, m=m, reg=reg, off=off: e.activation(
                                    ka_v(m), reg[:, off:off + 512], AF.Relu),
                                waits=[ev], inc="act")
                    reg_free3[m % 3] = rl
                    ka_evs[m] = P.emit("vector",
                                       lambda e, m=m: e.tensor_mul(ka_v(m), ka_v(m), ka_v(m)),
                                       waits=[rl], inc="dve")
                gs_evs = {}
                cs4 = {}
                ps2, par2 = acquire_site()
                gfree = g_free
                for m in range(CH):
                    reg, off = mm_reg3[m % 3]
                    for c in range(CH):
                        w = [ev_wv, ka_evs[c]]
                        if c == 0:
                            w.append(reg_free3[m % 3])
                        ev = P.emit("tensor",
                                    lambda e, c=c, m=m, reg=reg, off=off: e.matmul(
                                        reg[:, off:off + 512], wv_v(c, m), ka_v(c),
                                        start=(c == 0), stop=(c == CH - 1)),
                                    waits=w, inc="pe" if c == CH - 1 else None)
                    ge = P.emit("vector",
                                lambda e, m=m, reg=reg, off=off: e.tensor_mul(
                                    g_v(m), r_v(m), reg[:, off:off + 512]),
                                waits=[ev, sig_evs[m]] + (gfree if m == 0 else []), inc="dve")
                    reg_free3[m % 3] = ge
                    gs_evs[m] = ge
                    sqeng, sqsem = ("vector", "dve")
                    sqg = P.emit(sqeng,
                                 lambda e, m=m: e.tensor_mul(
                                     xsq_t[:, m % 2, :], g_v(m), g_v(m)),
                                 waits=[ge, cs4.get(m - 2)], inc=sqsem)
                    P.emit("tensor",
                           lambda e, m=m, ps2=ps2: e.matmul(
                               ps2[0:1, 0:512], ones_col[:], g_v(m),
                               start=(m == 0), stop=(m == CH - 1)),
                           waits=[ge] + ([site_free[par2]] if m == 0 else []))
                    cs4[m] = P.emit("tensor",
                                    lambda e, m=m, ps2=ps2: e.matmul(
                                        ps2[0:1, 512:1024], ones_col[:], xsq_t[:, m % 2, :],
                                        start=(m == 0), stop=(m == CH - 1)),
                                    waits=[sqg], inc="pe")
                evg1, evg2 = rows_and_bcast(ps2, par2, sl, cs4[CH - 1])
                xo_evs = {}
                for c in range(3):
                    osl = c % 3
                    xo_evs[c] = P.dma("sync", osm[osl][:], x2spill[:, c, QT * t:QT * (t + 1)],
                                      waits=[xo_free[osl]], inc=f"osm{osl}")
                for c in range(CH):
                    fsl = c % 4
                    osl = c % 3
                    eng, esem = ("vector", "dve")
                    o1 = P.emit(eng,
                                lambda e, c=c, fsl=fsl, sl=sl: e.tensor_mul(
                                    x2sm[fsl][:], g_v(c), bc_t[sl][:, 0, :]),
                                waits=[evg1, gs_evs[c], x2sm_free[fsl]], inc=esem)
                    o2 = P.emit(eng,
                                lambda e, fsl=fsl, sl=sl: e.tensor_sub(
                                    x2sm[fsl][:], x2sm[fsl][:], bc_t[sl][:, 1, :]),
                                waits=[evg2], inc=esem)
                    o3 = P.emit(eng,
                                lambda e, fsl=fsl, osl=osl: e.tensor_add(
                                    x2sm[fsl][:], x2sm[fsl][:], osm[osl][:]),
                                waits=[xo_evs[c]], inc=esem)
                    xo_free[osl] = o3
                    if c + 3 < CH:
                        xo_evs[c + 3] = P.dma("sync", osm[osl][:],
                                              x2spill[:, c + 3, QT * t:QT * (t + 1)],
                                              waits=[o3], inc=f"osm{osl}")
                    ev_st = P.dma("sync", outdev[:, c, QT * t:QT * (t + 1)], x2sm[fsl][:],
                                  waits=[o3], inc=f"stx{fsl}")
                    x2sm_free[fsl] = ev_st
                x2b_free[sl] = P.now("pe")
                rk_reads_z = P.now("pe")
                zcopy_prev = zb
                g_free = [P.now("dve"), P.now("gps")]
                prev_cs4 = cs4

        P.emit("sync", lambda e: e.nop(),
               waits=[P.now("st"), P.now("dve"), P.now("act"), P.now("pe"),
                      P.now("gps"), P.now("ld"), P.now("ldg"),
                      P.now("stx0"), P.now("stx1"), P.now("stx2"), P.now("stx3"),
                      P.now("stc0"), P.now("stc1")])
        P.replay()
    return nc, dbg_out


from concourse.bass_utils import run_bass_kernel_spmd

_NC_CACHE = {}


def kernel(**inputs):
    if "nc" not in _NC_CACHE:
        _NC_CACHE["nc"] = build(dbg=0, phases=4)[0]
    nc = _NC_CACHE["nc"]
    in_maps = [host_inputs(inputs, core) for core in range(8)]
    res = run_bass_kernel_spmd(nc, in_maps, core_ids=list(range(8)))
    outs = [host_output(r) for r in res.results]
    return np.stack(outs, axis=0).astype(np.float32)


# version 12
# speedup vs baseline: 1.2461x; 1.0158x over previous
import numpy as np
import ml_dtypes
import concourse.bass as bass
import concourse.mybir as mybir

F32 = mybir.dt.float32
BF16 = mybir.dt.bfloat16
AF = mybir.ActivationFunctionType
ALU = mybir.AluOpType

T = 4096
D = 1024
CH = 8
NT = 8
QT = 512
NJ = 32
NPAIR = 8
DH = 64
EXP_SHIFT = -12.0

ENGINES = ("tensor", "scalar", "vector", "gpsimd", "sync")


class MK:

    def __init__(self, nc):
        self.nc = nc
        self.ops = {e: [] for e in ENGINES}
        self.sems = {}
        self.cnt = {}
        self.last_wait = {e: {} for e in ENGINES}

    def sem(self, name):
        self.cnt[name] = 0
        return name

    def emit(self, eng, fn, waits=(), inc=None, n=1):
        ws = []
        for wv in waits:
            if wv is None:
                continue
            s, v = wv
            lw = self.last_wait[eng].get(s, -1)
            if v > lw:
                ws.append((s, v))
                self.last_wait[eng][s] = v
        ev = None
        if inc is not None:
            self.cnt[inc] += n
            ev = (inc, self.cnt[inc])
        self.ops[eng].append((ws, fn, (inc, n) if inc else None))
        return ev

    def dma(self, eng, out, in_, waits=(), inc="ld"):
        return self.emit(eng, lambda e: e.dma_start(out=out, in_=in_),
                         waits=waits, inc=inc, n=16)

    def now(self, sem):
        return (sem, self.cnt[sem])

    def replay(self):
        nc = self.nc
        from contextlib import ExitStack
        with ExitStack() as st:
            for name in self.cnt:
                self.sems[name] = st.enter_context(nc.semaphore(name))
            with nc.Block() as block:
                for eng_name in ENGINES:
                    deco = getattr(block, eng_name)

                    def body(eng, _ops=self.ops[eng_name]):
                        for ws, fn, inc in _ops:
                            for s, v in ws:
                                eng.wait_ge(self.sems[s], v)
                            instr = fn(eng)
                            if inc is not None:
                                instr.then_inc(self.sems[inc[0]], inc[1])
                    deco(body)


def host_inputs(inputs, core):
    x = np.asarray(inputs["L0_attn_x"])
    Wqkv = np.asarray(inputs["L0_attn_W_qkv"])
    Wo = np.asarray(inputs["L0_attn_W_o"])
    Wr = np.asarray(inputs["L0_ffn_W_r"])
    Wk = np.asarray(inputs["L0_ffn_W_k"])
    Wv = np.asarray(inputs["L0_ffn_W_v"])
    g = np.asarray(inputs["L0_attn_ln_gamma"])
    b = np.asarray(inputs["L0_attn_ln_beta"])
    mur = np.asarray(inputs["L0_ffn_mu_r"])
    muk = np.asarray(inputs["L0_ffn_mu_k"])

    d = {}
    xT = np.ascontiguousarray(x[core].T)
    d["xdev"] = np.ascontiguousarray(xT.reshape(CH, 128, T).transpose(1, 0, 2))
    rows = []
    for pr in range(NPAIR):
        h0, h1 = 2 * pr, 2 * pr + 1
        for sec in range(3):
            rows.append(Wqkv[192 * h0 + 64 * sec: 192 * h0 + 64 * sec + 64])
            rows.append(Wqkv[192 * h1 + 64 * sec: 192 * h1 + 64 * sec + 64])
    Wq_re = np.concatenate(rows, axis=0)
    WqT = Wq_re.T
    wq = WqT.reshape(CH, 128, NPAIR, 384).transpose(2, 1, 0, 3)
    d["wqkvdev"] = np.ascontiguousarray(wq)
    for name, W in (("wodev", Wo), ("wrdev", Wr), ("wkdev", Wk), ("wvdev", Wv)):
        WT = W.T
        d[name] = np.ascontiguousarray(WT.reshape(CH, 128, D).transpose(1, 0, 2))
    d["gbdev"] = np.ascontiguousarray(
        np.concatenate([g.reshape(CH, 128).T, b.reshape(CH, 128).T], axis=1))
    d["mudev"] = np.ascontiguousarray(
        np.concatenate([mur.reshape(CH, 128).T, muk.reshape(CH, 128).T], axis=1))
    d["identdev"] = np.eye(128, dtype=np.float32)
    d["eye16dev"] = np.eye(16, dtype=np.float32).reshape(1, 256)
    sel16 = np.zeros((16, 8 * 128), dtype=np.float32)
    for t in range(8):
        sel16[2 * t, 128 * t:128 * t + 64] = 1.0
        sel16[2 * t + 1, 128 * t + 64:128 * t + 128] = 1.0
    d["sel16dev"] = sel16
    masks = np.zeros((128, 4 * QT), dtype=np.float32)
    qq = np.arange(QT)
    kk = np.arange(128)
    for k in range(4):
        masks[:, k * QT:(k + 1) * QT] = (qq[None, :] - 128 * k >= kk[:, None])
    d["maskdev"] = masks
    return d


def host_output(r):
    o = r["outdev"]
    return np.ascontiguousarray(o.transpose(1, 0, 2).reshape(D, T).T)


def build(dbg=0, phases=4):
    nc = bass.Bass("TRN2", target_bir_lowering=False)
    P = MK(nc)
    for s in ("ld", "ldg", "st", "pe", "act", "dve", "gps", "cst", "wq",
              "wld", "wrd", "xbf0", "xbf1", "cx0", "cx1", "x2b0", "x2b1",
              "xf0", "xf1", "xf2", "xf3", "osm0", "osm1", "osm2",
              "stx0", "stx1", "stx2", "stx3", "stc0", "stc1"):
        P.sem(s)

    xdev = nc.dram_tensor("xdev", [128, CH, T], F32, kind="ExternalInput").ap()
    wqkvdev = nc.dram_tensor("wqkvdev", [NPAIR, 128, CH, 384], F32, kind="ExternalInput").ap()
    wodev = nc.dram_tensor("wodev", [128, CH, D], F32, kind="ExternalInput").ap()
    wrdev = nc.dram_tensor("wrdev", [128, CH, D], F32, kind="ExternalInput").ap()
    wkdev = nc.dram_tensor("wkdev", [128, CH, D], F32, kind="ExternalInput").ap()
    wvdev = nc.dram_tensor("wvdev", [128, CH, D], F32, kind="ExternalInput").ap()
    gbdev = nc.dram_tensor("gbdev", [128, 16], F32, kind="ExternalInput").ap()
    mudev = nc.dram_tensor("mudev", [128, 16], F32, kind="ExternalInput").ap()
    identdev = nc.dram_tensor("identdev", [128, 128], F32, kind="ExternalInput").ap()
    eye16dev = nc.dram_tensor("eye16dev", [1, 256], F32, kind="ExternalInput").ap()
    sel16dev = nc.dram_tensor("sel16dev", [16, 1024], F32, kind="ExternalInput").ap()
    maskdev = nc.dram_tensor("maskdev", [128, 4 * QT], F32, kind="ExternalInput").ap()
    outdev = nc.dram_tensor("outdev", [128, CH, T], F32, kind="ExternalOutput").ap()
    ctxspill = nc.dram_tensor("ctxspill", [NPAIR, 128, T], BF16, kind="Internal").ap()
    x2spill = nc.dram_tensor("x2spill", [128, CH, T], F32, kind="Internal").ap()
    dbg_out = {}
    if dbg >= 1 and phases == 1:
        dbg_out["xn2"] = nc.dram_tensor("dbg_xn2", [128, CH, T], F32, kind="ExternalOutput").ap()
    if dbg >= 2:
        dbg_out["ctx"] = nc.dram_tensor("dbg_ctx", [NPAIR, 128, T], F32, kind="ExternalOutput").ap()
        dbg_out["sums"] = nc.dram_tensor("dbg_sums", [NPAIR, 2 * T], F32, kind="ExternalOutput").ap()
    if dbg >= 3 and phases == 3:
        dbg_out["x2"] = nc.dram_tensor("dbg_x2", [128, CH, T], F32, kind="ExternalOutput").ap()
    if dbg >= 5 and phases == 3:
        dbg_out["au"] = nc.dram_tensor("dbg_au", [NT, 128, CH, QT], F32, kind="ExternalOutput").ap()

    from contextlib import ExitStack
    with ExitStack() as stack:
        sb = lambda name, shape, dt: stack.enter_context(nc.sbuf_tensor(name, shape, dt))
        psal = lambda name, shape, dt: stack.enter_context(nc.psum_tensor(name, shape, dt))

        gb_sb = sb("gb_sb", [128, 16], F32)
        mu_sb = sb("mu_sb", [128, 32], F32)
        ident_bf = sb("ident_bf", [128, 128], BF16)
        masks_bf = sb("masks_bf", [128, 4 * QT], BF16)
        ones_col = sb("ones_col", [128, 1], BF16)
        onesf_row = sb("onesf_row", [1, 128], F32)
        expbias = sb("expbias", [128, 1], F32)
        neghalf = sb("neghalf", [128, 1], F32)
        epscol = sb("epscol", [128, 1], F32)

        xn2 = sb("xn2", [128, CH * T], BF16)
        r_a = sb("r_a", [1, QT], F32)
        r_b = sb("r_b", [1, QT], F32)
        r_c = sb("r_c", [1, QT], F32)
        recip16 = sb("recip16", [16, QT], BF16)
        sel16_sb = sb("sel16_sb", [16, 1024], BF16)
        eye16_sb = sb("eye16_sb", [1, 256], BF16)

        xbf_t = [sb(f"xbf{i}", [128, CH, QT], BF16) for i in range(2)]
        srowA = bass.AP(xbf_t[0], 0, [[CH * QT, 1], [1, CH * QT]])
        srowB = bass.AP(xbf_t[1], 0, [[CH * QT, 1], [1, CH * QT]])
        xsq_t = sb("xsq", [128, 2, QT], BF16)
        xf_t = [sb(f"xf{i}", [128, QT], F32) for i in range(4)]
        bc_t = [sb(f"bc{i}", [128, 2, QT], F32) for i in range(2)]

        qp_sb = [sb(f"qp{i}", [128, T], BF16) for i in range(2)]
        kp_sb = [sb(f"kp{i}", [128, T], BF16) for i in range(2)]
        vtm_sb = [sb(f"vtm{i}", [128, NJ * 130], BF16) for i in range(2)]
        vfm_st = sb("vfm_st", [128, QT], BF16)
        wqkv_sb = sb("wqkv_sb", [128, 3072], BF16)
        exps_sb = [sb(f"exps{i}", [128, 1024], BF16) for i in range(4)]
        ctxu_sb = [sb(f"ctxu{i}", [128, T], BF16) for i in range(2)]
        x2sm = [sb(f"x2sm{i}", [128, QT], F32) for i in range(4)]
        osm = [sb(f"osm{i}", [128, QT], F32) for i in range(3)]

        wo_v = lambda c, m: xn2[:, 1024 * c + 128 * m: 1024 * c + 128 * (m + 1)]

        PS_S = [psal("ps_s0", [128, 1024], F32), psal("ps_s1", [128, 1024], F32)]
        PS_C = psal("ps_c", [128, 1024], F32)
        PS_D = psal("ps_d", [128, 512], F32)
        PS_TR = psal("ps_tr", [128, 1024], BF16)

        P.dma("sync", gb_sb[:], gbdev, inc="cst")
        P.dma("sync", mu_sb[:, 0:16], mudev, inc="cst")
        P.dma("gpsimd", ident_bf[:], identdev, inc="cst")
        P.dma("gpsimd", masks_bf[:], maskdev, inc="cst")
        P.dma("gpsimd", eye16_sb[:], eye16dev, inc="cst")
        ev_cst = P.dma("gpsimd", sel16_sb[:], sel16dev, inc="cst")
        ld0 = [ev_cst]
        for f in (lambda e: e.memset(ones_col[:], 1.0),
                  lambda e: e.memset(onesf_row[:], 1.0),
                  lambda e: e.memset(expbias[:], EXP_SHIFT),
                  lambda e: e.memset(neghalf[:], -0.5),
                  lambda e: e.memset(epscol[:], 1e-5),
                  lambda e: e.memset(vtm_sb[0][:], 1.0),
                  lambda e: e.memset(vtm_sb[1][:], 1.0)):
            P.emit("vector", f, inc="dve")
        P.emit("vector", lambda e: e.tensor_scalar(mu_sb[:, 16:32], mu_sb[:, 0:16],
                                                   -1.0, 1.0, ALU.mult, ALU.add),
               waits=ld0, inc="dve")
        consts_ev = P.now("dve")

        def xn2v(c, t0, n):
            return xn2[:, T * c + t0: T * c + t0 + n]

        xbf_free = [None, None]
        xf_free = [None] * 4
        sxx_mm_ev = {}
        stats_bank_free = None
        bcast_bank_free = None
        rows_free = None
        xn2_evs = {}

        for t in range(NT):
            sl = t % 2
            ev_x = P.dma("gpsimd", xbf_t[sl][:], xdev[:, :, QT * t:QT * (t + 1)],
                         waits=[xbf_free[sl]], inc=f"xbf{sl}")
            for c in range(CH):
                evq = P.emit("vector",
                             lambda e, c=c, sl=sl: e.tensor_mul(
                                 xsq_t[:, c % 2, :], xbf_t[sl][:, c, :], xbf_t[sl][:, c, :]),
                             waits=[ev_x, sxx_mm_ev.get(c - 2)], inc="dve")
                P.emit("tensor",
                       lambda e, c=c, sl=sl: e.matmul(
                           PS_S[0][0:1, 0:512], ones_col[:], xbf_t[sl][:, c, :],
                           start=(c == 0), stop=(c == CH - 1)),
                       waits=[ev_x, consts_ev] + ([stats_bank_free] if c == 0 else []))
                sxx_mm_ev[c] = P.emit("tensor",
                                      lambda e, c=c: e.matmul(
                                          PS_S[0][0:1, 512:1024], ones_col[:], xsq_t[:, c % 2, :],
                                          start=(c == 0), stop=(c == CH - 1)),
                                      waits=[evq], inc="pe")
            stats_pe = P.now("pe")
            xbf_free[sl] = stats_pe
            ev_sx = P.emit("vector",
                           lambda e: e.tensor_copy(r_a[0:1, :], PS_S[0][0:1, 0:512]),
                           waits=[stats_pe, rows_free], inc="dve")
            ev_sxx = P.emit("vector",
                            lambda e: e.tensor_copy(r_b[0:1, :], PS_S[0][0:1, 512:1024]),
                            waits=[stats_pe], inc="dve")
            stats_bank_free = ev_sxx
            e1 = P.emit("vector", lambda e: e.tensor_scalar_mul(r_a[0:1, :], r_a[0:1, :], 1.0 / D),
                        waits=[ev_sx], inc="dve")
            e2 = P.emit("vector", lambda e: e.tensor_scalar_mul(r_b[0:1, :], r_b[0:1, :], 1.0 / D),
                        waits=[ev_sxx], inc="dve")
            e3 = P.emit("vector", lambda e: e.tensor_mul(r_c[0:1, :], r_a[0:1, :], r_a[0:1, :]),
                        waits=[e1], inc="dve")
            e4 = P.emit("vector", lambda e: e.tensor_sub(r_b[0:1, :], r_b[0:1, :], r_c[0:1, :]),
                        waits=[e2, e3], inc="dve")
            e5 = P.emit("vector", lambda e: e.tensor_scalar(r_b[0:1, :], r_b[0:1, :],
                                                            1.0 + 1e-5, 1e-10, ALU.mult, ALU.add),
                        waits=[e4], inc="dve")
            e6 = P.emit("scalar", lambda e: e.activation(r_b[0:1, :], r_b[0:1, :], AF.Ln),
                        waits=[e5], inc="act")
            e7 = P.emit("scalar", lambda e: e.activation(r_c[0:1, :], r_b[0:1, :], AF.Exp,
                                                         scale=neghalf[0:1, 0:1]),
                        waits=[e6], inc="act")
            e8 = P.emit("vector", lambda e: e.tensor_mul(r_a[0:1, :], r_a[0:1, :], r_c[0:1, :]),
                        waits=[e7], inc="dve")
            evb1 = P.emit("tensor",
                          lambda e: e.matmul(PS_S[1][:, 0:512], onesf_row[0:1, :],
                                             r_c[0:1, :], start=True, stop=True),
                          waits=[e7, e8, bcast_bank_free], inc="pe")
            evb2 = P.emit("tensor",
                          lambda e: e.matmul(PS_S[1][:, 512:1024], onesf_row[0:1, :],
                                             r_a[0:1, :], start=True, stop=True),
                          waits=[e8], inc="pe")
            rows_free = evb2
            evc1 = P.emit("vector",
                          lambda e, sl=sl: e.tensor_copy(bc_t[sl][:, 0, :], PS_S[1][:, 0:512]),
                          waits=[evb1, P.now("gps")], inc="dve")
            evc2 = P.emit("vector",
                          lambda e, sl=sl: e.tensor_copy(bc_t[sl][:, 1, :], PS_S[1][:, 512:1024]),
                          waits=[evb2], inc="dve")
            bcast_bank_free = evc2
            for c in range(CH):
                fsl = (t * CH + c) % 4
                eng, esem = ("vector", "dve")
                ev_xf = P.dma("sync", xf_t[fsl][:], xdev[:, c, QT * t:QT * (t + 1)],
                              waits=[xf_free[fsl]], inc=f"xf{fsl}")
                P.emit(eng,
                       lambda e, fsl=fsl, sl=sl: e.tensor_mul(xf_t[fsl][:], xf_t[fsl][:], bc_t[sl][:, 0, :]),
                       waits=[ev_xf, evc1], inc=esem)
                esub = P.emit(eng,
                              lambda e, fsl=fsl, sl=sl: e.tensor_sub(xf_t[fsl][:], xf_t[fsl][:], bc_t[sl][:, 1, :]),
                              waits=[evc2], inc=esem)
                e_ap = P.emit("vector",
                              lambda e, c=c, t=t, fsl=fsl: e.tensor_scalar(
                                  xn2v(c, QT * t, QT), xf_t[fsl][:],
                                  gb_sb[:, c:c + 1], gb_sb[:, 8 + c:9 + c], ALU.mult, ALU.add),
                              waits=[esub], inc="dve")
                xf_free[fsl] = e_ap
                xn2_evs[(t, c)] = e_ap

        if dbg >= 1 and phases == 1:
            dump_free = {}
            for t in range(NT):
                for c in range(CH):
                    fsl = (t * CH + c) % 2
                    ev = P.emit("vector",
                                lambda e, c=c, t=t, fsl=fsl: e.tensor_copy(
                                    x2sm[fsl][:], xn2v(c, QT * t, QT)),
                                waits=[xn2_evs[(t, c)], dump_free.get(fsl)], inc="dve")
                    dump_free[fsl] = P.dma("sync", dbg_out["xn2"][:, c, QT * t:QT * (t + 1)],
                                           x2sm[fsl][:], waits=[ev], inc="st")

        xn2_ready = P.now("dve")

        if phases >= 2:
            s_free = [None, None]
            eb_free = [None] * 4
            wq_free = None
            proj_done = [None] * NPAIR
            qk_last = [None] * NPAIR
            av_last = [None] * NPAIR
            ctx_bank_free = None
            ctx_bank_free2 = None
            psd_free = None
            pstr_free = None
            ctxu_free = [None, None]
            spill_evs = {}
            norm_done = [None] * NPAIR
            jj = 0

            def emit_proj_group(p, gi):
                nonlocal psd_free, pstr_free
                sl = p % 2
                mt = gi % 3
                tt = gi // 3
                evs = []
                for c in range(CH):
                    w = [wq_ev[p], xn2_ready]
                    if c == 0:
                        w.append(psd_free)
                    ev = P.emit("tensor",
                                lambda e, c=c, mt=mt, tt=tt: e.matmul(
                                    PS_D[:, 0:512],
                                    wqkv_sb[:, 384 * c + 128 * mt: 384 * c + 128 * (mt + 1)],
                                    xn2v(c, QT * tt, QT),
                                    start=(c == 0), stop=(c == CH - 1)),
                                waits=w, inc="pe" if c == CH - 1 else None)
                    if c == CH - 1:
                        evs.append(ev)
                mm_ev = evs[-1]
                ev_guard = qk_last[p - 2] if p >= 2 and mt < 2 else None
                ev_guard_av = av_last[p - 2] if p >= 2 and mt == 2 else None
                if mt == 0:
                    evac = P.emit("vector",
                                  lambda e, sl=sl, tt=tt: e.tensor_scalar_mul(
                                      qp_sb[sl][:, QT * tt:QT * (tt + 1)], PS_D[:, 0:512], 0.125),
                                  waits=[mm_ev, ev_guard], inc="dve")
                    psd_free = evac
                elif mt == 1:
                    evac = P.emit("vector",
                                  lambda e, sl=sl, tt=tt: e.tensor_copy(
                                      kp_sb[sl][:, QT * tt:QT * (tt + 1)], PS_D[:, 0:512]),
                                  waits=[mm_ev, ev_guard], inc="dve")
                    psd_free = evac
                else:
                    evac0 = P.emit("vector",
                                   lambda e: e.tensor_copy(vfm_st[:, 0:512], PS_D[:, 0:512]),
                                   waits=[mm_ev, vfm_free[0]], inc="dve")
                    psd_free = evac0
                    trev = None
                    for b in range(4):
                        trev = P.emit("tensor",
                                      lambda e, b=b: e.transpose(
                                          PS_TR[:, 128 * b:128 * (b + 1)],
                                          vfm_st[:, 128 * b:128 * (b + 1)], ident_bf[:]),
                                      waits=[evac0] + ([pstr_free] if b == 0 else []),
                                      inc="pe" if b == 3 else None)
                    dst = bass.AP(vtm_sb[sl], 130 * (4 * tt),
                                  [[130 * NJ, 128], [130, 4], [65, 2], [1, 64]])
                    evac = P.emit("vector",
                                  lambda e, dst=dst: e.tensor_copy(dst, PS_TR[:, 0:512]),
                                  waits=[trev, ev_guard_av], inc="dve")
                    pstr_free = evac
                    vfm_free[0] = evac
                return evac

            vfm_free = [None]
            wq_ev = {}

            def emit_pair_load(p):
                w = [wq_free] if wq_free is not None else []
                wq_ev[p] = P.dma("gpsimd", wqkv_sb[:], wqkvdev[p], waits=w, inc="wq")

            emit_pair_load(0)
            proj_evs = [emit_proj_group(0, gi) for gi in range(24)]
            proj_done[0] = proj_evs[-1]
            wq_free = P.now("pe")
            srow_free = None

            for p in range(NPAIR):
                sl = p % 2
                proj_q = list(range(24)) if p + 1 < NPAIR else []
                pace_ctr = [0]
                if p + 1 < NPAIR:
                    emit_pair_load(p + 1)
                last_av_ev = None
                last_qk_ev = None
                pend_av = None
                for t in range(NT):
                    nj = 4 * t + 4
                    for j in range(nj):
                        par = jj % 2
                        eb = jj % 4
                        w = [proj_done[p], s_free[par]]
                        P.emit("tensor",
                               lambda e, sl=sl, t=t, j=j, par=par: e.matmul(
                                   PS_S[par][:, 0:512],
                                   kp_sb[sl][0:64, 128 * j:128 * (j + 1)],
                                   qp_sb[sl][0:64, QT * t:QT * (t + 1)],
                                   start=True, stop=True, tile_position=(0, 0)),
                               waits=w)
                        qk_ev = P.emit("tensor",
                                       lambda e, sl=sl, t=t, j=j, par=par: e.matmul(
                                           PS_S[par][:, 512:1024],
                                           kp_sb[sl][64:128, 128 * j:128 * (j + 1)],
                                           qp_sb[sl][64:128, QT * t:QT * (t + 1)],
                                           start=True, stop=True, tile_position=(64, 0)),
                                       inc="pe")
                        last_qk_ev = qk_ev
                        if pend_av is not None:
                            last_av_ev = pend_av()
                            pend_av = None
                        exp_ev = P.emit("scalar",
                                        lambda e, eb=eb, par=par: e.activation(
                                            exps_sb[eb][:, :], PS_S[par][:, :], AF.Exp,
                                            bias=expbias[:, 0:1]),
                                        waits=[qk_ev, eb_free[eb]], inc="act")
                        s_free[par] = exp_ev
                        ready_a = exp_ev
                        ready_b = exp_ev
                        if j >= 4 * t:
                            o = j - 4 * t
                            ready_a = P.emit("vector",
                                             lambda e, eb=eb, o=o: e.tensor_mul(
                                                 exps_sb[eb][:, 0:512], exps_sb[eb][:, 0:512],
                                                 masks_bf[:, QT * o:QT * (o + 1)]),
                                             waits=[exp_ev], inc="dve")
                            ready_b = P.emit("vector",
                                             lambda e, eb=eb, o=o: e.tensor_mul(
                                                 exps_sb[eb][:, 512:1024], exps_sb[eb][:, 512:1024],
                                                 masks_bf[:, QT * o:QT * (o + 1)]),
                                             inc="dve")

                        def mk_av(j=j, t=t, sl=sl, eb=eb, nj=nj, ready_a=ready_a, ready_b=ready_b):
                            w = [ready_a]
                            if j == 0:
                                w.append(ctx_bank_free)
                                w.append(ctx_bank_free2)
                            P.emit("tensor",
                                   lambda e: e.matmul(
                                       PS_C[0:65, 0:512],
                                       bass.AP(vtm_sb[sl], 130 * j, [[130 * NJ, 128], [1, 65]]),
                                       exps_sb[eb][:, 0:512],
                                       start=(j == 0), stop=(j == nj - 1)),
                                   waits=w)
                            ev = P.emit("tensor",
                                        lambda e: e.matmul(
                                            PS_C[0:65, 512:1024],
                                            bass.AP(vtm_sb[sl], 130 * j + 65, [[130 * NJ, 128], [1, 65]]),
                                            exps_sb[eb][:, 512:1024],
                                            start=(j == 0), stop=(j == nj - 1)),
                                        waits=[ready_b], inc="pe")
                            eb_free[eb] = ev
                            return ev
                        pend_av = mk_av
                        jj += 1
                        if p + 1 < NPAIR and proj_q:
                            pace_ctr[0] += 1
                            if pace_ctr[0] >= 6:
                                pace_ctr[0] = 0
                                gi = proj_q.pop(0)
                                ev = emit_proj_group(p + 1, gi)
                                if gi == 23:
                                    proj_done[p + 1] = ev
                    last_av_ev = pend_av()
                    pend_av = None
                    ca = P.emit("vector",
                                lambda e, sl=sl, t=t: e.tensor_copy(
                                    ctxu_sb[sl][0:64, QT * t:QT * (t + 1)], PS_C[0:64, 0:512]),
                                waits=[last_av_ev, ctxu_free[sl]], inc="dve")
                    cb = P.emit("vector",
                                lambda e, sl=sl, t=t: e.tensor_copy(
                                    ctxu_sb[sl][64:128, QT * t:QT * (t + 1)], PS_C[0:64, 512:1024]),
                                inc="dve")

                    sa = P.emit("scalar",
                                lambda e, t=t: e.copy(
                                    srowA[0:1, QT * t:QT * (t + 1)], PS_C[64:65, 0:512]),
                                waits=[srow_free, last_av_ev], inc="act")
                    sbv = P.emit("scalar",
                                 lambda e, t=t: e.copy(
                                     srowB[0:1, QT * t:QT * (t + 1)], PS_C[64:65, 512:1024]),
                                 inc="act")
                    ctx_bank_free = sbv
                    ctx_bank_free2 = cb
                while proj_q:
                    gi = proj_q.pop(0)
                    ev = emit_proj_group(p + 1, gi)
                    if gi == 23:
                        proj_done[p + 1] = ev
                qk_last[p] = last_qk_ev
                av_last[p] = last_av_ev
                wq_free = P.now("pe")
                r2 = P.now("act")
                gv = None
                for rr in range(16):
                    t2, h = rr // 2, rr % 2
                    srw = srowA if h == 0 else srowB
                    gv = P.emit("tensor",
                                lambda e, rr=rr, t2=t2, srw=srw: e.matmul(
                                    PS_D[0:16, 0:512], eye16_sb[0:1, 16 * rr:16 * (rr + 1)],
                                    srw[0:1, QT * t2:QT * (t2 + 1)],
                                    start=(rr == 0), stop=(rr == 15)),
                                waits=[r2, consts_ev, psd_free] if rr == 0 else [],
                                inc="pe" if rr == 15 else None)
                srow_free = gv

                def _recip16(e):
                    with nc.allow_low_precision(reason="bf16 softmax denominators"):
                        return e.reciprocal(recip16[:], PS_D[0:16, 0:512])
                r16 = P.emit("vector", _recip16, waits=[gv], inc="dve")
                psd_free = r16
                nb_ev = None
                psc_norm_free = None
                for t2 in range(NT):
                    if t2 % 2 == 0:
                        dst_ps = PS_D[:, 0:512]
                        wbc = [r16, psd_free]
                    else:
                        dst_ps = PS_C[:, 0:512]
                        wbc = [r16, psc_norm_free]
                    bm = P.emit("tensor",
                                lambda e, t2=t2, dst_ps=dst_ps: e.matmul(
                                    dst_ps, sel16_sb[:, 128 * t2:128 * (t2 + 1)],
                                    recip16[:], start=True, stop=True),
                                waits=wbc, inc="pe")
                    nb_ev = P.emit("vector",
                                   lambda e, sl=sl, t2=t2, dst_ps=dst_ps: e.tensor_mul(
                                       ctxu_sb[sl][:, QT * t2:QT * (t2 + 1)],
                                       ctxu_sb[sl][:, QT * t2:QT * (t2 + 1)], dst_ps),
                                   waits=[bm], inc="dve")
                    if t2 % 2 == 0:
                        psd_free = nb_ev
                    else:
                        psc_norm_free = nb_ev
                ctx_bank_free = psc_norm_free
                norm_done[p] = nb_ev
                ctxu_free[sl] = P.dma("sync", ctxspill[p], ctxu_sb[sl][:],
                                      waits=[nb_ev], inc=f"stc{sl}")
                spill_evs[p] = ctxu_free[sl]
                if dbg >= 2:
                    P.dma("gpsimd", dbg_out["ctx"][p], ctxspill[p],
                          waits=[ctxu_free[sl]], inc="ldg")
                    P.dma("gpsimd", dbg_out["sums"][p:p + 1, 0:T], srowA, waits=[r2], inc="ldg")
                    P.dma("gpsimd", dbg_out["sums"][p:p + 1, T:2 * T], srowB, waits=[r2], inc="ldg")

        if phases >= 3:
            bar = [P.now("pe"), P.now("act"), P.now("dve"), P.now("gps"),
                   P.now("st"), P.now("ld"), P.now("ldg")]
            wo_dst = bass.AP(xn2, 0, [[CH * T, 128], [1024, CH], [1, 1024]])
            ev_wo = P.dma("gpsimd", wo_dst, wodev, waits=bar, inc="wld")
            wk_dst = bass.AP(xn2, 8192, [[CH * T, 128], [1024, CH], [1, 1024]])
            wv_dst = bass.AP(xn2, 16384, [[CH * T, 128], [1024, CH], [1, 1024]])
            ev_wk = P.dma("gpsimd", wk_dst, wkdev, waits=bar, inc="wld")
            ev_wv = P.dma("gpsimd", wv_dst, wvdev, waits=bar, inc="wld")
            ev_wo = ev_wk = ev_wv
            wo_v = lambda c, m: xn2[:, 1024 * c + 128 * m: 1024 * c + 128 * (m + 1)]

            site_ps = [PS_C, PS_S[1]]
            site_free = [bar[2], bar[1]]
            site_idx = [0]
            rows_free = [bar[0]]

            def acquire_site():
                par = site_idx[0] % 2
                site_idx[0] += 1
                return site_ps[par], par

            def rows_and_bcast(ps, par, sl, stats_pe):
                a = P.emit("vector", lambda e, ps=ps: e.tensor_scalar_mul(
                               r_a[0:1, :], ps[0:1, 0:512], 1.0 / D),
                           waits=[stats_pe, rows_free[0]], inc="dve")
                m2 = P.emit("vector", lambda e: e.tensor_mul(r_c[0:1, :], r_a[0:1, :], r_a[0:1, :]),
                            waits=[a], inc="dve")
                v = P.emit("vector", lambda e, ps=ps: e.scalar_tensor_tensor(
                               r_b[0:1, :], ps[0:1, 512:1024], 1.0 / D, r_c[0:1, :],
                               ALU.mult, ALU.subtract),
                           waits=[m2], inc="dve")
                ln = P.emit("scalar", lambda e: e.activation(
                                r_b[0:1, :], r_b[0:1, :], AF.Ln,
                                bias=epscol[0:1, 0:1]),
                            waits=[v], inc="act")
                ex = P.emit("scalar", lambda e: e.activation(r_c[0:1, :], r_b[0:1, :], AF.Exp,
                                                             scale=neghalf[0:1, 0:1]),
                            waits=[ln], inc="act")
                mr = P.emit("vector", lambda e: e.tensor_mul(r_a[0:1, :], r_a[0:1, :], r_c[0:1, :]),
                            waits=[ex], inc="dve")
                evb1 = P.emit("tensor", lambda e, ps=ps: e.matmul(
                                  ps[:, 0:512], onesf_row[0:1, :], r_c[0:1, :],
                                  start=True, stop=True),
                              waits=[ex, mr], inc="pe")
                evb2 = P.emit("tensor", lambda e, ps=ps: e.matmul(
                                  ps[:, 512:1024], onesf_row[0:1, :], r_a[0:1, :],
                                  start=True, stop=True),
                              waits=[mr], inc="pe")
                rows_free[0] = evb2
                evc1 = P.emit("vector", lambda e, sl=sl, ps=ps: e.tensor_copy(
                                  bc_t[sl][:, 0, :], ps[:, 0:512]),
                              waits=[evb1, P.now("gps")], inc="dve")
                evc2 = P.emit("vector", lambda e, sl=sl, ps=ps: e.tensor_copy(
                                  bc_t[sl][:, 1, :], ps[:, 512:1024]),
                              waits=[evb2], inc="dve")
                site_free[par] = evc2
                return evc1, evc2

            mm_reg2 = [(PS_S[0], 0), (PS_S[0], 512)]
            reg_free = [bar[1], bar[1]]
            au_free = [[], []]
            cx_free = [None, None]
            x2_evs = {}
            xf_free2 = list(xf_free)
            x2sm_free = [None] * 4
            wo_mm_last = None

            for t in range(NT):
                sl = t % 2
                cx = ctxu_sb[sl]
                ev_cx = []
                for c in range(CH):
                    ev_cx.append(P.dma("sync", cx[:, QT * c:QT * (c + 1)],
                                       ctxspill[c, :, QT * t:QT * (t + 1)],
                                       waits=[spill_evs[6], spill_evs[7]]
                                       + ([cx_free[sl]] if c == 0 else []),
                                       inc=f"cx{sl}"))
                ev_cx = [ev_cx[-1]] * CH
                au = xbf_t[sl]
                evac_evs = {}
                cs_evs = {}
                ps, par = acquire_site()
                for m in range(CH):
                    reg, off = mm_reg2[m % 2]
                    for c in range(CH):
                        w = [ev_wo, ev_cx[c]]
                        if c == 0:
                            w.append(reg_free[m % 2])
                        ev = P.emit("tensor",
                                    lambda e, c=c, m=m, reg=reg, off=off, cx=cx: e.matmul(
                                        reg[:, off:off + 512], wo_v(c, m),
                                        cx[:, QT * c:QT * (c + 1)],
                                        start=(c == 0), stop=(c == CH - 1)),
                                    waits=w, inc="pe" if c == CH - 1 else None)
                    wo_mm_last = ev
                    ae = P.emit("vector",
                                lambda e, m=m, reg=reg, off=off, au=au: e.tensor_copy(
                                    au[:, m, :], reg[:, off:off + 512]),
                                waits=[ev] + (au_free[sl] if m == 0 else []), inc="dve")
                    reg_free[m % 2] = ae
                    evac_evs[m] = ae
                    sqeng, sqsem = ("vector", "dve")
                    sq = P.emit(sqeng,
                                lambda e, m=m, au=au: e.tensor_mul(
                                    xsq_t[:, m % 2, :], au[:, m, :], au[:, m, :]),
                                waits=[ae, cs_evs.get(m - 2)], inc=sqsem)
                    P.emit("tensor",
                           lambda e, m=m, au=au, ps=ps: e.matmul(
                               ps[0:1, 0:512], ones_col[:], au[:, m, :],
                               start=(m == 0), stop=(m == CH - 1)),
                           waits=[ae] + ([site_free[par]] if m == 0 else []))
                    cs_evs[m] = P.emit("tensor",
                                       lambda e, m=m, ps=ps: e.matmul(
                                           ps[0:1, 512:1024], ones_col[:], xsq_t[:, m % 2, :],
                                           start=(m == 0), stop=(m == CH - 1)),
                                       waits=[sq], inc="pe")
                evc1, evc2 = rows_and_bcast(ps, par, sl, cs_evs[CH - 1])
                if dbg >= 5 and phases == 3:
                    au_dump = P.dma("gpsimd", dbg_out["au"][t], au[:],
                                    waits=[evac_evs[CH - 1]], inc="ldg")
                    au_free[sl] = au_free[sl] + [au_dump]
                xf_evs = {}
                for c in range(4):
                    fsl = c % 4
                    xf_evs[c] = P.dma("sync", xf_t[fsl][:], xdev[:, c, QT * t:QT * (t + 1)],
                                      waits=[xf_free2[fsl]], inc=f"xf{fsl}")
                for c in range(CH):
                    fsl = (t * CH + c) % 4
                    eng, esem = ("vector", "dve")
                    ev_xf = xf_evs[c]
                    s1 = P.emit(eng,
                                lambda e, c=c, fsl=fsl, sl=sl, au=au: e.tensor_mul(
                                    x2sm[fsl][:], au[:, c, :], bc_t[sl][:, 0, :]),
                                waits=[evc1, evac_evs[c], x2sm_free[fsl]], inc=esem)
                    s2 = P.emit(eng,
                                lambda e, fsl=fsl, sl=sl: e.tensor_sub(
                                    x2sm[fsl][:], x2sm[fsl][:], bc_t[sl][:, 1, :]),
                                waits=[evc2], inc=esem)
                    s3 = P.emit(eng,
                                lambda e, fsl=fsl: e.tensor_add(
                                    x2sm[fsl][:], x2sm[fsl][:], xf_t[fsl][:]),
                                waits=[ev_xf], inc=esem)
                    xf_free2[fsl] = s3
                    if c + 4 < CH:
                        xf_evs[c + 4] = P.dma("sync", xf_t[fsl][:],
                                              xdev[:, c + 4, QT * t:QT * (t + 1)],
                                              waits=[s3], inc=f"xf{fsl}")
                    ev_st = P.dma("sync", x2spill[:, c, QT * t:QT * (t + 1)], x2sm[fsl][:],
                                  waits=[s3], inc=f"stx{fsl}")
                    x2sm_free[fsl] = ev_st
                    x2_evs[(t, c)] = ev_st
                    if c >= CH - 2:
                        au_free[sl] = (au_free[sl] + [s1])[-2:] if c == CH - 1 else [s1]
                    if dbg >= 3 and phases == 3:
                        P.dma("gpsimd", dbg_out["x2"][:, c, QT * t:QT * (t + 1)],
                              x2spill[:, c, QT * t:QT * (t + 1)], waits=[ev_st], inc="ldg")
                cx_free[sl] = P.now("pe")

        if phases >= 4:
            wr_dst = bass.AP(xn2, 0, [[CH * T, 128], [1024, CH], [1, 1024]])
            ev_wr = P.dma("gpsimd", wr_dst, wrdev, waits=[wo_mm_last], inc="wrd")
            wr_v = lambda c, m: xn2[:, 1024 * c + 128 * m: 1024 * c + 128 * (m + 1)]
            wk_v = lambda c, m: xn2[:, 8192 + 1024 * c + 128 * m: 8192 + 1024 * c + 128 * (m + 1)]
            wv_v = lambda c, m: xn2[:, 16384 + 1024 * c + 128 * m: 16384 + 1024 * c + 128 * (m + 1)]
            Z0 = 24576
            zv = lambda c, i0, n: xn2[:, Z0 + 513 * c + i0: Z0 + 513 * c + i0 + n]
            z_head = bass.AP(xn2, Z0, [[CH * T, 128], [513, CH], [1, 1]])
            z_tail = bass.AP(xn2, Z0 + 512, [[CH * T, 128], [513, CH], [1, 1]])
            xr_v = lambda c: qp_sb[0][:, QT * c:QT * (c + 1)]
            xk_v = lambda c: qp_sb[1][:, QT * c:QT * (c + 1)]
            r_v = lambda m: kp_sb[0][:, QT * m:QT * (m + 1)]
            ka_v = lambda m: kp_sb[1][:, QT * m:QT * (m + 1)]
            g_v = lambda m: vtm_sb[0][:, QT * m:QT * (m + 1)]

            stx_finals = [P.now(f"stx{i}") for i in range(4)]
            mm_reg3 = [(PS_S[0], 0), (PS_S[0], 512), (PS_D, 0)]
            reg_free3 = [P.now("act"), P.now("act"), P.now("dve")]
            x2b_free = list(cx_free)
            xq_free = list(xf_free2)
            xo_free = [None] * 3
            zcopy_prev = None
            xr_done_prev = []
            g_free = []
            prev_cs4 = {}
            rk_reads_z = None

            for t in range(NT):
                sl = t % 2
                x2b = bass.AP(ctxu_sb[sl], 0, [[T, 128], [512, CH], [1, 512]])
                ev_x2b = P.dma("gpsimd", x2b, x2spill[:, :, QT * t:QT * (t + 1)],
                               waits=[x2b_free[sl]] + stx_finals,
                               inc=f"x2b{sl}")
                cs3 = {}
                ps, par = acquire_site()
                for c in range(CH):
                    sqeng, sqsem = ("vector", "dve")
                    w_guard = cs3.get(c - 2) if c >= 2 else prev_cs4.get(CH - 2 + c)
                    sq = P.emit(sqeng,
                                lambda e, c=c, sl=sl: e.tensor_mul(
                                    xsq_t[:, c % 2, :],
                                    bass.AP(ctxu_sb[sl], 512 * c, [[T, 128], [1, 512]]),
                                    bass.AP(ctxu_sb[sl], 512 * c, [[T, 128], [1, 512]])),
                                waits=[ev_x2b, w_guard], inc=sqsem)
                    P.emit("tensor",
                           lambda e, c=c, sl=sl, ps=ps: e.matmul(
                               ps[0:1, 0:512], ones_col[:],
                               bass.AP(ctxu_sb[sl], 512 * c, [[T, 128], [1, 512]]),
                               start=(c == 0), stop=(c == CH - 1)),
                           waits=[ev_x2b] + ([site_free[par]] if c == 0 else []))
                    cs3[c] = P.emit("tensor",
                                    lambda e, c=c, ps=ps: e.matmul(
                                        ps[0:1, 512:1024], ones_col[:], xsq_t[:, c % 2, :],
                                        start=(c == 0), stop=(c == CH - 1)),
                                    waits=[sq], inc="pe")
                evc1, evc2 = rows_and_bcast(ps, par, sl, cs3[CH - 1])
                if t == 0:
                    zb = P.emit("vector", lambda e: e.memset(z_head, 0.0),
                                waits=[ev_wv], inc="dve")
                else:
                    zb = P.emit("vector", lambda e: e.tensor_copy(z_head, z_tail),
                                waits=xr_done_prev, inc="dve")
                xr_evs = {}
                xq_evs = {}
                tcol_free = {}
                for c in range(4):
                    fsl = c % 4
                    xq_evs[c] = P.dma("sync", xf_t[fsl][:], x2spill[:, c, QT * t:QT * (t + 1)],
                                      waits=[xq_free[fsl]] + stx_finals, inc=f"xf{fsl}")
                for c in range(CH):
                    fsl = c % 4
                    eng, esem = ("vector", "dve")
                    ev_xq = xq_evs[c]
                    z1 = P.emit(eng,
                                lambda e, fsl=fsl, sl=sl: e.tensor_mul(
                                    xf_t[fsl][:], xf_t[fsl][:], bc_t[sl][:, 0, :]),
                                waits=[ev_xq, evc1, rk_reads_z], inc=esem)
                    z2 = P.emit(eng,
                                lambda e, c=c, fsl=fsl, sl=sl: e.tensor_sub(
                                    zv(c, 1, QT), xf_t[fsl][:], bc_t[sl][:, 1, :]),
                                waits=[evc2, zb], inc=esem)
                    xq_free[fsl] = z2
                    if c + 4 < CH:
                        xq_evs[c + 4] = P.dma("sync", xf_t[fsl][:],
                                              x2spill[:, c + 4, QT * t:QT * (t + 1)],
                                              waits=[z2], inc=f"xf{fsl}")
                    tcol = vtm_sb[1][:, 0:512] if c % 2 == 0 else vtm_sb[1][:, 512:1024]
                    d1 = P.emit(eng,
                                lambda e, c=c, tcol=tcol: e.tensor_sub(
                                    tcol, zv(c, 1, QT), zv(c, 0, QT)),
                                waits=[z2, zcopy_prev, tcol_free.get(c % 2)], inc=esem)
                    x1 = P.emit("vector",
                                lambda e, c=c, tcol=tcol: e.scalar_tensor_tensor(
                                    xr_v(c), tcol, mu_sb[:, c:c + 1], zv(c, 0, QT),
                                    ALU.mult, ALU.add),
                                waits=[d1], inc="dve")
                    x2e = P.emit("vector",
                                 lambda e, c=c, tcol=tcol: e.scalar_tensor_tensor(
                                     xk_v(c), tcol, mu_sb[:, 8 + c:9 + c], zv(c, 0, QT),
                                     ALU.mult, ALU.add),
                                 waits=[x1], inc="dve")
                    xr_evs[c] = x2e
                    tcol_free[c % 2] = x2e
                xr_done_prev = [P.now("dve"), P.now("gps")]
                sig_evs = {}
                for m in range(CH):
                    reg, off = mm_reg3[m % 3]
                    for c in range(CH):
                        w = [ev_wr, xr_evs[c]]
                        if c == 0:
                            w.append(reg_free3[m % 3])
                        ev = P.emit("tensor",
                                    lambda e, c=c, m=m, reg=reg, off=off: e.matmul(
                                        reg[:, off:off + 512], wr_v(c, m), xr_v(c),
                                        start=(c == 0), stop=(c == CH - 1)),
                                    waits=w, inc="pe" if c == CH - 1 else None)
                    sig_evs[m] = P.emit("scalar",
                                        lambda e, m=m, reg=reg, off=off: e.activation(
                                            r_v(m), reg[:, off:off + 512], AF.Sigmoid),
                                        waits=[ev] + (g_free if m == 0 else []), inc="act")
                    reg_free3[m % 3] = sig_evs[m]
                ka_evs = {}
                for m in range(CH):
                    reg, off = mm_reg3[m % 3]
                    for c in range(CH):
                        w = [ev_wk, xr_evs[c]]
                        if c == 0:
                            w.append(reg_free3[m % 3])
                        ev = P.emit("tensor",
                                    lambda e, c=c, m=m, reg=reg, off=off: e.matmul(
                                        reg[:, off:off + 512], wk_v(c, m), xk_v(c),
                                        start=(c == 0), stop=(c == CH - 1)),
                                    waits=w, inc="pe" if c == CH - 1 else None)
                    rl = P.emit("scalar",
                                lambda e, m=m, reg=reg, off=off: e.activation(
                                    ka_v(m), reg[:, off:off + 512], AF.Relu),
                                waits=[ev], inc="act")
                    reg_free3[m % 3] = rl
                    ka_evs[m] = P.emit("vector",
                                       lambda e, m=m: e.tensor_mul(ka_v(m), ka_v(m), ka_v(m)),
                                       waits=[rl], inc="dve")
                gs_evs = {}
                cs4 = {}
                ps2, par2 = acquire_site()
                gfree = g_free
                for m in range(CH):
                    reg, off = mm_reg3[m % 3]
                    for c in range(CH):
                        w = [ev_wv, ka_evs[c]]
                        if c == 0:
                            w.append(reg_free3[m % 3])
                        ev = P.emit("tensor",
                                    lambda e, c=c, m=m, reg=reg, off=off: e.matmul(
                                        reg[:, off:off + 512], wv_v(c, m), ka_v(c),
                                        start=(c == 0), stop=(c == CH - 1)),
                                    waits=w, inc="pe" if c == CH - 1 else None)
                    ge = P.emit("vector",
                                lambda e, m=m, reg=reg, off=off: e.tensor_mul(
                                    g_v(m), r_v(m), reg[:, off:off + 512]),
                                waits=[ev, sig_evs[m]] + (gfree if m == 0 else []), inc="dve")
                    reg_free3[m % 3] = ge
                    gs_evs[m] = ge
                    sqeng, sqsem = ("vector", "dve")
                    sqg = P.emit(sqeng,
                                 lambda e, m=m: e.tensor_mul(
                                     xsq_t[:, m % 2, :], g_v(m), g_v(m)),
                                 waits=[ge, cs4.get(m - 2)], inc=sqsem)
                    P.emit("tensor",
                           lambda e, m=m, ps2=ps2: e.matmul(
                               ps2[0:1, 0:512], ones_col[:], g_v(m),
                               start=(m == 0), stop=(m == CH - 1)),
                           waits=[ge] + ([site_free[par2]] if m == 0 else []))
                    cs4[m] = P.emit("tensor",
                                    lambda e, m=m, ps2=ps2: e.matmul(
                                        ps2[0:1, 512:1024], ones_col[:], xsq_t[:, m % 2, :],
                                        start=(m == 0), stop=(m == CH - 1)),
                                    waits=[sqg], inc="pe")
                evg1, evg2 = rows_and_bcast(ps2, par2, sl, cs4[CH - 1])
                xo_evs = {}
                for c in range(3):
                    osl = c % 3
                    xo_evs[c] = P.dma("sync", osm[osl][:], x2spill[:, c, QT * t:QT * (t + 1)],
                                      waits=[xo_free[osl]], inc=f"osm{osl}")
                for c in range(CH):
                    fsl = c % 4
                    osl = c % 3
                    eng, esem = ("vector", "dve")
                    o1 = P.emit(eng,
                                lambda e, c=c, fsl=fsl, sl=sl: e.tensor_mul(
                                    x2sm[fsl][:], g_v(c), bc_t[sl][:, 0, :]),
                                waits=[evg1, gs_evs[c], x2sm_free[fsl]], inc=esem)
                    o2 = P.emit(eng,
                                lambda e, fsl=fsl, sl=sl: e.tensor_sub(
                                    x2sm[fsl][:], x2sm[fsl][:], bc_t[sl][:, 1, :]),
                                waits=[evg2], inc=esem)
                    o3 = P.emit(eng,
                                lambda e, fsl=fsl, osl=osl: e.tensor_add(
                                    x2sm[fsl][:], x2sm[fsl][:], osm[osl][:]),
                                waits=[xo_evs[c]], inc=esem)
                    xo_free[osl] = o3
                    if c + 3 < CH:
                        xo_evs[c + 3] = P.dma("sync", osm[osl][:],
                                              x2spill[:, c + 3, QT * t:QT * (t + 1)],
                                              waits=[o3], inc=f"osm{osl}")
                    ev_st = P.dma("sync", outdev[:, c, QT * t:QT * (t + 1)], x2sm[fsl][:],
                                  waits=[o3], inc=f"stx{fsl}")
                    x2sm_free[fsl] = ev_st
                x2b_free[sl] = P.now("pe")
                rk_reads_z = P.now("pe")
                zcopy_prev = zb
                g_free = [P.now("dve"), P.now("gps")]
                prev_cs4 = cs4

        P.emit("sync", lambda e: e.nop(),
               waits=[P.now("st"), P.now("dve"), P.now("act"), P.now("pe"),
                      P.now("gps"), P.now("ld"), P.now("ldg"),
                      P.now("stx0"), P.now("stx1"), P.now("stx2"), P.now("stx3"),
                      P.now("stc0"), P.now("stc1")])
        P.replay()
    return nc, dbg_out


from concourse.bass_utils import run_bass_kernel_spmd

_NC_CACHE = {}


def kernel(**inputs):
    if "nc" not in _NC_CACHE:
        _NC_CACHE["nc"] = build(dbg=0, phases=4)[0]
    nc = _NC_CACHE["nc"]
    in_maps = [host_inputs(inputs, core) for core in range(8)]
    res = run_bass_kernel_spmd(nc, in_maps, core_ids=list(range(8)))
    outs = [host_output(r) for r in res.results]
    return np.stack(outs, axis=0).astype(np.float32)


# version 13
# speedup vs baseline: 1.2566x; 1.0084x over previous
import numpy as np
import ml_dtypes
import concourse.bass as bass
import concourse.mybir as mybir

F32 = mybir.dt.float32
BF16 = mybir.dt.bfloat16
AF = mybir.ActivationFunctionType
ALU = mybir.AluOpType

T = 4096
D = 1024
CH = 8
NT = 8
QT = 512
NJ = 32
NPAIR = 8
DH = 64
EXP_SHIFT = -12.0

ENGINES = ("tensor", "scalar", "vector", "gpsimd", "sync")


class MK:

    def __init__(self, nc):
        self.nc = nc
        self.ops = {e: [] for e in ENGINES}
        self.sems = {}
        self.cnt = {}
        self.last_wait = {e: {} for e in ENGINES}

    def sem(self, name):
        self.cnt[name] = 0
        return name

    def emit(self, eng, fn, waits=(), inc=None, n=1):
        ws = []
        for wv in waits:
            if wv is None:
                continue
            s, v = wv
            lw = self.last_wait[eng].get(s, -1)
            if v > lw:
                ws.append((s, v))
                self.last_wait[eng][s] = v
        ev = None
        if inc is not None:
            self.cnt[inc] += n
            ev = (inc, self.cnt[inc])
        self.ops[eng].append((ws, fn, (inc, n) if inc else None))
        return ev

    def dma(self, eng, out, in_, waits=(), inc="ld"):
        return self.emit(eng, lambda e: e.dma_start(out=out, in_=in_),
                         waits=waits, inc=inc, n=16)

    def now(self, sem):
        return (sem, self.cnt[sem])

    def replay(self):
        nc = self.nc
        from contextlib import ExitStack
        with ExitStack() as st:
            for name in self.cnt:
                self.sems[name] = st.enter_context(nc.semaphore(name))
            with nc.Block() as block:
                for eng_name in ENGINES:
                    deco = getattr(block, eng_name)

                    def body(eng, _ops=self.ops[eng_name]):
                        for ws, fn, inc in _ops:
                            for s, v in ws:
                                eng.wait_ge(self.sems[s], v)
                            instr = fn(eng)
                            if inc is not None:
                                instr.then_inc(self.sems[inc[0]], inc[1])
                    deco(body)


def host_inputs(inputs, core):
    x = np.asarray(inputs["L0_attn_x"])
    Wqkv = np.asarray(inputs["L0_attn_W_qkv"])
    Wo = np.asarray(inputs["L0_attn_W_o"])
    Wr = np.asarray(inputs["L0_ffn_W_r"])
    Wk = np.asarray(inputs["L0_ffn_W_k"])
    Wv = np.asarray(inputs["L0_ffn_W_v"])
    g = np.asarray(inputs["L0_attn_ln_gamma"])
    b = np.asarray(inputs["L0_attn_ln_beta"])
    mur = np.asarray(inputs["L0_ffn_mu_r"])
    muk = np.asarray(inputs["L0_ffn_mu_k"])

    d = {}
    xT = np.ascontiguousarray(x[core].T)
    d["xdev"] = np.ascontiguousarray(xT.reshape(CH, 128, T).transpose(1, 0, 2))
    rows = []
    for pr in range(NPAIR):
        h0, h1 = 2 * pr, 2 * pr + 1
        for sec in range(3):
            rows.append(Wqkv[192 * h0 + 64 * sec: 192 * h0 + 64 * sec + 64])
            rows.append(Wqkv[192 * h1 + 64 * sec: 192 * h1 + 64 * sec + 64])
    Wq_re = np.concatenate(rows, axis=0)
    WqT = Wq_re.T
    wq = WqT.reshape(CH, 128, NPAIR, 384).transpose(2, 1, 0, 3)
    d["wqkvdev"] = np.ascontiguousarray(wq)
    for name, W in (("wodev", Wo), ("wrdev", Wr), ("wkdev", Wk), ("wvdev", Wv)):
        WT = W.T
        d[name] = np.ascontiguousarray(WT.reshape(CH, 128, D).transpose(1, 0, 2))
    d["gbdev"] = np.ascontiguousarray(
        np.concatenate([g.reshape(CH, 128).T, b.reshape(CH, 128).T], axis=1))
    d["mudev"] = np.ascontiguousarray(
        np.concatenate([mur.reshape(CH, 128).T, muk.reshape(CH, 128).T], axis=1))
    d["identdev"] = np.eye(128, dtype=np.float32)
    d["eye16dev"] = np.eye(16, dtype=np.float32).reshape(1, 256)
    sel16 = np.zeros((16, 8 * 128), dtype=np.float32)
    for t in range(8):
        sel16[2 * t, 128 * t:128 * t + 64] = 1.0
        sel16[2 * t + 1, 128 * t + 64:128 * t + 128] = 1.0
    d["sel16dev"] = sel16
    masks = np.zeros((128, 4 * QT), dtype=np.float32)
    qq = np.arange(QT)
    kk = np.arange(128)
    for k in range(4):
        masks[:, k * QT:(k + 1) * QT] = (qq[None, :] - 128 * k >= kk[:, None])
    d["maskdev"] = masks
    return d


def host_output(r):
    o = r["outdev"]
    return np.ascontiguousarray(o.transpose(1, 0, 2).reshape(D, T).T)


def build(dbg=0, phases=4):
    nc = bass.Bass("TRN2", target_bir_lowering=False)
    P = MK(nc)
    for s in ("ld", "ldg", "st", "pe", "act", "dve", "gps", "cst", "wq",
              "wld", "wrd", "xbf0", "xbf1", "cx0", "cx1", "x2b0", "x2b1",
              "xf0", "xf1", "xf2", "xf3", "osm0", "osm1", "osm2",
              "stx0", "stx1", "stx2", "stx3", "stc0", "stc1"):
        P.sem(s)

    xdev = nc.dram_tensor("xdev", [128, CH, T], F32, kind="ExternalInput").ap()
    wqkvdev = nc.dram_tensor("wqkvdev", [NPAIR, 128, CH, 384], F32, kind="ExternalInput").ap()
    wodev = nc.dram_tensor("wodev", [128, CH, D], F32, kind="ExternalInput").ap()
    wrdev = nc.dram_tensor("wrdev", [128, CH, D], F32, kind="ExternalInput").ap()
    wkdev = nc.dram_tensor("wkdev", [128, CH, D], F32, kind="ExternalInput").ap()
    wvdev = nc.dram_tensor("wvdev", [128, CH, D], F32, kind="ExternalInput").ap()
    gbdev = nc.dram_tensor("gbdev", [128, 16], F32, kind="ExternalInput").ap()
    mudev = nc.dram_tensor("mudev", [128, 16], F32, kind="ExternalInput").ap()
    identdev = nc.dram_tensor("identdev", [128, 128], F32, kind="ExternalInput").ap()
    eye16dev = nc.dram_tensor("eye16dev", [1, 256], F32, kind="ExternalInput").ap()
    sel16dev = nc.dram_tensor("sel16dev", [16, 1024], F32, kind="ExternalInput").ap()
    maskdev = nc.dram_tensor("maskdev", [128, 4 * QT], F32, kind="ExternalInput").ap()
    outdev = nc.dram_tensor("outdev", [128, CH, T], F32, kind="ExternalOutput").ap()
    ctxspill = nc.dram_tensor("ctxspill", [NPAIR, 128, T], BF16, kind="Internal").ap()
    x2spill = nc.dram_tensor("x2spill", [128, CH, T], F32, kind="Internal").ap()
    dbg_out = {}
    if dbg >= 1 and phases == 1:
        dbg_out["xn2"] = nc.dram_tensor("dbg_xn2", [128, CH, T], F32, kind="ExternalOutput").ap()
    if dbg >= 2:
        dbg_out["ctx"] = nc.dram_tensor("dbg_ctx", [NPAIR, 128, T], F32, kind="ExternalOutput").ap()
        dbg_out["sums"] = nc.dram_tensor("dbg_sums", [NPAIR, 2 * T], F32, kind="ExternalOutput").ap()
    if dbg >= 3 and phases == 3:
        dbg_out["x2"] = nc.dram_tensor("dbg_x2", [128, CH, T], F32, kind="ExternalOutput").ap()
    if dbg >= 5 and phases == 3:
        dbg_out["au"] = nc.dram_tensor("dbg_au", [NT, 128, CH, QT], F32, kind="ExternalOutput").ap()

    from contextlib import ExitStack
    with ExitStack() as stack:
        sb = lambda name, shape, dt: stack.enter_context(nc.sbuf_tensor(name, shape, dt))
        psal = lambda name, shape, dt: stack.enter_context(nc.psum_tensor(name, shape, dt))

        gb_sb = sb("gb_sb", [128, 16], F32)
        mu_sb = sb("mu_sb", [128, 32], F32)
        ident_bf = sb("ident_bf", [128, 128], BF16)
        masks_bf = sb("masks_bf", [128, 4 * QT], BF16)
        ones_col = sb("ones_col", [128, 1], BF16)
        onesf_row = sb("onesf_row", [1, 128], F32)
        expbias = sb("expbias", [128, 1], F32)
        neghalf = sb("neghalf", [128, 1], F32)
        epscol = sb("epscol", [128, 1], F32)

        xn2 = sb("xn2", [128, CH * T], BF16)
        r_a = sb("r_a", [1, QT], F32)
        r_b = sb("r_b", [1, QT], F32)
        r_c = sb("r_c", [1, QT], F32)
        recip16 = sb("recip16", [16, QT], BF16)
        sel16_sb = sb("sel16_sb", [16, 1024], BF16)
        eye16_sb = sb("eye16_sb", [1, 256], BF16)

        xbf_t = [sb(f"xbf{i}", [128, CH, QT], BF16) for i in range(2)]
        srowA = bass.AP(xbf_t[0], 0, [[CH * QT, 1], [1, CH * QT]])
        srowB = bass.AP(xbf_t[1], 0, [[CH * QT, 1], [1, CH * QT]])
        xsq_t = sb("xsq", [128, 2, QT], BF16)
        xf_t = [sb(f"xf{i}", [128, QT], F32) for i in range(4)]
        bc_t = [sb(f"bc{i}", [128, 2, QT], F32) for i in range(2)]

        qp_sb = [sb(f"qp{i}", [128, T], BF16) for i in range(2)]
        kp_sb = [sb(f"kp{i}", [128, T], BF16) for i in range(2)]
        vtm_sb = [sb(f"vtm{i}", [128, NJ * 130], BF16) for i in range(2)]
        vfm_st = sb("vfm_st", [128, QT], BF16)
        wqkv_sb = sb("wqkv_sb", [128, 3072], BF16)
        exps_sb = [sb(f"exps{i}", [128, 1024], BF16) for i in range(4)]
        ctxu_sb = [sb(f"ctxu{i}", [128, T], BF16) for i in range(2)]
        x2sm = [sb(f"x2sm{i}", [128, QT], F32) for i in range(4)]
        osm = [sb(f"osm{i}", [128, QT], F32) for i in range(3)]

        wo_v = lambda c, m: xn2[:, 1024 * c + 128 * m: 1024 * c + 128 * (m + 1)]

        PS_S = [psal("ps_s0", [128, 1024], F32), psal("ps_s1", [128, 1024], F32)]
        PS_C = psal("ps_c", [128, 1024], F32)
        PS_D = psal("ps_d", [128, 512], F32)
        PS_TR = psal("ps_tr", [128, 1024], BF16)

        P.dma("sync", gb_sb[:], gbdev, inc="cst")
        P.dma("sync", mu_sb[:, 0:16], mudev, inc="cst")
        P.dma("gpsimd", ident_bf[:], identdev, inc="cst")
        P.dma("gpsimd", masks_bf[:], maskdev, inc="cst")
        P.dma("gpsimd", eye16_sb[:], eye16dev, inc="cst")
        ev_cst = P.dma("gpsimd", sel16_sb[:], sel16dev, inc="cst")
        ld0 = [ev_cst]
        for f in (lambda e: e.memset(ones_col[:], 1.0),
                  lambda e: e.memset(onesf_row[:], 1.0),
                  lambda e: e.memset(expbias[:], EXP_SHIFT),
                  lambda e: e.memset(neghalf[:], -0.5),
                  lambda e: e.memset(epscol[:], 1e-5),
                  lambda e: e.memset(vtm_sb[0][:], 1.0),
                  lambda e: e.memset(vtm_sb[1][:], 1.0)):
            P.emit("vector", f, inc="dve")
        P.emit("vector", lambda e: e.tensor_scalar(mu_sb[:, 16:32], mu_sb[:, 0:16],
                                                   -1.0, 1.0, ALU.mult, ALU.add),
               waits=ld0, inc="dve")
        consts_ev = P.now("dve")

        def xn2v(c, t0, n):
            return xn2[:, T * c + t0: T * c + t0 + n]

        xbf_free = [None, None]
        xf_free = [None] * 4
        sxx_mm_ev = {}
        stats_bank_free = None
        bcast_bank_free = None
        rows_free = None
        xn2_evs = {}

        for t in range(NT):
            sl = t % 2
            ev_x = P.dma("gpsimd", xbf_t[sl][:], xdev[:, :, QT * t:QT * (t + 1)],
                         waits=[xbf_free[sl]], inc=f"xbf{sl}")
            for c in range(CH):
                evq = P.emit("vector",
                             lambda e, c=c, sl=sl: e.tensor_mul(
                                 xsq_t[:, c % 2, :], xbf_t[sl][:, c, :], xbf_t[sl][:, c, :]),
                             waits=[ev_x, sxx_mm_ev.get(c - 2)], inc="dve")
                P.emit("tensor",
                       lambda e, c=c, sl=sl: e.matmul(
                           PS_S[0][0:1, 0:512], ones_col[:], xbf_t[sl][:, c, :],
                           start=(c == 0), stop=(c == CH - 1)),
                       waits=[ev_x, consts_ev] + ([stats_bank_free] if c == 0 else []))
                sxx_mm_ev[c] = P.emit("tensor",
                                      lambda e, c=c: e.matmul(
                                          PS_S[0][0:1, 512:1024], ones_col[:], xsq_t[:, c % 2, :],
                                          start=(c == 0), stop=(c == CH - 1)),
                                      waits=[evq], inc="pe")
            stats_pe = P.now("pe")
            xbf_free[sl] = stats_pe
            ev_sx = P.emit("vector",
                           lambda e: e.tensor_copy(r_a[0:1, :], PS_S[0][0:1, 0:512]),
                           waits=[stats_pe, rows_free], inc="dve")
            ev_sxx = P.emit("vector",
                            lambda e: e.tensor_copy(r_b[0:1, :], PS_S[0][0:1, 512:1024]),
                            waits=[stats_pe], inc="dve")
            stats_bank_free = ev_sxx
            e1 = P.emit("vector", lambda e: e.tensor_scalar_mul(r_a[0:1, :], r_a[0:1, :], 1.0 / D),
                        waits=[ev_sx], inc="dve")
            e2 = P.emit("vector", lambda e: e.tensor_scalar_mul(r_b[0:1, :], r_b[0:1, :], 1.0 / D),
                        waits=[ev_sxx], inc="dve")
            e3 = P.emit("vector", lambda e: e.tensor_mul(r_c[0:1, :], r_a[0:1, :], r_a[0:1, :]),
                        waits=[e1], inc="dve")
            e4 = P.emit("vector", lambda e: e.tensor_sub(r_b[0:1, :], r_b[0:1, :], r_c[0:1, :]),
                        waits=[e2, e3], inc="dve")
            e5 = P.emit("vector", lambda e: e.tensor_scalar(r_b[0:1, :], r_b[0:1, :],
                                                            1.0 + 1e-5, 1e-10, ALU.mult, ALU.add),
                        waits=[e4], inc="dve")
            e6 = P.emit("scalar", lambda e: e.activation(r_b[0:1, :], r_b[0:1, :], AF.Ln),
                        waits=[e5], inc="act")
            e7 = P.emit("scalar", lambda e: e.activation(r_c[0:1, :], r_b[0:1, :], AF.Exp,
                                                         scale=neghalf[0:1, 0:1]),
                        waits=[e6], inc="act")
            e8 = P.emit("vector", lambda e: e.tensor_mul(r_a[0:1, :], r_a[0:1, :], r_c[0:1, :]),
                        waits=[e7], inc="dve")
            evb1 = P.emit("tensor",
                          lambda e: e.matmul(PS_S[1][:, 0:512], onesf_row[0:1, :],
                                             r_c[0:1, :], start=True, stop=True),
                          waits=[e7, e8, bcast_bank_free], inc="pe")
            evb2 = P.emit("tensor",
                          lambda e: e.matmul(PS_S[1][:, 512:1024], onesf_row[0:1, :],
                                             r_a[0:1, :], start=True, stop=True),
                          waits=[e8], inc="pe")
            rows_free = evb2
            evc1 = P.emit("vector",
                          lambda e, sl=sl: e.tensor_copy(bc_t[sl][:, 0, :], PS_S[1][:, 0:512]),
                          waits=[evb1, P.now("gps")], inc="dve")
            evc2 = P.emit("vector",
                          lambda e, sl=sl: e.tensor_copy(bc_t[sl][:, 1, :], PS_S[1][:, 512:1024]),
                          waits=[evb2], inc="dve")
            bcast_bank_free = evc2
            for c in range(CH):
                fsl = (t * CH + c) % 4
                eng, esem = ("vector", "dve")
                ev_xf = P.dma("sync", xf_t[fsl][:], xdev[:, c, QT * t:QT * (t + 1)],
                              waits=[xf_free[fsl]], inc=f"xf{fsl}")
                P.emit(eng,
                       lambda e, fsl=fsl, sl=sl: e.tensor_mul(xf_t[fsl][:], xf_t[fsl][:], bc_t[sl][:, 0, :]),
                       waits=[ev_xf, evc1], inc=esem)
                esub = P.emit(eng,
                              lambda e, fsl=fsl, sl=sl: e.tensor_sub(xf_t[fsl][:], xf_t[fsl][:], bc_t[sl][:, 1, :]),
                              waits=[evc2], inc=esem)
                e_ap = P.emit("vector",
                              lambda e, c=c, t=t, fsl=fsl: e.tensor_scalar(
                                  xn2v(c, QT * t, QT), xf_t[fsl][:],
                                  gb_sb[:, c:c + 1], gb_sb[:, 8 + c:9 + c], ALU.mult, ALU.add),
                              waits=[esub], inc="dve")
                xf_free[fsl] = e_ap
                xn2_evs[(t, c)] = e_ap

        if dbg >= 1 and phases == 1:
            dump_free = {}
            for t in range(NT):
                for c in range(CH):
                    fsl = (t * CH + c) % 2
                    ev = P.emit("vector",
                                lambda e, c=c, t=t, fsl=fsl: e.tensor_copy(
                                    x2sm[fsl][:], xn2v(c, QT * t, QT)),
                                waits=[xn2_evs[(t, c)], dump_free.get(fsl)], inc="dve")
                    dump_free[fsl] = P.dma("sync", dbg_out["xn2"][:, c, QT * t:QT * (t + 1)],
                                           x2sm[fsl][:], waits=[ev], inc="st")

        xn2_ready = P.now("dve")

        if phases >= 2:
            s_freeA = [None, None]
            s_freeB = [None, None]
            eb_freeA = [None] * 4
            eb_freeB = [None] * 4
            wq_free = None
            proj_done = [None] * NPAIR
            qk_last = [None] * NPAIR
            av_last = [None] * NPAIR
            ctx_bank_free = None
            ctx_bank_free2 = None
            psd_free = None
            pstr_free = None
            ctxu_free = [None, None]
            spill_evs = {}
            norm_done = [None] * NPAIR
            jj = 0

            def emit_proj_group(p, gi):
                nonlocal psd_free, pstr_free
                sl = p % 2
                mt = gi % 3
                tt = gi // 3
                evs = []
                for c in range(CH):
                    w = [wq_ev[p], xn2_ready]
                    if c == 0:
                        w.append(psd_free)
                    ev = P.emit("tensor",
                                lambda e, c=c, mt=mt, tt=tt: e.matmul(
                                    PS_D[:, 0:512],
                                    wqkv_sb[:, 384 * c + 128 * mt: 384 * c + 128 * (mt + 1)],
                                    xn2v(c, QT * tt, QT),
                                    start=(c == 0), stop=(c == CH - 1)),
                                waits=w, inc="pe" if c == CH - 1 else None)
                    if c == CH - 1:
                        evs.append(ev)
                mm_ev = evs[-1]
                ev_guard = qk_last[p - 2] if p >= 2 and mt < 2 else None
                ev_guard_av = av_last[p - 2] if p >= 2 and mt == 2 else None
                if mt == 0:
                    evac = P.emit("vector",
                                  lambda e, sl=sl, tt=tt: e.tensor_scalar_mul(
                                      qp_sb[sl][:, QT * tt:QT * (tt + 1)], PS_D[:, 0:512], 0.125),
                                  waits=[mm_ev, ev_guard], inc="dve")
                    psd_free = evac
                elif mt == 1:
                    evac = P.emit("vector",
                                  lambda e, sl=sl, tt=tt: e.tensor_copy(
                                      kp_sb[sl][:, QT * tt:QT * (tt + 1)], PS_D[:, 0:512]),
                                  waits=[mm_ev, ev_guard], inc="dve")
                    psd_free = evac
                else:
                    evac0 = P.emit("vector",
                                   lambda e: e.tensor_copy(vfm_st[:, 0:512], PS_D[:, 0:512]),
                                   waits=[mm_ev, vfm_free[0]], inc="dve")
                    psd_free = evac0
                    trev = None
                    for b in range(4):
                        trev = P.emit("tensor",
                                      lambda e, b=b: e.transpose(
                                          PS_TR[:, 128 * b:128 * (b + 1)],
                                          vfm_st[:, 128 * b:128 * (b + 1)], ident_bf[:]),
                                      waits=[evac0] + ([pstr_free] if b == 0 else []),
                                      inc="pe" if b == 3 else None)
                    dst = bass.AP(vtm_sb[sl], 130 * (4 * tt),
                                  [[130 * NJ, 128], [130, 4], [65, 2], [1, 64]])
                    evac = P.emit("vector",
                                  lambda e, dst=dst: e.tensor_copy(dst, PS_TR[:, 0:512]),
                                  waits=[trev, ev_guard_av], inc="dve")
                    pstr_free = evac
                    vfm_free[0] = evac
                return evac

            vfm_free = [None]
            wq_ev = {}

            def emit_pair_load(p):
                w = [wq_free] if wq_free is not None else []
                wq_ev[p] = P.dma("gpsimd", wqkv_sb[:], wqkvdev[p], waits=w, inc="wq")

            emit_pair_load(0)
            proj_evs = [emit_proj_group(0, gi) for gi in range(24)]
            proj_done[0] = proj_evs[-1]
            wq_free = P.now("pe")
            srow_free = None

            for p in range(NPAIR):
                sl = p % 2
                proj_q = list(range(24)) if p + 1 < NPAIR else []
                pace_ctr = [0]
                if p + 1 < NPAIR:
                    emit_pair_load(p + 1)
                last_av_ev = None
                last_qk_ev = None
                pend_av = None
                for t in range(NT):
                    nj = 4 * t + 4
                    for j in range(nj):
                        par = jj % 2
                        eb = jj % 4
                        w = [proj_done[p], s_freeA[par]]
                        qk_a_ev = P.emit("tensor",
                                         lambda e, sl=sl, t=t, j=j, par=par: e.matmul(
                                             PS_S[par][:, 0:512],
                                             kp_sb[sl][0:64, 128 * j:128 * (j + 1)],
                                             qp_sb[sl][0:64, QT * t:QT * (t + 1)],
                                             start=True, stop=True, tile_position=(0, 0)),
                                         waits=w, inc="pe")
                        qk_ev = P.emit("tensor",
                                       lambda e, sl=sl, t=t, j=j, par=par: e.matmul(
                                           PS_S[par][:, 512:1024],
                                           kp_sb[sl][64:128, 128 * j:128 * (j + 1)],
                                           qp_sb[sl][64:128, QT * t:QT * (t + 1)],
                                           start=True, stop=True, tile_position=(64, 0)),
                                       waits=[s_freeB[par]], inc="pe")
                        last_qk_ev = qk_ev
                        if pend_av is not None:
                            last_av_ev = pend_av()
                            pend_av = None
                        exp_a = P.emit("scalar",
                                       lambda e, eb=eb, par=par: e.activation(
                                           exps_sb[eb][:, 0:512], PS_S[par][:, 0:512], AF.Exp,
                                           bias=expbias[:, 0:1]),
                                       waits=[qk_a_ev, eb_freeA[eb]], inc="act")
                        exp_b = P.emit("scalar",
                                       lambda e, eb=eb, par=par: e.activation(
                                           exps_sb[eb][:, 512:1024], PS_S[par][:, 512:1024], AF.Exp,
                                           bias=expbias[:, 0:1]),
                                       waits=[qk_ev, eb_freeB[eb]], inc="act")
                        s_freeA[par] = exp_a
                        s_freeB[par] = exp_b
                        ready_a = exp_a
                        ready_b = exp_b
                        if j >= 4 * t:
                            o = j - 4 * t
                            ready_a = P.emit("vector",
                                             lambda e, eb=eb, o=o: e.tensor_mul(
                                                 exps_sb[eb][:, 0:512], exps_sb[eb][:, 0:512],
                                                 masks_bf[:, QT * o:QT * (o + 1)]),
                                             waits=[exp_a], inc="dve")
                            ready_b = P.emit("vector",
                                             lambda e, eb=eb, o=o: e.tensor_mul(
                                                 exps_sb[eb][:, 512:1024], exps_sb[eb][:, 512:1024],
                                                 masks_bf[:, QT * o:QT * (o + 1)]),
                                             waits=[exp_b], inc="dve")

                        def mk_av(j=j, t=t, sl=sl, eb=eb, nj=nj, ready_a=ready_a, ready_b=ready_b):
                            w = [ready_a]
                            if j == 0:
                                w.append(ctx_bank_free)
                                w.append(ctx_bank_free2)
                            av_a = P.emit("tensor",
                                          lambda e: e.matmul(
                                              PS_C[0:65, 0:512],
                                              bass.AP(vtm_sb[sl], 130 * j, [[130 * NJ, 128], [1, 65]]),
                                              exps_sb[eb][:, 0:512],
                                              start=(j == 0), stop=(j == nj - 1)),
                                          waits=w, inc="pe")
                            ev = P.emit("tensor",
                                        lambda e: e.matmul(
                                            PS_C[0:65, 512:1024],
                                            bass.AP(vtm_sb[sl], 130 * j + 65, [[130 * NJ, 128], [1, 65]]),
                                            exps_sb[eb][:, 512:1024],
                                            start=(j == 0), stop=(j == nj - 1)),
                                        waits=[ready_b], inc="pe")
                            eb_freeA[eb] = av_a
                            eb_freeB[eb] = ev
                            return ev
                        pend_av = mk_av
                        jj += 1
                        if p + 1 < NPAIR and proj_q:
                            pace_ctr[0] += 1
                            if pace_ctr[0] >= 6:
                                pace_ctr[0] = 0
                                gi = proj_q.pop(0)
                                ev = emit_proj_group(p + 1, gi)
                                if gi == 23:
                                    proj_done[p + 1] = ev
                    last_av_ev = pend_av()
                    pend_av = None
                    ca = P.emit("vector",
                                lambda e, sl=sl, t=t: e.tensor_copy(
                                    ctxu_sb[sl][0:64, QT * t:QT * (t + 1)], PS_C[0:64, 0:512]),
                                waits=[last_av_ev, ctxu_free[sl]], inc="dve")
                    cb = P.emit("vector",
                                lambda e, sl=sl, t=t: e.tensor_copy(
                                    ctxu_sb[sl][64:128, QT * t:QT * (t + 1)], PS_C[0:64, 512:1024]),
                                inc="dve")

                    sa = P.emit("scalar",
                                lambda e, t=t: e.copy(
                                    srowA[0:1, QT * t:QT * (t + 1)], PS_C[64:65, 0:512]),
                                waits=[srow_free, last_av_ev], inc="act")
                    sbv = P.emit("scalar",
                                 lambda e, t=t: e.copy(
                                     srowB[0:1, QT * t:QT * (t + 1)], PS_C[64:65, 512:1024]),
                                 inc="act")
                    ctx_bank_free = sbv
                    ctx_bank_free2 = cb
                while proj_q:
                    gi = proj_q.pop(0)
                    ev = emit_proj_group(p + 1, gi)
                    if gi == 23:
                        proj_done[p + 1] = ev
                qk_last[p] = last_qk_ev
                av_last[p] = last_av_ev
                wq_free = P.now("pe")
                r2 = P.now("act")
                gv = None
                for rr in range(16):
                    t2, h = rr // 2, rr % 2
                    srw = srowA if h == 0 else srowB
                    gv = P.emit("tensor",
                                lambda e, rr=rr, t2=t2, srw=srw: e.matmul(
                                    PS_D[0:16, 0:512], eye16_sb[0:1, 16 * rr:16 * (rr + 1)],
                                    srw[0:1, QT * t2:QT * (t2 + 1)],
                                    start=(rr == 0), stop=(rr == 15)),
                                waits=[r2, consts_ev, psd_free] if rr == 0 else [],
                                inc="pe" if rr == 15 else None)
                srow_free = gv

                def _recip16(e):
                    with nc.allow_low_precision(reason="bf16 softmax denominators"):
                        return e.reciprocal(recip16[:], PS_D[0:16, 0:512])
                r16 = P.emit("vector", _recip16, waits=[gv], inc="dve")
                psd_free = r16
                nb_ev = None
                psc_norm_free = None
                for t2 in range(NT):
                    if t2 % 2 == 0:
                        dst_ps = PS_D[:, 0:512]
                        wbc = [r16, psd_free]
                    else:
                        dst_ps = PS_C[:, 0:512]
                        wbc = [r16, psc_norm_free]
                    bm = P.emit("tensor",
                                lambda e, t2=t2, dst_ps=dst_ps: e.matmul(
                                    dst_ps, sel16_sb[:, 128 * t2:128 * (t2 + 1)],
                                    recip16[:], start=True, stop=True),
                                waits=wbc, inc="pe")
                    nb_ev = P.emit("vector",
                                   lambda e, sl=sl, t2=t2, dst_ps=dst_ps: e.tensor_mul(
                                       ctxu_sb[sl][:, QT * t2:QT * (t2 + 1)],
                                       ctxu_sb[sl][:, QT * t2:QT * (t2 + 1)], dst_ps),
                                   waits=[bm], inc="dve")
                    if t2 % 2 == 0:
                        psd_free = nb_ev
                    else:
                        psc_norm_free = nb_ev
                ctx_bank_free = psc_norm_free
                norm_done[p] = nb_ev
                ctxu_free[sl] = P.dma("sync", ctxspill[p], ctxu_sb[sl][:],
                                      waits=[nb_ev], inc=f"stc{sl}")
                spill_evs[p] = ctxu_free[sl]
                if dbg >= 2:
                    P.dma("gpsimd", dbg_out["ctx"][p], ctxspill[p],
                          waits=[ctxu_free[sl]], inc="ldg")
                    P.dma("gpsimd", dbg_out["sums"][p:p + 1, 0:T], srowA, waits=[r2], inc="ldg")
                    P.dma("gpsimd", dbg_out["sums"][p:p + 1, T:2 * T], srowB, waits=[r2], inc="ldg")

        if phases >= 3:
            bar = [P.now("pe"), P.now("act"), P.now("dve"), P.now("gps"),
                   P.now("st"), P.now("ld"), P.now("ldg")]
            wo_dst = bass.AP(xn2, 0, [[CH * T, 128], [1024, CH], [1, 1024]])
            ev_wo = P.dma("gpsimd", wo_dst, wodev, waits=bar, inc="wld")
            wk_dst = bass.AP(xn2, 8192, [[CH * T, 128], [1024, CH], [1, 1024]])
            wv_dst = bass.AP(xn2, 16384, [[CH * T, 128], [1024, CH], [1, 1024]])
            ev_wk = P.dma("gpsimd", wk_dst, wkdev, waits=bar, inc="wld")
            ev_wv = P.dma("gpsimd", wv_dst, wvdev, waits=bar, inc="wld")
            ev_wo = ev_wk = ev_wv
            wo_v = lambda c, m: xn2[:, 1024 * c + 128 * m: 1024 * c + 128 * (m + 1)]

            site_ps = [PS_C, PS_S[1]]
            site_free = [bar[2], bar[1]]
            site_idx = [0]
            rows_free = [bar[0]]

            def acquire_site():
                par = site_idx[0] % 2
                site_idx[0] += 1
                return site_ps[par], par

            def rows_and_bcast(ps, par, sl, stats_pe):
                a = P.emit("vector", lambda e, ps=ps: e.tensor_scalar_mul(
                               r_a[0:1, :], ps[0:1, 0:512], 1.0 / D),
                           waits=[stats_pe, rows_free[0]], inc="dve")
                m2 = P.emit("vector", lambda e: e.tensor_mul(r_c[0:1, :], r_a[0:1, :], r_a[0:1, :]),
                            waits=[a], inc="dve")
                v = P.emit("vector", lambda e, ps=ps: e.scalar_tensor_tensor(
                               r_b[0:1, :], ps[0:1, 512:1024], 1.0 / D, r_c[0:1, :],
                               ALU.mult, ALU.subtract),
                           waits=[m2], inc="dve")
                ln = P.emit("scalar", lambda e: e.activation(
                                r_b[0:1, :], r_b[0:1, :], AF.Ln,
                                bias=epscol[0:1, 0:1]),
                            waits=[v], inc="act")
                ex = P.emit("scalar", lambda e: e.activation(r_c[0:1, :], r_b[0:1, :], AF.Exp,
                                                             scale=neghalf[0:1, 0:1]),
                            waits=[ln], inc="act")
                mr = P.emit("vector", lambda e: e.tensor_mul(r_a[0:1, :], r_a[0:1, :], r_c[0:1, :]),
                            waits=[ex], inc="dve")
                evb1 = P.emit("tensor", lambda e, ps=ps: e.matmul(
                                  ps[:, 0:512], onesf_row[0:1, :], r_c[0:1, :],
                                  start=True, stop=True),
                              waits=[ex, mr], inc="pe")
                evb2 = P.emit("tensor", lambda e, ps=ps: e.matmul(
                                  ps[:, 512:1024], onesf_row[0:1, :], r_a[0:1, :],
                                  start=True, stop=True),
                              waits=[mr], inc="pe")
                rows_free[0] = evb2
                evc1 = P.emit("vector", lambda e, sl=sl, ps=ps: e.tensor_copy(
                                  bc_t[sl][:, 0, :], ps[:, 0:512]),
                              waits=[evb1, P.now("gps")], inc="dve")
                evc2 = P.emit("vector", lambda e, sl=sl, ps=ps: e.tensor_copy(
                                  bc_t[sl][:, 1, :], ps[:, 512:1024]),
                              waits=[evb2], inc="dve")
                site_free[par] = evc2
                return evc1, evc2

            mm_reg2 = [(PS_S[0], 0), (PS_S[0], 512)]
            reg_free = [bar[1], bar[1]]
            au_free = [[], []]
            cx_free = [None, None]
            x2_evs = {}
            xf_free2 = list(xf_free)
            x2sm_free = [None] * 4
            wo_mm_last = None

            for t in range(NT):
                sl = t % 2
                cx = ctxu_sb[sl]
                ev_cx = []
                for c in range(CH):
                    ev_cx.append(P.dma("sync", cx[:, QT * c:QT * (c + 1)],
                                       ctxspill[c, :, QT * t:QT * (t + 1)],
                                       waits=[spill_evs[6], spill_evs[7]]
                                       + ([cx_free[sl]] if c == 0 else []),
                                       inc=f"cx{sl}"))
                ev_cx = [ev_cx[-1]] * CH
                au = xbf_t[sl]
                evac_evs = {}
                cs_evs = {}
                ps, par = acquire_site()
                for m in range(CH):
                    reg, off = mm_reg2[m % 2]
                    for c in range(CH):
                        w = [ev_wo, ev_cx[c]]
                        if c == 0:
                            w.append(reg_free[m % 2])
                        ev = P.emit("tensor",
                                    lambda e, c=c, m=m, reg=reg, off=off, cx=cx: e.matmul(
                                        reg[:, off:off + 512], wo_v(c, m),
                                        cx[:, QT * c:QT * (c + 1)],
                                        start=(c == 0), stop=(c == CH - 1)),
                                    waits=w, inc="pe" if c == CH - 1 else None)
                    wo_mm_last = ev
                    ae = P.emit("vector",
                                lambda e, m=m, reg=reg, off=off, au=au: e.tensor_copy(
                                    au[:, m, :], reg[:, off:off + 512]),
                                waits=[ev] + (au_free[sl] if m == 0 else []), inc="dve")
                    reg_free[m % 2] = ae
                    evac_evs[m] = ae
                    sqeng, sqsem = ("vector", "dve")
                    sq = P.emit(sqeng,
                                lambda e, m=m, au=au: e.tensor_mul(
                                    xsq_t[:, m % 2, :], au[:, m, :], au[:, m, :]),
                                waits=[ae, cs_evs.get(m - 2)], inc=sqsem)
                    P.emit("tensor",
                           lambda e, m=m, au=au, ps=ps: e.matmul(
                               ps[0:1, 0:512], ones_col[:], au[:, m, :],
                               start=(m == 0), stop=(m == CH - 1)),
                           waits=[ae] + ([site_free[par]] if m == 0 else []))
                    cs_evs[m] = P.emit("tensor",
                                       lambda e, m=m, ps=ps: e.matmul(
                                           ps[0:1, 512:1024], ones_col[:], xsq_t[:, m % 2, :],
                                           start=(m == 0), stop=(m == CH - 1)),
                                       waits=[sq], inc="pe")
                evc1, evc2 = rows_and_bcast(ps, par, sl, cs_evs[CH - 1])
                if dbg >= 5 and phases == 3:
                    au_dump = P.dma("gpsimd", dbg_out["au"][t], au[:],
                                    waits=[evac_evs[CH - 1]], inc="ldg")
                    au_free[sl] = au_free[sl] + [au_dump]
                xf_evs = {}
                for c in range(4):
                    fsl = c % 4
                    xf_evs[c] = P.dma("sync", xf_t[fsl][:], xdev[:, c, QT * t:QT * (t + 1)],
                                      waits=[xf_free2[fsl]], inc=f"xf{fsl}")
                for c in range(CH):
                    fsl = (t * CH + c) % 4
                    eng, esem = ("vector", "dve")
                    ev_xf = xf_evs[c]
                    s1 = P.emit(eng,
                                lambda e, c=c, fsl=fsl, sl=sl, au=au: e.tensor_mul(
                                    x2sm[fsl][:], au[:, c, :], bc_t[sl][:, 0, :]),
                                waits=[evc1, evac_evs[c], x2sm_free[fsl]], inc=esem)
                    s2 = P.emit(eng,
                                lambda e, fsl=fsl, sl=sl: e.tensor_sub(
                                    x2sm[fsl][:], x2sm[fsl][:], bc_t[sl][:, 1, :]),
                                waits=[evc2], inc=esem)
                    s3 = P.emit(eng,
                                lambda e, fsl=fsl: e.tensor_add(
                                    x2sm[fsl][:], x2sm[fsl][:], xf_t[fsl][:]),
                                waits=[ev_xf], inc=esem)
                    xf_free2[fsl] = s3
                    if c + 4 < CH:
                        xf_evs[c + 4] = P.dma("sync", xf_t[fsl][:],
                                              xdev[:, c + 4, QT * t:QT * (t + 1)],
                                              waits=[s3], inc=f"xf{fsl}")
                    ev_st = P.dma("sync", x2spill[:, c, QT * t:QT * (t + 1)], x2sm[fsl][:],
                                  waits=[s3], inc=f"stx{fsl}")
                    x2sm_free[fsl] = ev_st
                    x2_evs[(t, c)] = ev_st
                    if c >= CH - 2:
                        au_free[sl] = (au_free[sl] + [s1])[-2:] if c == CH - 1 else [s1]
                    if dbg >= 3 and phases == 3:
                        P.dma("gpsimd", dbg_out["x2"][:, c, QT * t:QT * (t + 1)],
                              x2spill[:, c, QT * t:QT * (t + 1)], waits=[ev_st], inc="ldg")
                cx_free[sl] = P.now("pe")

        if phases >= 4:
            wr_dst = bass.AP(xn2, 0, [[CH * T, 128], [1024, CH], [1, 1024]])
            ev_wr = P.dma("gpsimd", wr_dst, wrdev, waits=[wo_mm_last], inc="wrd")
            wr_v = lambda c, m: xn2[:, 1024 * c + 128 * m: 1024 * c + 128 * (m + 1)]
            wk_v = lambda c, m: xn2[:, 8192 + 1024 * c + 128 * m: 8192 + 1024 * c + 128 * (m + 1)]
            wv_v = lambda c, m: xn2[:, 16384 + 1024 * c + 128 * m: 16384 + 1024 * c + 128 * (m + 1)]
            Z0 = 24576
            zv = lambda c, i0, n: xn2[:, Z0 + 513 * c + i0: Z0 + 513 * c + i0 + n]
            z_head = bass.AP(xn2, Z0, [[CH * T, 128], [513, CH], [1, 1]])
            z_tail = bass.AP(xn2, Z0 + 512, [[CH * T, 128], [513, CH], [1, 1]])
            xr_v = lambda c: qp_sb[0][:, QT * c:QT * (c + 1)]
            xk_v = lambda c: qp_sb[1][:, QT * c:QT * (c + 1)]
            r_v = lambda m: kp_sb[0][:, QT * m:QT * (m + 1)]
            ka_v = lambda m: kp_sb[1][:, QT * m:QT * (m + 1)]
            g_v = lambda m: vtm_sb[0][:, QT * m:QT * (m + 1)]

            stx_finals = [P.now(f"stx{i}") for i in range(4)]
            mm_reg3 = [(PS_S[0], 0), (PS_S[0], 512), (PS_D, 0)]
            reg_free3 = [P.now("act"), P.now("act"), P.now("dve")]
            x2b_free = list(cx_free)
            xq_free = list(xf_free2)
            xo_free = [None] * 3
            zcopy_prev = None
            xr_done_prev = []
            g_free = []
            prev_cs4 = {}
            rk_reads_z = None

            for t in range(NT):
                sl = t % 2
                x2b = bass.AP(ctxu_sb[sl], 0, [[T, 128], [512, CH], [1, 512]])
                ev_x2b = P.dma("gpsimd", x2b, x2spill[:, :, QT * t:QT * (t + 1)],
                               waits=[x2b_free[sl]] + stx_finals,
                               inc=f"x2b{sl}")
                cs3 = {}
                ps, par = acquire_site()
                for c in range(CH):
                    sqeng, sqsem = ("vector", "dve")
                    w_guard = cs3.get(c - 2) if c >= 2 else prev_cs4.get(CH - 2 + c)
                    sq = P.emit(sqeng,
                                lambda e, c=c, sl=sl: e.tensor_mul(
                                    xsq_t[:, c % 2, :],
                                    bass.AP(ctxu_sb[sl], 512 * c, [[T, 128], [1, 512]]),
                                    bass.AP(ctxu_sb[sl], 512 * c, [[T, 128], [1, 512]])),
                                waits=[ev_x2b, w_guard], inc=sqsem)
                    P.emit("tensor",
                           lambda e, c=c, sl=sl, ps=ps: e.matmul(
                               ps[0:1, 0:512], ones_col[:],
                               bass.AP(ctxu_sb[sl], 512 * c, [[T, 128], [1, 512]]),
                               start=(c == 0), stop=(c == CH - 1)),
                           waits=[ev_x2b] + ([site_free[par]] if c == 0 else []))
                    cs3[c] = P.emit("tensor",
                                    lambda e, c=c, ps=ps: e.matmul(
                                        ps[0:1, 512:1024], ones_col[:], xsq_t[:, c % 2, :],
                                        start=(c == 0), stop=(c == CH - 1)),
                                    waits=[sq], inc="pe")
                evc1, evc2 = rows_and_bcast(ps, par, sl, cs3[CH - 1])
                if t == 0:
                    zb = P.emit("vector", lambda e: e.memset(z_head, 0.0),
                                waits=[ev_wv], inc="dve")
                else:
                    zb = P.emit("vector", lambda e: e.tensor_copy(z_head, z_tail),
                                waits=xr_done_prev, inc="dve")
                xr_evs = {}
                xq_evs = {}
                tcol_free = {}
                for c in range(4):
                    fsl = c % 4
                    xq_evs[c] = P.dma("sync", xf_t[fsl][:], x2spill[:, c, QT * t:QT * (t + 1)],
                                      waits=[xq_free[fsl]] + stx_finals, inc=f"xf{fsl}")
                for c in range(CH):
                    fsl = c % 4
                    eng, esem = ("vector", "dve")
                    ev_xq = xq_evs[c]
                    z1 = P.emit(eng,
                                lambda e, fsl=fsl, sl=sl: e.tensor_mul(
                                    xf_t[fsl][:], xf_t[fsl][:], bc_t[sl][:, 0, :]),
                                waits=[ev_xq, evc1, rk_reads_z], inc=esem)
                    z2 = P.emit(eng,
                                lambda e, c=c, fsl=fsl, sl=sl: e.tensor_sub(
                                    zv(c, 1, QT), xf_t[fsl][:], bc_t[sl][:, 1, :]),
                                waits=[evc2, zb], inc=esem)
                    xq_free[fsl] = z2
                    if c + 4 < CH:
                        xq_evs[c + 4] = P.dma("sync", xf_t[fsl][:],
                                              x2spill[:, c + 4, QT * t:QT * (t + 1)],
                                              waits=[z2], inc=f"xf{fsl}")
                    tcol = vtm_sb[1][:, 0:512] if c % 2 == 0 else vtm_sb[1][:, 512:1024]
                    d1 = P.emit(eng,
                                lambda e, c=c, tcol=tcol: e.tensor_sub(
                                    tcol, zv(c, 1, QT), zv(c, 0, QT)),
                                waits=[z2, zcopy_prev, tcol_free.get(c % 2)], inc=esem)
                    x1 = P.emit("vector",
                                lambda e, c=c, tcol=tcol: e.scalar_tensor_tensor(
                                    xr_v(c), tcol, mu_sb[:, c:c + 1], zv(c, 0, QT),
                                    ALU.mult, ALU.add),
                                waits=[d1], inc="dve")
                    x2e = P.emit("vector",
                                 lambda e, c=c, tcol=tcol: e.scalar_tensor_tensor(
                                     xk_v(c), tcol, mu_sb[:, 8 + c:9 + c], zv(c, 0, QT),
                                     ALU.mult, ALU.add),
                                 waits=[x1], inc="dve")
                    xr_evs[c] = x2e
                    tcol_free[c % 2] = x2e
                xr_done_prev = [P.now("dve"), P.now("gps")]
                sig_evs = {}
                for m in range(CH):
                    reg, off = mm_reg3[m % 3]
                    for c in range(CH):
                        w = [ev_wr, xr_evs[c]]
                        if c == 0:
                            w.append(reg_free3[m % 3])
                        ev = P.emit("tensor",
                                    lambda e, c=c, m=m, reg=reg, off=off: e.matmul(
                                        reg[:, off:off + 512], wr_v(c, m), xr_v(c),
                                        start=(c == 0), stop=(c == CH - 1)),
                                    waits=w, inc="pe" if c == CH - 1 else None)
                    sig_evs[m] = P.emit("scalar",
                                        lambda e, m=m, reg=reg, off=off: e.activation(
                                            r_v(m), reg[:, off:off + 512], AF.Sigmoid),
                                        waits=[ev] + (g_free if m == 0 else []), inc="act")
                    reg_free3[m % 3] = sig_evs[m]
                ka_evs = {}
                for m in range(CH):
                    reg, off = mm_reg3[m % 3]
                    for c in range(CH):
                        w = [ev_wk, xr_evs[c]]
                        if c == 0:
                            w.append(reg_free3[m % 3])
                        ev = P.emit("tensor",
                                    lambda e, c=c, m=m, reg=reg, off=off: e.matmul(
                                        reg[:, off:off + 512], wk_v(c, m), xk_v(c),
                                        start=(c == 0), stop=(c == CH - 1)),
                                    waits=w, inc="pe" if c == CH - 1 else None)
                    rl = P.emit("scalar",
                                lambda e, m=m, reg=reg, off=off: e.activation(
                                    ka_v(m), reg[:, off:off + 512], AF.Relu),
                                waits=[ev], inc="act")
                    reg_free3[m % 3] = rl
                    ka_evs[m] = P.emit("vector",
                                       lambda e, m=m: e.tensor_mul(ka_v(m), ka_v(m), ka_v(m)),
                                       waits=[rl], inc="dve")
                gs_evs = {}
                cs4 = {}
                ps2, par2 = acquire_site()
                gfree = g_free
                for m in range(CH):
                    reg, off = mm_reg3[m % 3]
                    for c in range(CH):
                        w = [ev_wv, ka_evs[c]]
                        if c == 0:
                            w.append(reg_free3[m % 3])
                        ev = P.emit("tensor",
                                    lambda e, c=c, m=m, reg=reg, off=off: e.matmul(
                                        reg[:, off:off + 512], wv_v(c, m), ka_v(c),
                                        start=(c == 0), stop=(c == CH - 1)),
                                    waits=w, inc="pe" if c == CH - 1 else None)
                    ge = P.emit("vector",
                                lambda e, m=m, reg=reg, off=off: e.tensor_mul(
                                    g_v(m), r_v(m), reg[:, off:off + 512]),
                                waits=[ev, sig_evs[m]] + (gfree if m == 0 else []), inc="dve")
                    reg_free3[m % 3] = ge
                    gs_evs[m] = ge
                    sqeng, sqsem = ("vector", "dve")
                    sqg = P.emit(sqeng,
                                 lambda e, m=m: e.tensor_mul(
                                     xsq_t[:, m % 2, :], g_v(m), g_v(m)),
                                 waits=[ge, cs4.get(m - 2)], inc=sqsem)
                    P.emit("tensor",
                           lambda e, m=m, ps2=ps2: e.matmul(
                               ps2[0:1, 0:512], ones_col[:], g_v(m),
                               start=(m == 0), stop=(m == CH - 1)),
                           waits=[ge] + ([site_free[par2]] if m == 0 else []))
                    cs4[m] = P.emit("tensor",
                                    lambda e, m=m, ps2=ps2: e.matmul(
                                        ps2[0:1, 512:1024], ones_col[:], xsq_t[:, m % 2, :],
                                        start=(m == 0), stop=(m == CH - 1)),
                                    waits=[sqg], inc="pe")
                evg1, evg2 = rows_and_bcast(ps2, par2, sl, cs4[CH - 1])
                xo_evs = {}
                for c in range(3):
                    osl = c % 3
                    xo_evs[c] = P.dma("sync", osm[osl][:], x2spill[:, c, QT * t:QT * (t + 1)],
                                      waits=[xo_free[osl]], inc=f"osm{osl}")
                for c in range(CH):
                    fsl = c % 4
                    osl = c % 3
                    eng, esem = ("vector", "dve")
                    o1 = P.emit(eng,
                                lambda e, c=c, fsl=fsl, sl=sl: e.tensor_mul(
                                    x2sm[fsl][:], g_v(c), bc_t[sl][:, 0, :]),
                                waits=[evg1, gs_evs[c], x2sm_free[fsl]], inc=esem)
                    o2 = P.emit(eng,
                                lambda e, fsl=fsl, sl=sl: e.tensor_sub(
                                    x2sm[fsl][:], x2sm[fsl][:], bc_t[sl][:, 1, :]),
                                waits=[evg2], inc=esem)
                    o3 = P.emit(eng,
                                lambda e, fsl=fsl, osl=osl: e.tensor_add(
                                    x2sm[fsl][:], x2sm[fsl][:], osm[osl][:]),
                                waits=[xo_evs[c]], inc=esem)
                    xo_free[osl] = o3
                    if c + 3 < CH:
                        xo_evs[c + 3] = P.dma("sync", osm[osl][:],
                                              x2spill[:, c + 3, QT * t:QT * (t + 1)],
                                              waits=[o3], inc=f"osm{osl}")
                    ev_st = P.dma("sync", outdev[:, c, QT * t:QT * (t + 1)], x2sm[fsl][:],
                                  waits=[o3], inc=f"stx{fsl}")
                    x2sm_free[fsl] = ev_st
                x2b_free[sl] = P.now("pe")
                rk_reads_z = P.now("pe")
                zcopy_prev = zb
                g_free = [P.now("dve"), P.now("gps")]
                prev_cs4 = cs4

        P.emit("sync", lambda e: e.nop(),
               waits=[P.now("st"), P.now("dve"), P.now("act"), P.now("pe"),
                      P.now("gps"), P.now("ld"), P.now("ldg"),
                      P.now("stx0"), P.now("stx1"), P.now("stx2"), P.now("stx3"),
                      P.now("stc0"), P.now("stc1")])
        P.replay()
    return nc, dbg_out


from concourse.bass_utils import run_bass_kernel_spmd

_NC_CACHE = {}


def kernel(**inputs):
    if "nc" not in _NC_CACHE:
        _NC_CACHE["nc"] = build(dbg=0, phases=4)[0]
    nc = _NC_CACHE["nc"]
    in_maps = [host_inputs(inputs, core) for core in range(8)]
    res = run_bass_kernel_spmd(nc, in_maps, core_ids=list(range(8)))
    outs = [host_output(r) for r in res.results]
    return np.stack(outs, axis=0).astype(np.float32)
